# version 08
# baseline (speedup 1.0000x reference)
import numpy as np
from contextlib import ExitStack

import concourse.bass as bass
import concourse.mybir as mybir
from concourse.bass_utils import run_bass_kernel_spmd

F32 = mybir.dt.float32
BF16 = mybir.dt.bfloat16
I32 = mybir.dt.int32
AF = mybir.ActivationFunctionType
ALU = mybir.AluOpType
AX = mybir.AxisListType.X

T = 2048
D = 1024
NC8 = 8
NT = 16
NTG = 4
NHEAD = 16
HARM = 32
E = 8
NFM = 11
NMOE = E * NFM
NFS = 22
NHID = NMOE + NFS
GSZ = 4
BIG = 100.0
EPS_LN = 1e-5
TWO_PI = 6.283185307179586
HALF_PI = 1.5707963267948966
INV_2PI = 1.0 / TWO_PI
PI_LO = 3.1415925

ENGS = ("tensor", "vector", "scalar", "gpsimd", "sync")


def build_nc():
    nc = bass.Bass("TRN2", target_bir_lowering=False)

    x_h = nc.dram_tensor("x", [T, D], F32, kind="ExternalInput")
    basis = {p: nc.dram_tensor(f"basis_{p}", [HARM, D], F32, kind="ExternalInput")
             for p in "qkvo"}
    phase = {p: nc.dram_tensor(f"phase_{p}", [D, HARM], F32, kind="ExternalInput")
             for p in "qkvo"}
    amp = {p: nc.dram_tensor(f"amp_{p}", [D, HARM], F32, kind="ExternalInput")
           for p in "qkvo"}
    gate_w_h = nc.dram_tensor("gate_w", [E, D], F32, kind="ExternalInput")
    w_gate_h = nc.dram_tensor("w_gate", [E, D, 1408], F32, kind="ExternalInput")
    w_up_h = nc.dram_tensor("w_up", [E, D, 1408], F32, kind="ExternalInput")
    w_down_h = nc.dram_tensor("w_down", [E, 1408, D], F32, kind="ExternalInput")
    sw_gate_h = nc.dram_tensor("sw_gate", [D, 2816], F32, kind="ExternalInput")
    sw_up_h = nc.dram_tensor("sw_up", [D, 2816], F32, kind="ExternalInput")
    sw_down_h = nc.dram_tensor("sw_down", [2816, D], F32, kind="ExternalInput")
    out_h = nc.dram_tensor("out", [T, D], F32, kind="ExternalOutput")

    ops = {e: [] for e in ENGS}
    cnt = {s: 0 for s in ("t", "v", "s", "g", "dp", "dwa", "dwb",
                          "dxa", "dxb", "doa", "dob", "dr0", "dr1")}

    SELF_SEM = {"vector": "v", "scalar": "s"}

    def emit(eng, fn, waits=(), incs=()):
        for sname, val in waits:
            assert val <= cnt[sname], f"wait-ahead: {sname} {val} > {cnt[sname]}"
        incs = list(incs)
        ss = SELF_SEM.get(eng)
        if ss is not None and not any(n == ss for n, _ in incs):
            incs.append((ss, 1))
        rec = []
        for sname, amt in incs:
            cnt[sname] += amt
            rec.append((sname, amt))
        ops[eng].append((fn, tuple(waits), tuple(rec)))

    def sb(name, shape, dtype, stack, side=None):
        if side is None:
            return stack.enter_context(nc.sbuf_tensor(name, shape, dtype))
        return stack.enter_context(nc.sbuf_tensor(name, shape, dtype, side=side))

    def pt(name, shape, dtype, stack):
        return stack.enter_context(nc.psum_tensor(name, shape, dtype))

    so = ExitStack()
    ident_bf = sb("ident_bf", [128, 128], BF16, so)
    ident_f = sb("ident_f", [128, 128], F32, so)
    st_iota = ExitStack()
    iota2d = sb("iota2d", [128, 128], F32, st_iota, side="right")
    onesrow = sb("onesrow", [1, 128], BF16, so)
    epsln = sb("epsln", [128, 1], F32, so)
    selc = sb("selc", [16, NC8 * 128], BF16, so)
    sele = sb("sele", [E, E * 128], BF16, so)
    xnh_fm = sb("xnh_fm", [128, NC8 * T], BF16, so)
    comb_bf = sb("comb_bf", [E, T], BF16, so)
    bias128 = sb("bias128", [128, 1], F32, so)
    stw = ExitStack()
    basT = {p: sb(f"basT_{p}", [128, NC8 * HARM], BF16, stw) for p in "qkvo"}
    wT = {p: sb(f"wT_{p}", [HARM, D], BF16, stw) for p in "qkvo"}
    gwTh = sb("gwTh", [128, NC8 * 8], BF16, stw)
    gwTl = sb("gwTl", [128, NC8 * 8], BF16, stw)

    emit("gpsimd", lambda g: g.iota(iota2d[:, :], [[1, 128]], channel_multiplier=-1,
                                    allow_small_or_imprecise_dtypes=True),
         incs=[("g", 1)])
    g_iota = cnt["g"]
    emit("vector", lambda v: v.tensor_single_scalar(ident_bf[:], iota2d[:], 0.0, ALU.is_equal),
         waits=[("g", g_iota)], incs=[("v", 1)])
    emit("vector", lambda v: v.tensor_single_scalar(ident_f[:], iota2d[:], 0.0, ALU.is_equal),
         incs=[("v", 1)])
    emit("vector", lambda v: v.memset(onesrow[:], 1.0), incs=[("v", 1)])
    emit("vector", lambda v: v.memset(epsln[:], EPS_LN), incs=[("v", 1)])
    v_const = cnt["v"]
    st_iota.close()

    stp = ExitStack()
    bas_bf = {p: sb(f"bas_bf_{p}", [HARM, D], BF16, stp, side="right") for p in "qkvo"}
    ph_f = {p: sb(f"ph_f_{p}", [128, NC8 * HARM], F32, stp, side="right") for p in "qkvo"}
    am_f = {p: sb(f"am_f_{p}", [128, NC8 * HARM], F32, stp, side="right") for p in "qkvo"}
    gw_f = sb("gw_f", [E, D], F32, stp, side="right")
    wfull = {p: sb(f"wfull_{p}", [128, NC8 * HARM], BF16, stp, side="right") for p in "qkvo"}
    gwh = sb("gwh", [E, D], BF16, stp, side="right")
    gwl = sb("gwl", [E, D], BF16, stp, side="right")
    cw_t = sb("cw_t", [128, NC8 * HARM], F32, stp, side="right")
    cw_k = sb("cw_k", [128, NC8 * HARM], I32, stp, side="right")
    cw_f = sb("cw_f", [128, NC8 * HARM], F32, stp, side="right")
    gwt = sb("gwt", [E, D], F32, stp, side="right")
    scr_sel = sb("scr_sel", [16, 1024], F32, stp, side="right")

    emit("gpsimd", lambda g: g.iota(
        scr_sel.ap().rearrange("p (g j) -> p g j", g=16),
        [[1, 16], [0, 64]], channel_multiplier=-1,
        allow_small_or_imprecise_dtypes=True), incs=[("g", 1)])
    g_sel = cnt["g"]
    emit("vector", lambda v: v.tensor_single_scalar(selc[:, :], scr_sel[:, :],
                                                    0.0, ALU.is_equal),
         waits=[("g", g_sel)], incs=[("v", 1)])
    v_sel1 = cnt["v"]
    emit("gpsimd", lambda g: g.iota(
        scr_sel[0:8, :].rearrange("p (f j) -> p f j", f=8),
        [[1, 8], [0, 128]], channel_multiplier=-1,
        allow_small_or_imprecise_dtypes=True),
        waits=[("v", v_sel1)], incs=[("g", 1)])
    g_sel2 = cnt["g"]
    emit("vector", lambda v: v.tensor_single_scalar(sele[:, :], scr_sel[0:8, :],
                                                    0.0, ALU.is_equal),
         waits=[("g", g_sel2)], incs=[("v", 1)])

    for p in "qkvo":
        emit("gpsimd", lambda g, p=p: g.dma_start(bas_bf[p][:], basis[p][:, :]),
             incs=[("dp", 16)])
        emit("gpsimd", lambda g, p=p: g.dma_start(
            ph_f[p].ap().rearrange("p (c h) -> p c h", c=NC8),
            phase[p].ap().rearrange("(c p) h -> p c h", p=128)),
            incs=[("dp", 16)])
        emit("gpsimd", lambda g, p=p: g.dma_start(
            am_f[p].ap().rearrange("p (c h) -> p c h", c=NC8),
            amp[p].ap().rearrange("(c p) h -> p c h", p=128)),
            incs=[("dp", 16)])
    emit("gpsimd", lambda g: g.dma_start(gw_f[:], gate_w_h[:, :]), incs=[("dp", 16)])
    dp_params = cnt["dp"]

    for p in "qkvo":
        P = ph_f[p]
        emit("vector", lambda v, P=P: v.tensor_scalar(
            cw_t[:], P[:], HALF_PI, INV_2PI, ALU.add, ALU.mult),
            waits=[("dp", dp_params)], incs=[("v", 1)])
        emit("vector", lambda v: v.tensor_copy(cw_k[:], cw_t[:]), incs=[("v", 1)])
        emit("vector", lambda v: v.tensor_copy(cw_f[:], cw_k[:]), incs=[("v", 1)])
        emit("vector", lambda v, P=P: v.tensor_single_scalar(cw_t[:], P[:], HALF_PI, ALU.add),
             incs=[("v", 1)])
        emit("vector", lambda v: v.scalar_tensor_tensor(
            cw_t[:], cw_f[:], -TWO_PI, cw_t[:], ALU.mult, ALU.add), incs=[("v", 1)])
        emit("vector", lambda v: v.tensor_single_scalar(cw_f[:], cw_t[:], PI_LO, ALU.is_gt),
             incs=[("v", 1)])
        emit("vector", lambda v: v.scalar_tensor_tensor(
            cw_t[:], cw_f[:], -TWO_PI, cw_t[:], ALU.mult, ALU.add), incs=[("v", 1)])
        emit("vector", lambda v: v.tensor_single_scalar(cw_f[:], cw_t[:], -PI_LO, ALU.is_lt),
             incs=[("v", 1)])
        emit("vector", lambda v: v.scalar_tensor_tensor(
            cw_t[:], cw_f[:], TWO_PI, cw_t[:], ALU.mult, ALU.add), incs=[("v", 1)])
        emit("vector", lambda v: v.tensor_single_scalar(cw_t[:], cw_t[:], PI_LO, ALU.min),
             incs=[("v", 1)])
        emit("vector", lambda v: v.tensor_single_scalar(cw_t[:], cw_t[:], -PI_LO, ALU.max),
             incs=[("v", 1)])
        v_red = cnt["v"]
        emit("scalar", lambda s: s.activation(cw_f[:], cw_t[:], AF.Sin, scale=-1.0),
             waits=[("v", v_red)], incs=[("s", 1)])
        s_sin = cnt["s"]
        emit("vector", lambda v, p=p: v.tensor_tensor(wfull[p][:], cw_f[:], am_f[p][:], ALU.mult),
             waits=[("s", s_sin)], incs=[("v", 1)])
    emit("vector", lambda v: v.tensor_copy(gwh[:], gw_f[:]),
         waits=[("dp", dp_params)], incs=[("v", 1)])
    emit("vector", lambda v: v.tensor_tensor(gwt[:], gw_f[:], gwh[:], ALU.subtract),
         incs=[("v", 1)])
    emit("vector", lambda v: v.tensor_copy(gwl[:], gwt[:]), incs=[("v", 1)])
    v_params = cnt["v"]

    stpp = ExitStack()
    p_tr = [pt(f"p_tr{b}", [128, 1024], BF16, stpp) for b in range(2)]
    tr_free = {0: ("v", 0), 1: ("v", 0)}

    def bf_transpose(src_ap, dst_ap, idn):
        slot = bf_transpose.flip
        bf_transpose.flip ^= 1
        ps = p_tr[slot]
        emit("tensor", lambda te, s=src_ap, i=idn, ps=ps: te.transpose(
            ps[0:s.shape[1], 0:s.shape[0]], s, i),
            waits=[tr_free[slot], ("v", v_params)], incs=[("t", 1)])
        t_now = cnt["t"]
        emit("vector", lambda v, d=dst_ap, ps=ps, s=src_ap: v.tensor_copy(
            d, ps[0:s.shape[1], 0:s.shape[0]]),
            waits=[("t", t_now)], incs=[("v", 1)])
        tr_free[slot] = ("v", cnt["v"])
    bf_transpose.flip = 0

    for p in "qkvo":
        for c in range(NC8):
            bf_transpose(bas_bf[p][:, c * 128:(c + 1) * 128],
                         basT[p][:, c * HARM:(c + 1) * HARM],
                         ident_bf[0:HARM, 0:HARM])
        for c in range(NC8):
            bf_transpose(wfull[p][:, c * HARM:(c + 1) * HARM],
                         wT[p][:, c * 128:(c + 1) * 128],
                         ident_bf[:, :])
    for c in range(NC8):
        bf_transpose(gwh[:, c * 128:(c + 1) * 128], gwTh[:, c * 8:(c + 1) * 8],
                     ident_bf[0:E, 0:E])
        bf_transpose(gwl[:, c * 128:(c + 1) * 128], gwTl[:, c * 8:(c + 1) * 8],
                     ident_bf[0:E, 0:E])
    v_prologue = cnt["v"]
    stpp.close()
    stp.close()

    st_qkv = ExitStack()
    q_fm = sb("q_fm", [128, NC8 * T], BF16, st_qkv, side="right")
    k_fm = sb("k_fm", [128, NC8 * T], BF16, st_qkv, side="right")
    v_aug = sb("v_aug", [128, NT * NHEAD * 65], BF16, st_qkv, side="right")
    st1 = ExitStack()
    res_sm = {p: sb(f"res_sm_{p}", [HARM, 512], BF16, st1) for p in "qkv"}
    _vs0 = sb("v_stage0", [128, 512], BF16, st1)
    v_stage = [_vs0, _vs0]
    x_tm = [sb(f"x_tm{b}", [128, D], F32, st1) for b in range(2)]
    xn_tm = sb("xn_tm", [128, D], F32, st1)
    xnh_tm = sb("xnh_tm", [128, D], BF16, st1)
    xnl_tm = sb("xnl_tm", [128, D], BF16, st1)
    t1024 = sb("t1024", [128, D], F32, st1)
    xnl_fm = sb("xnl_fm", [128, NC8 * T], BF16, st1)
    m_t = sb("m_t", [128, 1], F32, st1)
    ss_t = sb("ss_t", [128, 1], F32, st1)
    rstd_t = sb("rstd_t", [128, 1], F32, st1)
    lg_all = sb("lg_all", [128, 128], F32, st1)
    s_all = sb("s_all", [128, 128], F32, st1)
    eq_all = sb("eq_all", [128, 128], F32, st1)
    tmp_all = eq_all
    mk1 = sb("mk1", [128, 128], F32, st1)
    mk2 = sb("mk2", [128, 128], F32, st1)
    idxB_all = sb("idxB_all", [128, 128], F32, st1)
    comb_all = lg_all
    m16 = sb("m16", [128, 16], F32, st1)
    m2_16 = sb("m2_16", [128, 16], F32, st1)
    z16 = sb("z16", [128, 16], F32, st1)
    a16 = z16
    rw16 = sb("rw16", [128, 16], F32, st1)

    def v3(th):
        return th.ap().rearrange("p (i e) -> p i e", e=8)

    def b3(th16):
        return bass.AP(th16, 0, [[16, 128], [1, 16], [0, 8]])

    emit("gpsimd", lambda g: g.iota(v3(idxB_all), [[0, 16], [1, 8]], base=int(BIG),
                                    channel_multiplier=0,
                                    allow_small_or_imprecise_dtypes=True),
         incs=[("g", 1)])
    g_idx = cnt["g"]
    v_idx_all = 0

    st1p = ExitStack()
    p_t1 = [pt(f"p_t1{b}", [128, 1024], BF16, st1p) for b in range(2)]
    p_g8 = pt("p_g8", [128, 512], F32, st1p)
    p_cb = pt("p_cb", [128, 512], F32, st1p)
    p_r = [pt(f"p_r{b}", [128, 512], F32, st1p) for b in range(2)]
    p_pj = [pt(f"p_pj{b}", [128, 512], F32, st1p) for b in range(2)]

    xt_free = [("v", 0), ("v", 0)]
    t1_free = {0: ("v", v_prologue), 1: ("v", v_prologue)}
    pg_free = ("s", 0)
    pcb_free = ("v", 0)
    pr_free = [("s", 0), ("s", 0)]
    pj_free = [("s", 0), ("s", 0)]
    rsm_free = {p: ("t", 0) for p in "qkv"}
    vstage_free = [("t", 0)]
    pjflip = [0]

    def emit_proj_tg(tg, s_fm_ready):
        for pi, p in enumerate("qkv"):
            slot = pi % 2
            for c in range(NC8):
                emit("tensor", lambda te, p=p, c=c, tg=tg, pb=p_r[slot]: te.matmul(
                    pb[0:HARM, :], basT[p][:, c * HARM:(c + 1) * HARM],
                    xnh_fm[:, c * T + tg * 512: c * T + tg * 512 + 512],
                    start=(c == 0), stop=(c == NC8 - 1)),
                    waits=([("s", s_fm_ready), pr_free[slot], rsm_free[p]]
                           if c == 0 else []),
                    incs=([("t", 1)] if c == NC8 - 1 else []))
            t_res = cnt["t"]
            emit("scalar", lambda sc_, p=p, pb=p_r[slot]: sc_.activation(
                res_sm[p][:, :], pb[0:HARM, :], AF.Copy),
                waits=[("t", t_res)], incs=[("s", 1)])
            pr_free[slot] = ("s", cnt["s"])
        s_res_all = cnt["s"]
        for p in "qkv":
            for c8 in range(NC8):
                slot = pjflip[0]
                pjflip[0] ^= 1
                emit("tensor", lambda te, p=p, c8=c8, pb=p_pj[slot]: te.matmul(
                    pb[:, :], wT[p][:, c8 * 128:(c8 + 1) * 128],
                    res_sm[p][:, :], start=True, stop=True),
                    waits=[("s", s_res_all), pj_free[slot]], incs=[("t", 1)])
                t_pr = cnt["t"]
                rsm_free[p] = ("t", t_pr)
                if p == "q" or p == "k":
                    dst = q_fm if p == "q" else k_fm
                    emit("scalar", lambda sc_, dst=dst, c8=c8, tg=tg, pb=p_pj[slot]:
                         sc_.activation(
                             dst[:, c8 * T + tg * 512: c8 * T + tg * 512 + 512],
                             pb[:, :], AF.Copy),
                         waits=[("t", t_pr)], incs=[("s", 1)])
                    pj_free[slot] = ("s", cnt["s"])
                else:
                    vs = 0
                    emit("scalar", lambda sc_, vs=vs, pb=p_pj[slot]: sc_.activation(
                        v_stage[vs][:, :], pb[:, :], AF.Copy),
                        waits=[("t", t_pr), vstage_free[0]], incs=[("s", 1)])
                    pj_free[slot] = ("s", cnt["s"])
                    s_vst = cnt["s"]
                    for tt in range(4):
                        itok = tg * 4 + tt
                        slot2 = tt % 2
                        ps = p_t1[slot2]
                        emit("tensor", lambda te, vs=vs, tt=tt, ps=ps: te.transpose(
                            ps[:, 0:128], v_stage[vs][:, tt * 128:(tt + 1) * 128],
                            ident_bf[:]),
                            waits=[("s", s_vst), t1_free[slot2]], incs=[("t", 1)])
                        t_vt = cnt["t"]
                        for half in range(2):
                            h = 2 * c8 + half
                            emit("scalar", lambda sc_, itok=itok, h=h, half=half, ps=ps:
                                 sc_.activation(
                                     v_aug[:, itok * NHEAD * 65 + h * 65:
                                           itok * NHEAD * 65 + h * 65 + 64],
                                     ps[:, half * 64: half * 64 + 64], AF.Copy),
                                 waits=[("t", t_vt)], incs=[("s", 1)])
                        t1_free[slot2] = ("s", cnt["s"])
                    vstage_free[0] = ("t", cnt["t"])
    dx_needed = []
    v_rstd_read = 0
    s_sq_prev = 0
    t_xn_read = 0

    for i in range(2):
        sem = "dxa" if i % 2 == 0 else "dxb"
        emit("sync", lambda sy, i=i: sy.dma_start(
            x_tm[i][:, :], x_h[i * 128:(i + 1) * 128, :]), incs=[(sem, 16)])
        dx_needed.append((sem, cnt[sem]))

    for i in range(NT):
        b = i % 2
        dxw = dx_needed[i]
        emit("vector", lambda v, b=b: v.reduce_sum(m_t[:], x_tm[b][:], axis=AX),
             waits=[dxw], incs=[("v", 1)])
        emit("vector", lambda v: v.tensor_single_scalar(m_t[:], m_t[:], 1.0 / D, ALU.mult),
             incs=[("v", 1)])
        emit("vector", lambda v, b=b: v.tensor_single_scalar(
            xn_tm[:], x_tm[b][:], m_t[:], ALU.subtract),
            waits=[("s", s_sq_prev)], incs=[("v", 1)])
        v_xc = cnt["v"]
        xt_free[b] = ("v", v_xc)
        if i + 2 < NT:
            sem = "dxa" if (i + 2) % 2 == 0 else "dxb"
            emit("sync", lambda sy, i=i, b=b: sy.dma_start(
                x_tm[b][:, :], x_h[(i + 2) * 128:(i + 3) * 128, :]),
                waits=[xt_free[b]], incs=[(sem, 16)])
            dx_needed.append((sem, cnt[sem]))
        emit("scalar", lambda s: s.activation(xnh_tm[:], xn_tm[:], AF.Square,
                                              accum_out=ss_t[:]),
             waits=[("v", v_xc)], incs=[("s", 1)])
        s_sq_prev = cnt["s"]
        emit("scalar", lambda s: s.activation(rstd_t[:], ss_t[:], AF.Sqrt,
                                              scale=1.0 / D, bias=epsln[:]),
             waits=[("v", max(v_const, v_rstd_read))], incs=[("s", 1)])
        s_rs = cnt["s"]
        emit("vector", lambda v: v.reciprocal(rstd_t[:], rstd_t[:]),
             waits=[("s", s_rs)], incs=[("v", 1)])
        emit("vector", lambda v: v.tensor_single_scalar(xn_tm[:], xn_tm[:], rstd_t[:], ALU.mult),
             incs=[("v", 1)])
        v_rstd_read = cnt["v"]
        emit("vector", lambda v: v.tensor_copy(xnh_tm[:], xn_tm[:]),
             waits=[("t", t_xn_read)], incs=[("v", 1)])
        emit("vector", lambda v: v.tensor_tensor(t1024[:], xn_tm[:], xnh_tm[:], ALU.subtract),
             incs=[("v", 1)])
        emit("vector", lambda v: v.tensor_copy(xnl_tm[:], t1024[:]), incs=[("v", 1)])
        v_hl = cnt["v"]
        for si, (src, dst) in enumerate(((xnh_tm, xnh_fm), (xnl_tm, xnl_fm))):
            for c in range(NC8):
                slot = c % 2
                ps = p_t1[slot]
                emit("tensor", lambda te, src=src, c=c, ps=ps: te.transpose(
                    ps[:, 0:128], src[:, c * 128:(c + 1) * 128], ident_bf[:]),
                    waits=[t1_free[slot], ("v", v_hl)], incs=[("t", 1)])
                t_now = cnt["t"]
                emit("scalar", lambda sc_, dst=dst, c=c, i=i, ps=ps: sc_.activation(
                    dst[:, c * T + i * 128: c * T + i * 128 + 128], ps[:, 0:128], AF.Copy),
                    waits=[("t", t_now)], incs=[("s", 1)])
                t1_free[slot] = ("s", cnt["s"])
        t_xn_read = cnt["t"]
        s_fm_tile = cnt["s"]
        combos = [(xnh_fm, gwTh), (xnh_fm, gwTl), (xnl_fm, gwTh)]
        nmm = 0
        for (xsrc, gsrc) in combos:
            for c in range(NC8):
                first, last = nmm == 0, nmm == 23
                emit("tensor", lambda te, xsrc=xsrc, gsrc=gsrc, c=c, i=i,
                     first=first, last=last: te.matmul(
                         p_g8[:, 0:8],
                         xsrc[:, c * T + i * 128: c * T + i * 128 + 128],
                         gsrc[:, c * 8:(c + 1) * 8],
                         start=first, stop=last),
                     waits=([("s", s_fm_tile), pg_free] if first else []),
                     incs=([("t", 1)] if last else []))
                nmm += 1
        t_g8 = cnt["t"]
        emit("vector", lambda v, i=i: v.tensor_copy(lg_all[:, i * 8:(i + 1) * 8],
                                                    p_g8[:, 0:8]),
             waits=[("t", t_g8)], incs=[("v", 1)])
        pg_free = ("v", cnt["v"])
        if i % 4 == 3:
            emit_proj_tg(i // 4, s_fm_tile)

    emit("vector", lambda v: v.tensor_reduce(m16[:], v3(lg_all), AX, ALU.max),
         incs=[("v", 1)])
    emit("vector", lambda v: v.tensor_tensor(v3(s_all), v3(lg_all), b3(m16),
                                             ALU.subtract), incs=[("v", 1)])
    v_sub = cnt["v"]
    emit("scalar", lambda s: s.activation(s_all[:], s_all[:], AF.Exp),
         waits=[("v", v_sub)], incs=[("s", 1)])
    s_expall = cnt["s"]
    emit("vector", lambda v: v.tensor_reduce(z16[:], v3(s_all), AX, ALU.add),
         waits=[("s", s_expall)], incs=[("v", 1)])
    emit("vector", lambda v: v.reciprocal(z16[:], z16[:]), incs=[("v", 1)])
    emit("vector", lambda v: v.tensor_tensor(v3(s_all), v3(s_all), b3(z16), ALU.mult),
         incs=[("v", 1)])
    emit("vector", lambda v: v.tensor_reduce(m16[:], v3(s_all), AX, ALU.max),
         incs=[("v", 1)])
    emit("vector", lambda v: v.tensor_tensor(v3(eq_all), v3(s_all), b3(m16),
                                             ALU.is_equal), incs=[("v", 1)])
    emit("vector", lambda v: v.scalar_tensor_tensor(tmp_all[:], eq_all[:], -BIG,
                                                    idxB_all[:], ALU.mult, ALU.add),
         waits=[("g", g_idx)], incs=[("v", 1)])
    emit("vector", lambda v: v.tensor_reduce(a16[:], v3(tmp_all), AX, ALU.min),
         incs=[("v", 1)])
    emit("vector", lambda v: v.tensor_tensor(v3(mk1), v3(tmp_all), b3(a16),
                                             ALU.is_equal), incs=[("v", 1)])
    emit("vector", lambda v: v.tensor_tensor(tmp_all[:], s_all[:], mk1[:], ALU.mult),
         incs=[("v", 1)])
    emit("vector", lambda v: v.tensor_tensor(s_all[:], s_all[:], tmp_all[:],
                                             ALU.subtract), incs=[("v", 1)])
    emit("vector", lambda v: v.tensor_reduce(m2_16[:], v3(s_all), AX, ALU.max),
         incs=[("v", 1)])
    emit("vector", lambda v: v.tensor_tensor(v3(eq_all), v3(s_all), b3(m2_16),
                                             ALU.is_equal), incs=[("v", 1)])
    emit("vector", lambda v: v.scalar_tensor_tensor(tmp_all[:], eq_all[:], -BIG,
                                                    idxB_all[:], ALU.mult, ALU.add),
         incs=[("v", 1)])
    emit("vector", lambda v: v.tensor_reduce(a16[:], v3(tmp_all), AX, ALU.min),
         incs=[("v", 1)])
    emit("vector", lambda v: v.tensor_tensor(v3(mk2), v3(tmp_all), b3(a16),
                                             ALU.is_equal), incs=[("v", 1)])
    emit("vector", lambda v: v.tensor_tensor(rw16[:], m16[:], m2_16[:], ALU.add),
         incs=[("v", 1)])
    emit("vector", lambda v: v.tensor_single_scalar(rw16[:], rw16[:], 1e-20, ALU.add),
         incs=[("v", 1)])
    emit("vector", lambda v: v.reciprocal(rw16[:], rw16[:]), incs=[("v", 1)])
    emit("vector", lambda v: v.tensor_single_scalar(rw16[:], rw16[:], 2.5, ALU.mult),
         incs=[("v", 1)])
    emit("vector", lambda v: v.tensor_tensor(v3(mk1), v3(mk1), b3(m16), ALU.mult),
         incs=[("v", 1)])
    emit("vector", lambda v: v.tensor_tensor(v3(mk2), v3(mk2), b3(m2_16), ALU.mult),
         incs=[("v", 1)])
    emit("vector", lambda v: v.tensor_tensor(comb_all[:], mk1[:], mk2[:], ALU.add),
         incs=[("v", 1)])
    emit("vector", lambda v: v.tensor_tensor(v3(comb_all), v3(comb_all), b3(rw16),
                                             ALU.mult), incs=[("v", 1)])
    v_comb = cnt["v"]
    for i in range(NT):
        emit("tensor", lambda te, i=i: te.transpose(
            p_cb[0:8, 0:128], comb_all[:, i * 8:(i + 1) * 8], ident_f[:]),
            waits=[("v", v_comb), pcb_free], incs=[("t", 1)])
        t_cb = cnt["t"]
        emit("vector", lambda v, i=i: v.tensor_copy(comb_bf[:, i * 128:(i + 1) * 128],
                                                    p_cb[0:8, 0:128]),
             waits=[("t", t_cb)], incs=[("v", 1)])
        pcb_free = ("v", cnt["v"])

    v_p1_done = cnt["v"]
    s_p1_done = cnt["s"]
    st1p.close()
    st1.close()

    st2o = ExitStack()
    o_heads = sb("o_heads", [128, NC8 * T], BF16, st2o)
    recip16 = sb("recip16", [NHEAD, T], BF16, st2o)
    res_bf = sb("res_bf", [HARM, T], BF16, st2o)
    mrow = sb("mrow", [1, 272], F32, st2o)
    mval = sb("mval", [1, 1], BF16, st2o)
    rrow = sb("rrow", [1, 512], F32, st2o)
    rrow_bf = [sb(f"rrow_bf{b}", [1, 512], BF16, st2o) for b in range(2)]

    st2 = ExitStack()
    zrow = [sb(f"zrow{b}", [1, 512], BF16, st2o) for b in range(2)]
    pTs = sb("pTs", [128, 4096], BF16, st2)
    colmax = sb("colmax", [128, 1], F32, st2)

    st2p = ExitStack()
    p_a = pt("p_a", [128, 512], F32, st2p)
    p_b = pt("p_b", [128, 512], F32, st2p)
    p_c = pt("p_c", [128, 512], F32, st2p)

    pa_free = [("v", v_p1_done), ("s", s_p1_done)]
    pbc_free = {0: [("v", v_p1_done), ("s", s_p1_done)],
                1: [("v", v_p1_done), ("s", s_p1_done)]}

    emit("vector", lambda v: v.memset(
        v_aug.ap().rearrange("p (i h c) -> p i h c", i=NT, h=NHEAD)[:, :, :, 64:65], 1.0),
        incs=[("v", 1)])
    v_vaug_ones = cnt["v"]
    stx = ExitStack()
    p_st = pt("p_st", [128, 2048], F32, stx)

    s_qk_done = s_p1_done
    emit("vector", lambda v: v.tensor_reduce(colmax[:], q_fm[:], AX, ALU.max,
                                             apply_absolute_value=True),
         waits=[("s", s_qk_done)], incs=[("v", 1)])
    v_cm = cnt["v"]
    emit("tensor", lambda te: te.transpose(p_a[0:1, 0:128], colmax[:], ident_f[:]),
         waits=[("v", v_cm), pa_free[0]], incs=[("t", 1)])
    t_cm = cnt["t"]
    emit("vector", lambda v: v.tensor_copy(mrow[:, 0:128], p_a[0:1, 0:128]),
         waits=[("t", t_cm)], incs=[("v", 1)])
    emit("vector", lambda v: v.reduce_max(mrow[:, 256:257], mrow[:, 0:128], axis=AX),
         incs=[("v", 1)])
    emit("vector", lambda v: v.tensor_reduce(colmax[:], k_fm[:], AX, ALU.max,
                                             apply_absolute_value=True), incs=[("v", 1)])
    v_cm2 = cnt["v"]
    emit("tensor", lambda te: te.transpose(p_a[0:1, 0:128], colmax[:], ident_f[:]),
         waits=[("v", v_cm2)], incs=[("t", 1)])
    t_cm2 = cnt["t"]
    emit("vector", lambda v: v.tensor_copy(mrow[:, 128:256], p_a[0:1, 0:128]),
         waits=[("t", t_cm2)], incs=[("v", 1)])
    pa_free[0] = ("v", cnt["v"])
    emit("vector", lambda v: v.reduce_max(mrow[:, 257:258], mrow[:, 128:256], axis=AX),
         incs=[("v", 1)])
    emit("vector", lambda v: v.tensor_tensor(mrow[:, 258:259], mrow[:, 256:257],
                                             mrow[:, 257:258], ALU.mult), incs=[("v", 1)])
    emit("vector", lambda v: v.tensor_copy(mval[:], mrow[:, 258:259]), incs=[("v", 1)])
    v_mval = cnt["v"]
    emit("tensor", lambda te: te.matmul(p_a[:, 0:1], onesrow[:], mval[:],
                                        start=True, stop=True),
         waits=[("v", v_mval)], incs=[("t", 1)])
    t_bb = cnt["t"]
    emit("vector", lambda v: v.tensor_single_scalar(bias128[:], p_a[:, 0:1], -8.0, ALU.mult),
         waits=[("t", t_bb)], incs=[("v", 1)])
    pa_free[0] = ("v", cnt["v"])
    v_bias = cnt["v"]

    rrow_free = [("dr0", 0), ("dr1", 0)]
    st_free = [("s", s_p1_done), ("v", v_p1_done)]
    av_free = {0: list(pa_free), 1: list(pbc_free[0])}
    pTs_free = {0: ("t", 0), 1: ("t", 0)}
    pairs = [(2 * hp, 2 * hp + 1, qb) for hp in range(NHEAD // 2) for qb in range(NTG)]
    for mi, (hA, hB, qb) in enumerate(pairs):
        exp_done = [0] * 8
        for pair in range(10):
            if pair >= 2:
                p2 = pair - 2
                pp2 = p2 % 2
                for sidx, h in ((0, hA), (1, hB)):
                    avbank = p_a if sidx == 0 else p_b
                    for j in (0, 1):
                        kc = 2 * p2 + j
                        emit("tensor", lambda te, h=h, kc=kc, sidx=sidx, j=j, pp2=pp2,
                             pb=avbank,
                             first=(kc == 0), last=(kc == 15): te.matmul(
                                 pb[0:65, :],
                                 v_aug[:, kc * NHEAD * 65 + h * 65: kc * NHEAD * 65 + h * 65 + 65],
                                 pTs[:, pp2 * 2048 + sidx * 1024 + j * 512:
                                     pp2 * 2048 + sidx * 1024 + j * 512 + 512],
                                 start=first, stop=last),
                            waits=([("s", exp_done[p2])] if (sidx == 0 and j == 0) else []) +
                                  (av_free[sidx] + ([("v", v_vaug_ones)] if mi == 0 else [])
                                   if kc == 0 else []),
                            incs=[("t", 1)])
                pTs_free[pp2] = ("t", cnt["t"])
            if pair < 8:
                pp = pair % 2
                first = True
                for j in (0, 1):
                    for sidx, h in ((0, hA), (1, hB)):
                        c8, off = h // 2, (h % 2) * 64
                        kc = 2 * pair + j
                        emit("tensor", lambda te, c8=c8, off=off, kc=kc, qb=qb,
                             sidx=sidx, j=j: te.matmul(
                                 p_st[:, sidx * 1024 + j * 512: sidx * 1024 + j * 512 + 512],
                                 k_fm[off:off + 64, c8 * T + kc * 128: c8 * T + kc * 128 + 128],
                                 q_fm[off:off + 64, c8 * T + qb * 512: c8 * T + qb * 512 + 512],
                                 start=True, stop=True),
                            waits=(st_free if first else []) +
                                  ([("v", v_bias)] if (mi == 0 and pair == 0 and first)
                                   else []),
                            incs=[("t", 1)])
                        first = False
                t_sp = cnt["t"]
                emit("scalar", lambda s, pp=pp: s.activation(
                    pTs[:, pp * 2048: pp * 2048 + 2048],
                    p_st[:, 0:2048], AF.Exp,
                    scale=0.125, bias=bias128[:]),
                    waits=[("t", t_sp), pTs_free[pp]], incs=[("s", 1)])
                st_free = [("s", cnt["s"])]
                exp_done[pair] = cnt["s"]
        t_av = cnt["t"]
        for sidx, h in ((0, hA), (1, hB)):
            c8, off = h // 2, (h % 2) * 64
            avbank = p_a if sidx == 0 else p_b
            emit("vector", lambda v, pb=avbank, slot=sidx: v.tensor_copy(
                zrow[slot][:, :], pb[64:65, :]),
                waits=[("t", t_av)], incs=[("v", 1)])
            emit("vector", lambda v, qb=qb, c8=c8, off=off, pb=avbank: v.tensor_copy(
                o_heads[off:off + 64, c8 * T + qb * 512: c8 * T + qb * 512 + 512],
                pb[0:64, :]),
                incs=[("v", 1)])
            av_free[sidx] = [("v", cnt["v"])]
        for sidx, h in ((0, hA), (1, hB)):
            qb_ = qb
            emit("vector", lambda v, slot=sidx: v.reciprocal(rrow[:], zrow[slot][:, :]),
                 incs=[("v", 1)])
            emit("vector", lambda v, slot=sidx: v.tensor_copy(
                rrow_bf[slot][:, :], rrow[:]),
                waits=[rrow_free[sidx]], incs=[("v", 1)])
            v_rr = cnt["v"]
            drsem = "dr0" if sidx == 0 else "dr1"
            emit("sync", lambda sy, h=h, qb=qb_, slot=sidx: sy.dma_start(
                recip16[h:h + 1, qb * 512:(qb + 1) * 512], rrow_bf[slot][:, :]),
                waits=[("v", v_rr)], incs=[(drsem, 16)])
            rrow_free[sidx] = (drsem, cnt[drsem])
    v_recip_all = cnt["v"]
    dr0_all, dr1_all = cnt["dr0"], cnt["dr1"]

    pbc2_free = list(av_free[1]) + list(pbc_free[1])
    for c8 in range(NC8):
        for tg in range(NTG):
            emit("tensor", lambda te, c8=c8, tg=tg: te.matmul(
                p_b[:, :], selc[:, c8 * 128:(c8 + 1) * 128],
                recip16[:, tg * 512:(tg + 1) * 512], start=True, stop=True),
                waits=[("v", v_recip_all), ("dr0", dr0_all), ("dr1", dr1_all)]
                      + list(pbc2_free), incs=[("t", 1)])
            t_bc = cnt["t"]
            emit("vector", lambda v, c8=c8, tg=tg: v.tensor_tensor(
                o_heads[:, c8 * T + tg * 512: c8 * T + tg * 512 + 512],
                o_heads[:, c8 * T + tg * 512: c8 * T + tg * 512 + 512],
                p_b[:, :], ALU.mult),
                waits=[("t", t_bc)], incs=[("v", 1)])
            pbc2_free = [("v", cnt["v"])]
    v_onorm = cnt["v"]

    st2.close()
    st_qkv.close()
    st_acc = ExitStack()
    acc = sb("acc", [128, NC8 * T], F32, st_acc, side="right")

    pa_free2 = [av_free[0][0]]
    pbc_free2 = {0: ("v", v_onorm), 1: ("v", v_onorm)}
    for tg in range(NTG):
        for c in range(NC8):
            emit("tensor", lambda te, c=c, tg=tg: te.matmul(
                p_a[0:HARM, :], basT["o"][:, c * HARM:(c + 1) * HARM],
                o_heads[:, c * T + tg * 512: c * T + tg * 512 + 512],
                start=(c == 0), stop=(c == NC8 - 1)),
                waits=([("v", v_onorm), pa_free2[0], av_free[0][-1]] if c == 0 else []),
                incs=([("t", 1)] if c == NC8 - 1 else []))
        t_ro = cnt["t"]
        emit("scalar", lambda s, tg=tg: s.activation(
            res_bf[:, tg * 512: tg * 512 + 512], p_a[0:HARM, :], AF.Copy),
            waits=[("t", t_ro)], incs=[("s", 1)])
        pa_free2[0] = ("s", cnt["s"])
    s_reso = cnt["s"]
    for c8 in range(NC8):
        for tg in range(NTG):
            slot = (c8 * NTG + tg) % 2
            pbank = p_b if slot == 0 else p_c
            emit("tensor", lambda te, c8=c8, tg=tg, pb=pbank: te.matmul(
                pb[:, :], wT["o"][:, c8 * 128:(c8 + 1) * 128],
                res_bf[:, tg * 512: tg * 512 + 512], start=True, stop=True),
                waits=[("s", s_reso), pbc_free2[slot]], incs=[("t", 1)])
            t_ao = cnt["t"]
            emit("vector", lambda v, c8=c8, tg=tg, pb=pbank: v.tensor_copy(
                acc[:, c8 * T + tg * 512: c8 * T + tg * 512 + 512], pb[:, :]),
                waits=[("t", t_ao)], incs=[("v", 1)])
            pbc_free2[slot] = ("v", cnt["v"])
    v_acc0 = cnt["v"]

    st2o.close()
    stx.close()
    st2p.close()
    stw.close()

    st3 = ExitStack()
    wgbuf = [sb(f"wgbuf{b}", [128, GSZ * 1024], BF16, st3) for b in range(2)]
    wubuf = [sb(f"wubuf{b}", [128, GSZ * 1024], BF16, st3) for b in range(2)]
    wdbuf = [sb(f"wdbuf{b}", [128, GSZ * 1024], BF16, st3) for b in range(2)]
    gbuf = [sb(f"gbuf{b}", [128, GSZ * T], BF16, st3) for b in range(2)]
    c_cur = sb("c_cur", [128, T], F32, st3)
    sgt = [sb(f"sgt{b}", [128, 512], BF16, st3) for b in range(2)]

    st3p = ExitStack()
    p_xg = pt("p_xg", [128, 512], F32, st3p)
    p_xu = pt("p_xu", [128, 512], F32, st3p)
    p_d0 = pt("p_d0", [128, 512], F32, st3p)
    p_d1 = pt("p_d1", [128, 512], F32, st3p)
    p_cc = pt("p_cc", [128, 512], F32, st3p)

    groups = [list(range(g, min(g + GSZ, NMOE))) for g in range(0, NMOE, GSZ)]
    groups += [list(range(g, min(g + GSZ, NHID))) for g in range(NMOE, NHID, GSZ)]

    def chunk_srcs(h):
        if h < NMOE:
            e, fi = h // NFM, h % NFM
            return (w_gate_h[e, :, fi * 128:(fi + 1) * 128],
                    w_up_h[e, :, fi * 128:(fi + 1) * 128],
                    w_down_h[e, fi * 128:(fi + 1) * 128, :])
        fi = h - NMOE
        return (sw_gate_h[:, fi * 128:(fi + 1) * 128],
                sw_up_h[:, fi * 128:(fi + 1) * 128],
                sw_down_h[fi * 128:(fi + 1) * 128, :])

    def emit_group_dma(gi, buf_free_wait):
        buf = gi % 2
        sem = "dwa" if buf == 0 else "dwb"
        first = True
        for loc, h in enumerate(groups[gi]):
            sg, su, sd = chunk_srcs(h)
            for (dst, src) in ((wgbuf[buf], sg), (wubuf[buf], su)):
                emit("gpsimd", lambda g, dst=dst, loc=loc, src=src: g.dma_start(
                    dst.ap()[:, loc * 1024:(loc + 1) * 1024].rearrange(
                        "p (c f) -> p c f", c=NC8),
                    src.rearrange("(c p) f -> p c f", p=128)),
                    waits=([buf_free_wait] if first else []),
                    incs=[(sem, 16)])
                first = False
            emit("gpsimd", lambda g, buf=buf, loc=loc, sd=sd: g.dma_start(
                wdbuf[buf][:, loc * 1024:(loc + 1) * 1024], sd),
                incs=[(sem, 16)])
        return (sem, cnt[sem])

    dma_done = [emit_group_dma(0, ("v", v_acc0)), emit_group_dma(1, ("v", v_acc0))]
    grp_last_t = [0] * len(groups)

    pxg_free = ("v", v_acc0)
    pxu_free = ("v", v_acc0)
    pd_free = {0: ("v", v_acc0), 1: ("v", v_acc0)}
    pcc_free = ("v", v_acc0)
    sgt_free = [("v", 0), ("v", 0)]
    gbuf_free = [("t", 0), ("t", 0)]
    cur_e = -1
    su_slot = 0
    add_done = {}

    for gi, grp in enumerate(groups):
        buf = gi % 2
        dww = dma_done[gi]
        for loc, h in enumerate(grp):
            is_moe = h < NMOE
            if is_moe and (h // NFM) != cur_e:
                cur_e = h // NFM
                for tg in range(NTG):
                    emit("tensor", lambda te, e=cur_e, tg=tg: te.matmul(
                        p_cc[:, :], sele[:, e * 128:(e + 1) * 128],
                        comb_bf[0:E, tg * 512:(tg + 1) * 512],
                        start=True, stop=True),
                        waits=[("v", v_p1_done), pcc_free], incs=[("t", 1)])
                    t_cc = cnt["t"]
                    emit("vector", lambda v, tg=tg: v.tensor_copy(
                        c_cur[:, tg * 512:(tg + 1) * 512], p_cc[:, :]),
                        waits=[("t", t_cc)], incs=[("v", 1)])
                    pcc_free = ("v", cnt["v"])
            v_ccur = cnt["v"]
            for tg in range(NTG):
                for c in range(NC8):
                    emit("tensor", lambda te, buf=buf, loc=loc, c=c, tg=tg: te.matmul(
                        p_xg[:, :],
                        wgbuf[buf][:, loc * 1024 + c * 128: loc * 1024 + c * 128 + 128],
                        xnh_fm[:, c * T + tg * 512: c * T + tg * 512 + 512],
                        start=(c == 0), stop=(c == NC8 - 1)),
                        waits=([dww, pxg_free] if c == 0 else []),
                        incs=([("t", 1)] if c == NC8 - 1 else []))
                t_xg = cnt["t"]
                for c in range(NC8):
                    emit("tensor", lambda te, buf=buf, loc=loc, c=c, tg=tg: te.matmul(
                        p_xu[:, :],
                        wubuf[buf][:, loc * 1024 + c * 128: loc * 1024 + c * 128 + 128],
                        xnh_fm[:, c * T + tg * 512: c * T + tg * 512 + 512],
                        start=(c == 0), stop=(c == NC8 - 1)),
                        waits=([pxu_free] if c == 0 else []),
                        incs=([("t", 1)] if c == NC8 - 1 else []))
                t_xu = cnt["t"]
                ss = su_slot % 2
                su_slot += 1
                emit("scalar", lambda s, ss=ss: s.activation(sgt[ss][:], p_xg[:, :], AF.Silu),
                     waits=[("t", t_xg), sgt_free[ss]], incs=[("s", 1)])
                pxg_free = ("s", cnt["s"])
                s_silu = cnt["s"]
                gslice = gbuf[buf][:, loc * T + tg * 512: loc * T + tg * 512 + 512]
                emit("vector", lambda v, gs=gslice, ss=ss: v.tensor_tensor(
                    gs, sgt[ss][:], p_xu[:, :], ALU.mult),
                    waits=[("t", t_xu), ("s", s_silu)] +
                          ([gbuf_free[buf]] if (loc == 0 and tg == 0) else []),
                    incs=[("v", 1)])
                pxu_free = ("v", cnt["v"])
                sgt_free[ss] = ("v", cnt["v"])
                if is_moe:
                    emit("vector", lambda v, gs=gslice, tg=tg: v.tensor_tensor(
                        gs, gs, c_cur[:, tg * 512:(tg + 1) * 512], ALU.mult),
                        waits=[("v", v_ccur)], incs=[("v", 1)])
        v_g_grp = cnt["v"]
        for tg in range(NTG):
            for dc in range(NC8):
                slot = (tg * NC8 + dc) % 2
                pbank = p_d0 if slot == 0 else p_d1
                for li in range(len(grp)):
                    emit("tensor", lambda te, buf=buf, li=li, dc=dc, tg=tg, pb=pbank,
                         first=(li == 0), last=(li == len(grp) - 1): te.matmul(
                             pb[:, :],
                             wdbuf[buf][:, li * 1024 + dc * 128: li * 1024 + dc * 128 + 128],
                             gbuf[buf][:, li * T + tg * 512: li * T + tg * 512 + 512],
                             start=first, stop=last),
                        waits=([("v", v_g_grp), pd_free[slot]] if li == 0 else []),
                        incs=([("t", 1)] if li == len(grp) - 1 else []))
                t_dn = cnt["t"]
                emit("vector", lambda v, dc=dc, tg=tg, pb=pbank: v.tensor_tensor(
                    acc[:, dc * T + tg * 512: dc * T + tg * 512 + 512],
                    acc[:, dc * T + tg * 512: dc * T + tg * 512 + 512],
                    pb[:, :], ALU.add),
                    waits=[("t", t_dn)], incs=[("v", 1)])
                if gi == len(groups) - 1:
                    add_done[(tg, dc)] = cnt["v"]
                pd_free[slot] = ("v", cnt["v"])
        grp_last_t[gi] = cnt["t"]
        gbuf_free[buf] = ("t", cnt["t"])
        if gi + 2 < len(groups):
            dma_done.append(emit_group_dma(gi + 2, ("t", grp_last_t[gi])))
    v_moe_done = cnt["v"]

    st3p.close()
    st3.close()

    st4 = ExitStack()
    x4 = [sb(f"x4_{b}", [128, D], F32, st4, side="right") for b in range(2)]
    o4 = [sb(f"o4_{b}", [128, D], F32, st4, side="right") for b in range(2)]
    st4p = ExitStack()
    p_o = [pt(f"p_o{b}", [128, 512], F32, st4p) for b in range(2)]

    x4_free = [("v", 0)] * 2
    o4_free = [("doa", 0), ("dob", 0)]
    po_free = {0: ("v", v_moe_done), 1: ("v", v_moe_done)}
    dx4 = []
    X4SEMS = ("dxa", "dxb")
    for i in range(2):
        sem = X4SEMS[i % 2]
        emit("sync", lambda sy, i=i: sy.dma_start(x4[i][:, :], x_h[i * 128:(i + 1) * 128, :]),
             waits=[(sem, cnt[sem])], incs=[(sem, 16)])
        dx4.append((sem, cnt[sem]))
    for i in range(NT):
        b = i % 2
        ob = i % 2
        for c8 in range(NC8):
            slot = (i * NC8 + c8) % 2
            emit("tensor", lambda te, c8=c8, i=i, slot=slot: te.transpose(
                p_o[slot][:, 0:128], acc[:, c8 * T + i * 128: c8 * T + i * 128 + 128],
                ident_f[:]),
                waits=[("v", add_done[(i // 4, c8)]), po_free[slot]], incs=[("t", 1)])
            t_tr = cnt["t"]
            emit("vector", lambda v, c8=c8, b=b, ob=ob, slot=slot: v.tensor_tensor(
                o4[ob][:, c8 * 128:(c8 + 1) * 128], x4[b][:, c8 * 128:(c8 + 1) * 128],
                p_o[slot][:, 0:128], ALU.add),
                waits=[("t", t_tr), dx4[i], o4_free[ob]], incs=[("v", 1)])
            po_free[slot] = ("v", cnt["v"])
        v_out = cnt["v"]
        x4_free[b] = ("v", v_out)
        if i + 2 < NT:
            sem = X4SEMS[(i + 2) % 2]
            emit("sync", lambda sy, i=i, b=b: sy.dma_start(
                x4[b][:, :], x_h[(i + 2) * 128:(i + 3) * 128, :]),
                waits=[x4_free[b], (sem, cnt[sem])], incs=[(sem, 16)])
            dx4.append((sem, cnt[sem]))
        dosem = "doa" if ob == 0 else "dob"
        emit("sync", lambda sy, i=i, ob=ob: sy.dma_start(
            out_h[i * 128:(i + 1) * 128, :], o4[ob][:, :]),
            waits=[("v", v_out)], incs=[(dosem, 16)])
        o4_free[ob] = (dosem, cnt[dosem])
    doa_total, dob_total = cnt["doa"], cnt["dob"]

    with ExitStack() as st_sem:
        sems = {nm: st_sem.enter_context(nc.semaphore(nm)) for nm in cnt}
        with nc.Block() as block:
            def runner(engname):
                ss = {"vector": "v", "scalar": "s"}.get(engname)
                def run(e):
                    for k, (fn, waits, incs) in enumerate(ops[engname]):
                        if ss is not None and k > 0:
                            e.wait_ge(sems[ss], k)
                        for sname, val in waits:
                            e.wait_ge(sems[sname], val)
                        inst = fn(e)
                        for sname, amt in incs:
                            inst.then_inc(sems[sname], amt)
                    if engname == "sync":
                        e.wait_ge(sems["doa"], doa_total)
                        e.wait_ge(sems["dob"], dob_total)
                return run
            block.tensor(runner("tensor"))
            block.vector(runner("vector"))
            block.scalar(runner("scalar"))
            block.gpsimd(runner("gpsimd"))
            block.sync(runner("sync"))

    st4p.close()
    st4.close()
    st_acc.close()
    so.close()
    return nc


_NC_CACHE = None


def kernel(**inputs):
    global _NC_CACHE
    if _NC_CACHE is None:
        _NC_CACHE = build_nc()
    nc = _NC_CACHE
    x = np.ascontiguousarray(np.asarray(inputs["L0_attn_x"], dtype=np.float32))
    common = {
        "gate_w": np.ascontiguousarray(inputs["L0_ffn_gate_w"], dtype=np.float32),
        "w_gate": np.ascontiguousarray(inputs["L0_ffn_w_gate"], dtype=np.float32),
        "w_up": np.ascontiguousarray(inputs["L0_ffn_w_up"], dtype=np.float32),
        "w_down": np.ascontiguousarray(inputs["L0_ffn_w_down"], dtype=np.float32),
        "sw_gate": np.ascontiguousarray(inputs["L0_ffn_sw_gate"], dtype=np.float32),
        "sw_up": np.ascontiguousarray(inputs["L0_ffn_sw_up"], dtype=np.float32),
        "sw_down": np.ascontiguousarray(inputs["L0_ffn_sw_down"], dtype=np.float32),
    }
    for p in "qkvo":
        common[f"basis_{p}"] = np.ascontiguousarray(inputs[f"L0_attn_basis_{p}"], dtype=np.float32)
        common[f"phase_{p}"] = np.ascontiguousarray(inputs[f"L0_attn_phase_{p}"], dtype=np.float32)
        common[f"amp_{p}"] = np.ascontiguousarray(inputs[f"L0_attn_amp_{p}"], dtype=np.float32)
    in_maps = [dict(common, x=np.ascontiguousarray(x[i])) for i in range(8)]
    res = run_bass_kernel_spmd(nc, in_maps, core_ids=list(range(8)))
    return np.stack([r["out"] for r in res.results], axis=0)


if __name__ == "__main__":
    build_nc()
    print("built ok")


# version 10
# speedup vs baseline: 1.0225x; 1.0225x over previous
import numpy as np
from contextlib import ExitStack

import concourse.bass as bass
import concourse.mybir as mybir
from concourse.bass_utils import run_bass_kernel_spmd

F32 = mybir.dt.float32
BF16 = mybir.dt.bfloat16
I32 = mybir.dt.int32
AF = mybir.ActivationFunctionType
ALU = mybir.AluOpType
AX = mybir.AxisListType.X

T = 2048
D = 1024
NC8 = 8
NT = 16
NTG = 4
NHEAD = 16
HARM = 32
E = 8
NFM = 11
NMOE = E * NFM
NFS = 22
NHID = NMOE + NFS
GSZ = 4
BIG = 100.0
EPS_LN = 1e-5
TWO_PI = 6.283185307179586
HALF_PI = 1.5707963267948966
INV_2PI = 1.0 / TWO_PI
PI_LO = 3.1415925

ENGS = ("tensor", "vector", "scalar", "gpsimd", "sync")


def build_nc():
    nc = bass.Bass("TRN2", target_bir_lowering=False)

    x_h = nc.dram_tensor("x", [T, D], F32, kind="ExternalInput")
    basis = {p: nc.dram_tensor(f"basis_{p}", [HARM, D], F32, kind="ExternalInput")
             for p in "qkvo"}
    phase = {p: nc.dram_tensor(f"phase_{p}", [D, HARM], F32, kind="ExternalInput")
             for p in "qkvo"}
    amp = {p: nc.dram_tensor(f"amp_{p}", [D, HARM], F32, kind="ExternalInput")
           for p in "qkvo"}
    gate_w_h = nc.dram_tensor("gate_w", [E, D], F32, kind="ExternalInput")
    w_gate_h = nc.dram_tensor("w_gate", [E, D, 1408], F32, kind="ExternalInput")
    w_up_h = nc.dram_tensor("w_up", [E, D, 1408], F32, kind="ExternalInput")
    w_down_h = nc.dram_tensor("w_down", [E, 1408, D], F32, kind="ExternalInput")
    sw_gate_h = nc.dram_tensor("sw_gate", [D, 2816], F32, kind="ExternalInput")
    sw_up_h = nc.dram_tensor("sw_up", [D, 2816], F32, kind="ExternalInput")
    sw_down_h = nc.dram_tensor("sw_down", [2816, D], F32, kind="ExternalInput")
    out_h = nc.dram_tensor("out", [T, D], F32, kind="ExternalOutput")

    ops = {e: [] for e in ENGS}
    cnt = {s: 0 for s in ("t", "v", "s", "g", "dp", "dwa", "dwb",
                          "dxa", "dxb", "doa", "dob", "dr0", "dr1")}

    SELF_SEM = {"vector": "v", "scalar": "s"}

    def emit(eng, fn, waits=(), incs=()):
        for sname, val in waits:
            assert val <= cnt[sname], f"wait-ahead: {sname} {val} > {cnt[sname]}"
        incs = list(incs)
        ss = SELF_SEM.get(eng)
        if ss is not None and not any(n == ss for n, _ in incs):
            incs.append((ss, 1))
        rec = []
        for sname, amt in incs:
            cnt[sname] += amt
            rec.append((sname, amt))
        ops[eng].append((fn, tuple(waits), tuple(rec)))

    def sb(name, shape, dtype, stack, side=None):
        if side is None:
            return stack.enter_context(nc.sbuf_tensor(name, shape, dtype))
        return stack.enter_context(nc.sbuf_tensor(name, shape, dtype, side=side))

    def pt(name, shape, dtype, stack):
        return stack.enter_context(nc.psum_tensor(name, shape, dtype))

    so = ExitStack()
    ident_bf = sb("ident_bf", [128, 128], BF16, so)
    ident_f = sb("ident_f", [128, 128], F32, so)
    st_iota = ExitStack()
    iota2d = sb("iota2d", [128, 128], F32, st_iota, side="right")
    onesrow = sb("onesrow", [1, 128], BF16, so)
    epsln = sb("epsln", [128, 1], F32, so)
    selc = sb("selc", [16, NC8 * 128], BF16, so)
    sele = sb("sele", [E, E * 128], BF16, so)
    xnh_fm = sb("xnh_fm", [128, NC8 * T], BF16, so)
    comb_bf = sb("comb_bf", [E, T], BF16, so)
    bias128 = sb("bias128", [128, 1], F32, so)
    stw = ExitStack()
    basT = {p: sb(f"basT_{p}", [128, NC8 * HARM], BF16, stw) for p in "qkvo"}
    wT = {p: sb(f"wT_{p}", [HARM, D], BF16, stw) for p in "qkvo"}
    gwTh = sb("gwTh", [128, NC8 * 8], BF16, stw)
    gwTl = sb("gwTl", [128, NC8 * 8], BF16, stw)

    emit("gpsimd", lambda g: g.iota(iota2d[:, :], [[1, 128]], channel_multiplier=-1,
                                    allow_small_or_imprecise_dtypes=True),
         incs=[("g", 1)])
    g_iota = cnt["g"]
    emit("vector", lambda v: v.tensor_single_scalar(ident_bf[:], iota2d[:], 0.0, ALU.is_equal),
         waits=[("g", g_iota)], incs=[("v", 1)])
    emit("vector", lambda v: v.tensor_single_scalar(ident_f[:], iota2d[:], 0.0, ALU.is_equal),
         incs=[("v", 1)])
    emit("vector", lambda v: v.memset(onesrow[:], 1.0), incs=[("v", 1)])
    emit("vector", lambda v: v.memset(epsln[:], EPS_LN), incs=[("v", 1)])
    v_const = cnt["v"]
    st_iota.close()

    stp = ExitStack()
    bas_bf = {p: sb(f"bas_bf_{p}", [HARM, D], BF16, stp, side="right") for p in "qkvo"}
    ph_f = {p: sb(f"ph_f_{p}", [128, NC8 * HARM], F32, stp, side="right") for p in "qkvo"}
    am_f = {p: sb(f"am_f_{p}", [128, NC8 * HARM], F32, stp, side="right") for p in "qkvo"}
    gw_f = sb("gw_f", [E, D], F32, stp, side="right")
    wfull = {p: sb(f"wfull_{p}", [128, NC8 * HARM], BF16, stp, side="right") for p in "qkvo"}
    cw_t = sb("cw_t", [128, NC8 * HARM], F32, stp, side="right")
    cw_k = sb("cw_k", [128, NC8 * HARM], I32, stp, side="right")
    cw_f = sb("cw_f", [128, NC8 * HARM], F32, stp, side="right")
    gwh = sb("gwh", [E, D], BF16, stp, side="right")
    gwl = sb("gwl", [E, D], BF16, stp, side="right")
    gwt = sb("gwt", [E, D], F32, stp, side="right")
    scr_sel = sb("scr_sel", [16, 1024], F32, stp, side="right")

    emit("gpsimd", lambda g: g.iota(
        scr_sel.ap().rearrange("p (g j) -> p g j", g=16),
        [[1, 16], [0, 64]], channel_multiplier=-1,
        allow_small_or_imprecise_dtypes=True), incs=[("g", 1)])
    g_sel = cnt["g"]
    emit("vector", lambda v: v.tensor_single_scalar(selc[:, :], scr_sel[:, :],
                                                    0.0, ALU.is_equal),
         waits=[("g", g_sel)], incs=[("v", 1)])
    v_sel1 = cnt["v"]
    emit("gpsimd", lambda g: g.iota(
        scr_sel[0:8, :].rearrange("p (f j) -> p f j", f=8),
        [[1, 8], [0, 128]], channel_multiplier=-1,
        allow_small_or_imprecise_dtypes=True),
        waits=[("v", v_sel1)], incs=[("g", 1)])
    g_sel2 = cnt["g"]
    emit("vector", lambda v: v.tensor_single_scalar(sele[:, :], scr_sel[0:8, :],
                                                    0.0, ALU.is_equal),
         waits=[("g", g_sel2)], incs=[("v", 1)])

    for p in "qkvo":
        emit("gpsimd", lambda g, p=p: g.dma_start(bas_bf[p][:], basis[p][:, :]),
             incs=[("dp", 16)])
        emit("gpsimd", lambda g, p=p: g.dma_start(
            ph_f[p].ap().rearrange("p (c h) -> p c h", c=NC8),
            phase[p].ap().rearrange("(c p) h -> p c h", p=128)),
            incs=[("dp", 16)])
        emit("gpsimd", lambda g, p=p: g.dma_start(
            am_f[p].ap().rearrange("p (c h) -> p c h", c=NC8),
            amp[p].ap().rearrange("(c p) h -> p c h", p=128)),
            incs=[("dp", 16)])
    emit("gpsimd", lambda g: g.dma_start(gw_f[:], gate_w_h[:, :]), incs=[("dp", 16)])
    dp_params = cnt["dp"]

    for p in "qkvo":
        P = ph_f[p]
        emit("vector", lambda v, P=P: v.tensor_scalar(
            cw_t[:], P[:], HALF_PI, INV_2PI, ALU.add, ALU.mult),
            waits=[("dp", dp_params)], incs=[("v", 1)])
        emit("vector", lambda v: v.tensor_copy(cw_k[:], cw_t[:]), incs=[("v", 1)])
        emit("vector", lambda v: v.tensor_copy(cw_f[:], cw_k[:]), incs=[("v", 1)])
        emit("vector", lambda v, P=P: v.tensor_single_scalar(cw_t[:], P[:], HALF_PI, ALU.add),
             incs=[("v", 1)])
        emit("vector", lambda v: v.scalar_tensor_tensor(
            cw_t[:], cw_f[:], -TWO_PI, cw_t[:], ALU.mult, ALU.add), incs=[("v", 1)])
        emit("vector", lambda v: v.tensor_single_scalar(cw_f[:], cw_t[:], PI_LO, ALU.is_gt),
             incs=[("v", 1)])
        emit("vector", lambda v: v.scalar_tensor_tensor(
            cw_t[:], cw_f[:], -TWO_PI, cw_t[:], ALU.mult, ALU.add), incs=[("v", 1)])
        emit("vector", lambda v: v.tensor_single_scalar(cw_f[:], cw_t[:], -PI_LO, ALU.is_lt),
             incs=[("v", 1)])
        emit("vector", lambda v: v.scalar_tensor_tensor(
            cw_t[:], cw_f[:], TWO_PI, cw_t[:], ALU.mult, ALU.add), incs=[("v", 1)])
        emit("vector", lambda v: v.tensor_single_scalar(cw_t[:], cw_t[:], PI_LO, ALU.min),
             incs=[("v", 1)])
        emit("vector", lambda v: v.tensor_single_scalar(cw_t[:], cw_t[:], -PI_LO, ALU.max),
             incs=[("v", 1)])
        v_red = cnt["v"]
        emit("scalar", lambda s: s.activation(cw_f[:], cw_t[:], AF.Sin, scale=-1.0),
             waits=[("v", v_red)], incs=[("s", 1)])
        s_sin = cnt["s"]
        emit("vector", lambda v, p=p: v.tensor_tensor(wfull[p][:], cw_f[:], am_f[p][:], ALU.mult),
             waits=[("s", s_sin)], incs=[("v", 1)])
    emit("vector", lambda v: v.tensor_copy(gwh[:], gw_f[:]),
         waits=[("dp", dp_params)], incs=[("v", 1)])
    emit("vector", lambda v: v.tensor_tensor(gwt[:], gw_f[:], gwh[:], ALU.subtract),
         incs=[("v", 1)])
    emit("vector", lambda v: v.tensor_copy(gwl[:], gwt[:]), incs=[("v", 1)])
    v_params = cnt["v"]

    stpp = ExitStack()
    p_tr = [pt(f"p_tr{b}", [128, 1024], BF16, stpp) for b in range(2)]
    tr_free = {0: ("v", 0), 1: ("v", 0)}

    def bf_transpose(src_ap, dst_ap, idn):
        slot = bf_transpose.flip
        bf_transpose.flip ^= 1
        ps = p_tr[slot]
        emit("tensor", lambda te, s=src_ap, i=idn, ps=ps: te.transpose(
            ps[0:s.shape[1], 0:s.shape[0]], s, i),
            waits=[tr_free[slot], ("v", v_params)], incs=[("t", 1)])
        t_now = cnt["t"]
        emit("vector", lambda v, d=dst_ap, ps=ps, s=src_ap: v.tensor_copy(
            d, ps[0:s.shape[1], 0:s.shape[0]]),
            waits=[("t", t_now)], incs=[("v", 1)])
        tr_free[slot] = ("v", cnt["v"])
    bf_transpose.flip = 0

    for p in "qkvo":
        for c in range(NC8):
            bf_transpose(bas_bf[p][:, c * 128:(c + 1) * 128],
                         basT[p][:, c * HARM:(c + 1) * HARM],
                         ident_bf[0:HARM, 0:HARM])
        for c in range(NC8):
            bf_transpose(wfull[p][:, c * HARM:(c + 1) * HARM],
                         wT[p][:, c * 128:(c + 1) * 128],
                         ident_bf[:, :])
    for c in range(NC8):
        bf_transpose(gwh[:, c * 128:(c + 1) * 128], gwTh[:, c * 8:(c + 1) * 8],
                     ident_bf[0:E, 0:E])
        bf_transpose(gwl[:, c * 128:(c + 1) * 128], gwTl[:, c * 8:(c + 1) * 8],
                     ident_bf[0:E, 0:E])
    v_prologue = cnt["v"]
    stpp.close()
    stp.close()

    st_qkv = ExitStack()
    q_fm = sb("q_fm", [128, NC8 * T], BF16, st_qkv, side="right")
    k_fm = sb("k_fm", [128, NC8 * T], BF16, st_qkv, side="right")
    v_aug = sb("v_aug", [128, NT * NHEAD * 65], BF16, st_qkv, side="right")
    st1 = ExitStack()
    res_sm = {p: sb(f"res_sm_{p}", [HARM, 512], BF16, st1) for p in "qkv"}
    _vs0 = sb("v_stage0", [128, 512], BF16, st1)
    v_stage = [_vs0, _vs0]
    x_tm = [sb(f"x_tm{b}", [128, D], F32, st1) for b in range(2)]
    xn_tm = sb("xn_tm", [128, D], F32, st1)
    xnh_tm = sb("xnh_tm", [128, D], BF16, st1)
    xnl_t = sb("xnl_t", [128, D], BF16, st1)
    m_t = sb("m_t", [128, 1], F32, st1)
    ss_t = sb("ss_t", [128, 1], F32, st1)
    rstd_t = sb("rstd_t", [128, 1], F32, st1)
    lg_all = sb("lg_all", [128, 128], F32, st1)
    s_all = sb("s_all", [128, 128], F32, st1)
    eq_all = sb("eq_all", [128, 128], F32, st1)
    tmp_all = eq_all
    mk1 = sb("mk1", [128, 128], F32, st1)
    mk2 = sb("mk2", [128, 128], F32, st1)
    idxB_all = sb("idxB_all", [128, 128], F32, st1)
    comb_all = lg_all
    m16 = sb("m16", [128, 16], F32, st1)
    m2_16 = sb("m2_16", [128, 16], F32, st1)
    z16 = sb("z16", [128, 16], F32, st1)
    a16 = z16
    rw16 = sb("rw16", [128, 16], F32, st1)

    def v3(th):
        return th.ap().rearrange("p (i e) -> p i e", e=8)

    def b3(th16):
        return bass.AP(th16, 0, [[16, 128], [1, 16], [0, 8]])

    emit("gpsimd", lambda g: g.iota(v3(idxB_all), [[0, 16], [1, 8]], base=int(BIG),
                                    channel_multiplier=0,
                                    allow_small_or_imprecise_dtypes=True),
         incs=[("g", 1)])
    g_idx = cnt["g"]
    v_idx_all = 0

    st1p = ExitStack()
    p_t1 = [pt(f"p_t1{b}", [128, 1024], BF16, st1p) for b in range(2)]
    p_tx = [pt(f"p_tx{b}", [128, 512], F32, st1p) for b in range(2)]
    p_g8 = pt("p_g8", [128, 512], F32, st1p)
    p_cb = pt("p_cb", [128, 512], F32, st1p)
    p_pj = [pt(f"p_pj{b}", [128, 512], F32, st1p) for b in range(2)]

    xt_free = [("v", 0), ("v", 0)]
    t1_free = {0: [("v", v_prologue)], 1: [("v", v_prologue)]}
    tx_free = {0: [("v", v_prologue)], 1: [("v", v_prologue)]}
    pg_free = ("s", 0)
    pcb_free = ("v", 0)
    pj_free = [("s", 0), ("s", 0)]
    rsm_free = {p: ("t", 0) for p in "qkv"}
    vstage_free = [("t", 0)]
    pjflip = [0]

    def emit_proj_tg(tg, s_fm_ready):
        for pi, p in enumerate("qkv"):
            slot = pi % 2
            for c in range(NC8):
                emit("tensor", lambda te, p=p, c=c, tg=tg, pb=p_pj[slot]: te.matmul(
                    pb[0:HARM, :], basT[p][:, c * HARM:(c + 1) * HARM],
                    xnh_fm[:, c * T + tg * 512: c * T + tg * 512 + 512],
                    start=(c == 0), stop=(c == NC8 - 1)),
                    waits=([("s", s_fm_ready), pj_free[slot], rsm_free[p]]
                           if c == 0 else []),
                    incs=([("t", 1)] if c == NC8 - 1 else []))
            t_res = cnt["t"]
            emit("scalar", lambda sc_, p=p, pb=p_pj[slot]: sc_.activation(
                res_sm[p][:, :], pb[0:HARM, :], AF.Copy),
                waits=[("t", t_res)], incs=[("s", 1)])
            pj_free[slot] = ("s", cnt["s"])
        s_res_all = cnt["s"]
        for p in "qkv":
            for c8 in range(NC8):
                slot = pjflip[0]
                pjflip[0] ^= 1
                emit("tensor", lambda te, p=p, c8=c8, pb=p_pj[slot]: te.matmul(
                    pb[:, :], wT[p][:, c8 * 128:(c8 + 1) * 128],
                    res_sm[p][:, :], start=True, stop=True),
                    waits=[("s", s_res_all), pj_free[slot]], incs=[("t", 1)])
                t_pr = cnt["t"]
                rsm_free[p] = ("t", t_pr)
                if p == "q" or p == "k":
                    dst = q_fm if p == "q" else k_fm
                    emit("scalar", lambda sc_, dst=dst, c8=c8, tg=tg, pb=p_pj[slot]:
                         sc_.activation(
                             dst[:, c8 * T + tg * 512: c8 * T + tg * 512 + 512],
                             pb[:, :], AF.Copy),
                         waits=[("t", t_pr)], incs=[("s", 1)])
                    pj_free[slot] = ("s", cnt["s"])
                else:
                    vs = 0
                    emit("scalar", lambda sc_, vs=vs, pb=p_pj[slot]: sc_.activation(
                        v_stage[vs][:, :], pb[:, :], AF.Copy),
                        waits=[("t", t_pr), vstage_free[0]], incs=[("s", 1)])
                    pj_free[slot] = ("s", cnt["s"])
                    s_vst = cnt["s"]
                    for tt in range(4):
                        itok = tg * 4 + tt
                        slot2 = tt % 2
                        ps = p_t1[slot2]
                        emit("tensor", lambda te, vs=vs, tt=tt, ps=ps: te.transpose(
                            ps[:, 0:128], v_stage[vs][:, tt * 128:(tt + 1) * 128],
                            ident_bf[:]),
                            waits=[("s", s_vst)] + t1_free[slot2], incs=[("t", 1)])
                        t_vt = cnt["t"]
                        for half in range(2):
                            h = 2 * c8 + half
                            emit("scalar", lambda sc_, itok=itok, h=h, half=half, ps=ps:
                                 sc_.activation(
                                     v_aug[:, itok * NHEAD * 65 + h * 65:
                                           itok * NHEAD * 65 + h * 65 + 64],
                                     ps[:, half * 64: half * 64 + 64], AF.Copy),
                                 waits=[("t", t_vt)], incs=[("s", 1)])
                        t1_free[slot2] = [("s", cnt["s"])]
                    vstage_free[0] = ("t", cnt["t"])
    dx_needed = []
    v_rstd_read = 0
    s_sq_prev = 0
    t_xn_read = 0

    for i in range(2):
        sem = "dxa" if i % 2 == 0 else "dxb"
        emit("sync", lambda sy, i=i: sy.dma_start(
            x_tm[i][:, :], x_h[i * 128:(i + 1) * 128, :]), incs=[(sem, 16)])
        dx_needed.append((sem, cnt[sem]))

    for i in range(NT):
        b = i % 2
        dxw = dx_needed[i]
        emit("vector", lambda v, b=b: v.reduce_sum(m_t[:], x_tm[b][:], axis=AX),
             waits=[dxw], incs=[("v", 1)])
        emit("vector", lambda v: v.tensor_single_scalar(m_t[:], m_t[:], 1.0 / D, ALU.mult),
             incs=[("v", 1)])
        emit("vector", lambda v, b=b: v.tensor_single_scalar(
            xn_tm[:], x_tm[b][:], m_t[:], ALU.subtract),
            waits=[("s", s_sq_prev)], incs=[("v", 1)])
        v_xc = cnt["v"]
        xt_free[b] = ("v", v_xc)
        if i + 2 < NT:
            sem = "dxa" if (i + 2) % 2 == 0 else "dxb"
            emit("sync", lambda sy, i=i, b=b: sy.dma_start(
                x_tm[b][:, :], x_h[(i + 2) * 128:(i + 3) * 128, :]),
                waits=[xt_free[b]], incs=[(sem, 16)])
            dx_needed.append((sem, cnt[sem]))
        emit("scalar", lambda s: s.activation(xnh_tm[:], xn_tm[:], AF.Square,
                                              accum_out=ss_t[:]),
             waits=[("v", v_xc)], incs=[("s", 1)])
        s_sq_prev = cnt["s"]
        emit("scalar", lambda s: s.activation(rstd_t[:], ss_t[:], AF.Sqrt,
                                              scale=1.0 / D, bias=epsln[:]),
             waits=[("v", max(v_const, v_rstd_read))], incs=[("s", 1)])
        s_rs = cnt["s"]
        emit("vector", lambda v: v.reciprocal(rstd_t[:], rstd_t[:]),
             waits=[("s", s_rs)], incs=[("v", 1)])
        emit("vector", lambda v: v.tensor_single_scalar(xn_tm[:], xn_tm[:], rstd_t[:], ALU.mult),
             incs=[("v", 1)])
        v_rstd_read = cnt["v"]
        v_hl = cnt["v"]
        for c in range(NC8):
            slot = c % 2
            ps = p_tx[slot]
            emit("tensor", lambda te, c=c, ps=ps: te.transpose(
                ps[:, 0:128], xn_tm[:, c * 128:(c + 1) * 128], ident_f[:]),
                waits=tx_free[slot] + [("v", v_hl)], incs=[("t", 1)])
            t_now = cnt["t"]
            emit("scalar", lambda sc_, c=c, i=i, ps=ps: sc_.activation(
                xnh_fm[:, c * T + i * 128: c * T + i * 128 + 128], ps[:, 0:128], AF.Copy),
                waits=[("t", t_now)], incs=[("s", 1)])
            s_hi = cnt["s"]
            emit("vector", lambda v, c=c, i=i, ps=ps: v.tensor_tensor(
                xnl_t[:, c * 128:(c + 1) * 128], ps[:, 0:128],
                xnh_fm[:, c * T + i * 128: c * T + i * 128 + 128], ALU.subtract),
                waits=[("t", t_now), ("s", s_hi)], incs=[("v", 1)])
            tx_free[slot] = [("s", cnt["s"]), ("v", cnt["v"])]
        t_xn_read = cnt["t"]
        s_fm_tile = cnt["s"]
        v_fm_tile = cnt["v"]
        nmm = 0
        for (xsel, gsel) in (("h", gwTh), ("h", gwTl), ("l", gwTh)):
            for c in range(NC8):
                first, last = nmm == 0, nmm == 23
                if xsel == "h":
                    xap = xnh_fm[:, c * T + i * 128: c * T + i * 128 + 128]
                else:
                    xap = xnl_t[:, c * 128:(c + 1) * 128]
                emit("tensor", lambda te, xap=xap, gsel=gsel, c=c,
                     first=first, last=last: te.matmul(
                         p_g8[:, 0:8], xap, gsel[:, c * 8:(c + 1) * 8],
                         start=first, stop=last),
                     waits=([("v", v_fm_tile), ("s", s_fm_tile), pg_free] if first else []),
                     incs=([("t", 1)] if last else []))
                nmm += 1
        t_g8 = cnt["t"]
        emit("vector", lambda v, i=i: v.tensor_copy(lg_all[:, i * 8:(i + 1) * 8],
                                                    p_g8[:, 0:8]),
             waits=[("t", t_g8)], incs=[("v", 1)])
        pg_free = ("v", cnt["v"])
        if i % 4 == 3:
            emit_proj_tg(i // 4, s_fm_tile)

    emit("vector", lambda v: v.tensor_reduce(m16[:], v3(lg_all), AX, ALU.max),
         incs=[("v", 1)])
    emit("vector", lambda v: v.tensor_tensor(v3(s_all), v3(lg_all), b3(m16),
                                             ALU.subtract), incs=[("v", 1)])
    v_sub = cnt["v"]
    emit("scalar", lambda s: s.activation(s_all[:], s_all[:], AF.Exp),
         waits=[("v", v_sub)], incs=[("s", 1)])
    s_expall = cnt["s"]
    emit("vector", lambda v: v.tensor_reduce(z16[:], v3(s_all), AX, ALU.add),
         waits=[("s", s_expall)], incs=[("v", 1)])
    emit("vector", lambda v: v.reciprocal(z16[:], z16[:]), incs=[("v", 1)])
    emit("vector", lambda v: v.tensor_tensor(v3(s_all), v3(s_all), b3(z16), ALU.mult),
         incs=[("v", 1)])
    emit("vector", lambda v: v.tensor_reduce(m16[:], v3(s_all), AX, ALU.max),
         incs=[("v", 1)])
    emit("vector", lambda v: v.tensor_tensor(v3(eq_all), v3(s_all), b3(m16),
                                             ALU.is_equal), incs=[("v", 1)])
    emit("vector", lambda v: v.scalar_tensor_tensor(tmp_all[:], eq_all[:], -BIG,
                                                    idxB_all[:], ALU.mult, ALU.add),
         waits=[("g", g_idx)], incs=[("v", 1)])
    emit("vector", lambda v: v.tensor_reduce(a16[:], v3(tmp_all), AX, ALU.min),
         incs=[("v", 1)])
    emit("vector", lambda v: v.tensor_tensor(v3(mk1), v3(tmp_all), b3(a16),
                                             ALU.is_equal), incs=[("v", 1)])
    emit("vector", lambda v: v.tensor_tensor(tmp_all[:], s_all[:], mk1[:], ALU.mult),
         incs=[("v", 1)])
    emit("vector", lambda v: v.tensor_tensor(s_all[:], s_all[:], tmp_all[:],
                                             ALU.subtract), incs=[("v", 1)])
    emit("vector", lambda v: v.tensor_reduce(m2_16[:], v3(s_all), AX, ALU.max),
         incs=[("v", 1)])
    emit("vector", lambda v: v.tensor_tensor(v3(eq_all), v3(s_all), b3(m2_16),
                                             ALU.is_equal), incs=[("v", 1)])
    emit("vector", lambda v: v.scalar_tensor_tensor(tmp_all[:], eq_all[:], -BIG,
                                                    idxB_all[:], ALU.mult, ALU.add),
         incs=[("v", 1)])
    emit("vector", lambda v: v.tensor_reduce(a16[:], v3(tmp_all), AX, ALU.min),
         incs=[("v", 1)])
    emit("vector", lambda v: v.tensor_tensor(v3(mk2), v3(tmp_all), b3(a16),
                                             ALU.is_equal), incs=[("v", 1)])
    emit("vector", lambda v: v.tensor_tensor(rw16[:], m16[:], m2_16[:], ALU.add),
         incs=[("v", 1)])
    emit("vector", lambda v: v.tensor_single_scalar(rw16[:], rw16[:], 1e-20, ALU.add),
         incs=[("v", 1)])
    emit("vector", lambda v: v.reciprocal(rw16[:], rw16[:]), incs=[("v", 1)])
    emit("vector", lambda v: v.tensor_single_scalar(rw16[:], rw16[:], 2.5, ALU.mult),
         incs=[("v", 1)])
    emit("vector", lambda v: v.tensor_tensor(v3(mk1), v3(mk1), b3(m16), ALU.mult),
         incs=[("v", 1)])
    emit("vector", lambda v: v.tensor_tensor(v3(mk2), v3(mk2), b3(m2_16), ALU.mult),
         incs=[("v", 1)])
    emit("vector", lambda v: v.tensor_tensor(comb_all[:], mk1[:], mk2[:], ALU.add),
         incs=[("v", 1)])
    emit("vector", lambda v: v.tensor_tensor(v3(comb_all), v3(comb_all), b3(rw16),
                                             ALU.mult), incs=[("v", 1)])
    v_comb = cnt["v"]
    for i in range(NT):
        emit("tensor", lambda te, i=i: te.transpose(
            p_cb[0:8, 0:128], comb_all[:, i * 8:(i + 1) * 8], ident_f[:]),
            waits=[("v", v_comb), pcb_free], incs=[("t", 1)])
        t_cb = cnt["t"]
        emit("vector", lambda v, i=i: v.tensor_copy(comb_bf[:, i * 128:(i + 1) * 128],
                                                    p_cb[0:8, 0:128]),
             waits=[("t", t_cb)], incs=[("v", 1)])
        pcb_free = ("v", cnt["v"])

    v_p1_done = cnt["v"]
    s_p1_done = cnt["s"]
    st1p.close()
    st1.close()

    st2o = ExitStack()
    o_heads = sb("o_heads", [128, NC8 * T], BF16, st2o)
    recip16 = sb("recip16", [NHEAD, T], BF16, st2o)
    res_bf = sb("res_bf", [HARM, T], BF16, st2o)
    mrow = sb("mrow", [1, 272], F32, st2o)
    mval = sb("mval", [1, 1], BF16, st2o)
    rrow = sb("rrow", [1, 512], F32, st2o)
    rrow_bf = [sb(f"rrow_bf{b}", [1, 512], BF16, st2o) for b in range(2)]

    st2 = ExitStack()
    zrow = [sb(f"zrow{b}", [1, 512], BF16, st2o) for b in range(2)]
    pTs = sb("pTs", [128, 4096], BF16, st2)
    colmax = sb("colmax", [128, 1], F32, st2)

    st2p = ExitStack()
    p_a = pt("p_a", [128, 512], F32, st2p)
    p_b = pt("p_b", [128, 512], F32, st2p)
    p_c = pt("p_c", [128, 512], F32, st2p)

    pa_free = [("v", v_p1_done), ("s", s_p1_done)]
    pbc_free = {0: [("v", v_p1_done), ("s", s_p1_done)],
                1: [("v", v_p1_done), ("s", s_p1_done)]}

    emit("vector", lambda v: v.memset(
        v_aug.ap().rearrange("p (i h c) -> p i h c", i=NT, h=NHEAD)[:, :, :, 64:65], 1.0),
        incs=[("v", 1)])
    v_vaug_ones = cnt["v"]
    stx = ExitStack()
    p_st = pt("p_st", [128, 2048], F32, stx)

    s_qk_done = s_p1_done
    emit("vector", lambda v: v.tensor_reduce(colmax[:], q_fm[:], AX, ALU.max,
                                             apply_absolute_value=True),
         waits=[("s", s_qk_done)], incs=[("v", 1)])
    v_cm = cnt["v"]
    emit("tensor", lambda te: te.transpose(p_a[0:1, 0:128], colmax[:], ident_f[:]),
         waits=[("v", v_cm), pa_free[0]], incs=[("t", 1)])
    t_cm = cnt["t"]
    emit("vector", lambda v: v.tensor_copy(mrow[:, 0:128], p_a[0:1, 0:128]),
         waits=[("t", t_cm)], incs=[("v", 1)])
    emit("vector", lambda v: v.reduce_max(mrow[:, 256:257], mrow[:, 0:128], axis=AX),
         incs=[("v", 1)])
    emit("vector", lambda v: v.tensor_reduce(colmax[:], k_fm[:], AX, ALU.max,
                                             apply_absolute_value=True), incs=[("v", 1)])
    v_cm2 = cnt["v"]
    emit("tensor", lambda te: te.transpose(p_a[0:1, 0:128], colmax[:], ident_f[:]),
         waits=[("v", v_cm2)], incs=[("t", 1)])
    t_cm2 = cnt["t"]
    emit("vector", lambda v: v.tensor_copy(mrow[:, 128:256], p_a[0:1, 0:128]),
         waits=[("t", t_cm2)], incs=[("v", 1)])
    pa_free[0] = ("v", cnt["v"])
    emit("vector", lambda v: v.reduce_max(mrow[:, 257:258], mrow[:, 128:256], axis=AX),
         incs=[("v", 1)])
    emit("vector", lambda v: v.tensor_tensor(mrow[:, 258:259], mrow[:, 256:257],
                                             mrow[:, 257:258], ALU.mult), incs=[("v", 1)])
    emit("vector", lambda v: v.tensor_copy(mval[:], mrow[:, 258:259]), incs=[("v", 1)])
    v_mval = cnt["v"]
    emit("tensor", lambda te: te.matmul(p_a[:, 0:1], onesrow[:], mval[:],
                                        start=True, stop=True),
         waits=[("v", v_mval)], incs=[("t", 1)])
    t_bb = cnt["t"]
    emit("vector", lambda v: v.tensor_single_scalar(bias128[:], p_a[:, 0:1], -8.0, ALU.mult),
         waits=[("t", t_bb)], incs=[("v", 1)])
    pa_free[0] = ("v", cnt["v"])
    v_bias = cnt["v"]

    rrow_free = [("dr0", 0), ("dr1", 0)]
    st_free = [("s", s_p1_done), ("v", v_p1_done)]
    av_free = {0: list(pa_free), 1: list(pbc_free[0])}
    pTs_free = {0: ("t", 0), 1: ("t", 0)}
    pairs = [(2 * hp, 2 * hp + 1, qb) for hp in range(NHEAD // 2) for qb in range(NTG)]
    for mi, (hA, hB, qb) in enumerate(pairs):
        exp_done = [0] * 8
        for pair in range(10):
            if pair >= 2:
                p2 = pair - 2
                pp2 = p2 % 2
                for sidx, h in ((0, hA), (1, hB)):
                    avbank = p_a if sidx == 0 else p_b
                    for j in (0, 1):
                        kc = 2 * p2 + j
                        emit("tensor", lambda te, h=h, kc=kc, sidx=sidx, j=j, pp2=pp2,
                             pb=avbank,
                             first=(kc == 0), last=(kc == 15): te.matmul(
                                 pb[0:65, :],
                                 v_aug[:, kc * NHEAD * 65 + h * 65: kc * NHEAD * 65 + h * 65 + 65],
                                 pTs[:, pp2 * 2048 + sidx * 1024 + j * 512:
                                     pp2 * 2048 + sidx * 1024 + j * 512 + 512],
                                 start=first, stop=last),
                            waits=([("s", exp_done[p2])] if (sidx == 0 and j == 0) else []) +
                                  (av_free[sidx] + ([("v", v_vaug_ones)] if mi == 0 else [])
                                   if kc == 0 else []),
                            incs=[("t", 1)])
                pTs_free[pp2] = ("t", cnt["t"])
            if pair < 8:
                pp = pair % 2
                first = True
                for j in (0, 1):
                    for sidx, h in ((0, hA), (1, hB)):
                        c8, off = h // 2, (h % 2) * 64
                        kc = 2 * pair + j
                        emit("tensor", lambda te, c8=c8, off=off, kc=kc, qb=qb,
                             sidx=sidx, j=j: te.matmul(
                                 p_st[:, sidx * 1024 + j * 512: sidx * 1024 + j * 512 + 512],
                                 k_fm[off:off + 64, c8 * T + kc * 128: c8 * T + kc * 128 + 128],
                                 q_fm[off:off + 64, c8 * T + qb * 512: c8 * T + qb * 512 + 512],
                                 start=True, stop=True),
                            waits=(st_free if first else []) +
                                  ([("v", v_bias)] if (mi == 0 and pair == 0 and first)
                                   else []),
                            incs=[("t", 1)])
                        first = False
                t_sp = cnt["t"]
                emit("scalar", lambda s, pp=pp: s.activation(
                    pTs[:, pp * 2048: pp * 2048 + 2048],
                    p_st[:, 0:2048], AF.Exp,
                    scale=0.125, bias=bias128[:]),
                    waits=[("t", t_sp), pTs_free[pp]], incs=[("s", 1)])
                st_free = [("s", cnt["s"])]
                exp_done[pair] = cnt["s"]
        t_av = cnt["t"]
        for sidx, h in ((0, hA), (1, hB)):
            c8, off = h // 2, (h % 2) * 64
            avbank = p_a if sidx == 0 else p_b
            emit("vector", lambda v, pb=avbank, slot=sidx: v.tensor_copy(
                zrow[slot][:, :], pb[64:65, :]),
                waits=[("t", t_av)], incs=[("v", 1)])
            emit("vector", lambda v, qb=qb, c8=c8, off=off, pb=avbank: v.tensor_copy(
                o_heads[off:off + 64, c8 * T + qb * 512: c8 * T + qb * 512 + 512],
                pb[0:64, :]),
                incs=[("v", 1)])
            av_free[sidx] = [("v", cnt["v"])]
        for sidx, h in ((0, hA), (1, hB)):
            qb_ = qb
            emit("vector", lambda v, slot=sidx: v.reciprocal(rrow[:], zrow[slot][:, :]),
                 incs=[("v", 1)])
            emit("vector", lambda v, slot=sidx: v.tensor_copy(
                rrow_bf[slot][:, :], rrow[:]),
                waits=[rrow_free[sidx]], incs=[("v", 1)])
            v_rr = cnt["v"]
            drsem = "dr0" if sidx == 0 else "dr1"
            emit("sync", lambda sy, h=h, qb=qb_, slot=sidx: sy.dma_start(
                recip16[h:h + 1, qb * 512:(qb + 1) * 512], rrow_bf[slot][:, :]),
                waits=[("v", v_rr)], incs=[(drsem, 16)])
            rrow_free[sidx] = (drsem, cnt[drsem])
    v_recip_all = cnt["v"]
    dr0_all, dr1_all = cnt["dr0"], cnt["dr1"]

    pbc2_free = list(av_free[1]) + list(pbc_free[1])
    for c8 in range(NC8):
        for tg in range(NTG):
            emit("tensor", lambda te, c8=c8, tg=tg: te.matmul(
                p_b[:, :], selc[:, c8 * 128:(c8 + 1) * 128],
                recip16[:, tg * 512:(tg + 1) * 512], start=True, stop=True),
                waits=[("v", v_recip_all), ("dr0", dr0_all), ("dr1", dr1_all)]
                      + list(pbc2_free), incs=[("t", 1)])
            t_bc = cnt["t"]
            emit("vector", lambda v, c8=c8, tg=tg: v.tensor_tensor(
                o_heads[:, c8 * T + tg * 512: c8 * T + tg * 512 + 512],
                o_heads[:, c8 * T + tg * 512: c8 * T + tg * 512 + 512],
                p_b[:, :], ALU.mult),
                waits=[("t", t_bc)], incs=[("v", 1)])
            pbc2_free = [("v", cnt["v"])]
    v_onorm = cnt["v"]

    st2.close()
    st_qkv.close()
    st_acc = ExitStack()
    acc = sb("acc", [128, NC8 * T], F32, st_acc, side="right")

    pa_free2 = [av_free[0][0]]
    pbc_free2 = {0: ("v", v_onorm), 1: ("v", v_onorm)}
    for tg in range(NTG):
        for c in range(NC8):
            emit("tensor", lambda te, c=c, tg=tg: te.matmul(
                p_a[0:HARM, :], basT["o"][:, c * HARM:(c + 1) * HARM],
                o_heads[:, c * T + tg * 512: c * T + tg * 512 + 512],
                start=(c == 0), stop=(c == NC8 - 1)),
                waits=([("v", v_onorm), pa_free2[0], av_free[0][-1]] if c == 0 else []),
                incs=([("t", 1)] if c == NC8 - 1 else []))
        t_ro = cnt["t"]
        emit("scalar", lambda s, tg=tg: s.activation(
            res_bf[:, tg * 512: tg * 512 + 512], p_a[0:HARM, :], AF.Copy),
            waits=[("t", t_ro)], incs=[("s", 1)])
        pa_free2[0] = ("s", cnt["s"])
    s_reso = cnt["s"]
    for c8 in range(NC8):
        for tg in range(NTG):
            slot = (c8 * NTG + tg) % 2
            pbank = p_b if slot == 0 else p_c
            emit("tensor", lambda te, c8=c8, tg=tg, pb=pbank: te.matmul(
                pb[:, :], wT["o"][:, c8 * 128:(c8 + 1) * 128],
                res_bf[:, tg * 512: tg * 512 + 512], start=True, stop=True),
                waits=[("s", s_reso), pbc_free2[slot]], incs=[("t", 1)])
            t_ao = cnt["t"]
            emit("vector", lambda v, c8=c8, tg=tg, pb=pbank: v.tensor_copy(
                acc[:, c8 * T + tg * 512: c8 * T + tg * 512 + 512], pb[:, :]),
                waits=[("t", t_ao)], incs=[("v", 1)])
            pbc_free2[slot] = ("v", cnt["v"])
    v_acc0 = cnt["v"]

    st2o.close()
    stx.close()
    st2p.close()
    stw.close()

    st3 = ExitStack()
    wgbuf = [sb(f"wgbuf{b}", [128, GSZ * 1024], BF16, st3) for b in range(2)]
    wubuf = [sb(f"wubuf{b}", [128, GSZ * 1024], BF16, st3) for b in range(2)]
    wdbuf = [sb(f"wdbuf{b}", [128, GSZ * 1024], BF16, st3) for b in range(2)]
    gbuf = [sb(f"gbuf{b}", [128, GSZ * T], BF16, st3) for b in range(2)]
    c_cur = sb("c_cur", [128, T], F32, st3)
    sgt = [sb(f"sgt{b}", [128, 512], BF16, st3) for b in range(2)]

    st3p = ExitStack()
    p_xg = pt("p_xg", [128, 512], F32, st3p)
    p_xu = pt("p_xu", [128, 512], F32, st3p)
    p_d0 = pt("p_d0", [128, 512], F32, st3p)
    p_d1 = pt("p_d1", [128, 512], F32, st3p)
    p_cc = pt("p_cc", [128, 512], F32, st3p)

    groups = [list(range(g, min(g + GSZ, NMOE))) for g in range(0, NMOE, GSZ)]
    groups += [list(range(g, min(g + GSZ, NHID))) for g in range(NMOE, NHID, GSZ)]

    def chunk_srcs(h):
        if h < NMOE:
            e, fi = h // NFM, h % NFM
            return (w_gate_h[e, :, fi * 128:(fi + 1) * 128],
                    w_up_h[e, :, fi * 128:(fi + 1) * 128],
                    w_down_h[e, fi * 128:(fi + 1) * 128, :])
        fi = h - NMOE
        return (sw_gate_h[:, fi * 128:(fi + 1) * 128],
                sw_up_h[:, fi * 128:(fi + 1) * 128],
                sw_down_h[fi * 128:(fi + 1) * 128, :])

    def emit_group_dma(gi, buf_free_wait):
        buf = gi % 2
        sem = "dwa" if buf == 0 else "dwb"
        first = True
        for loc, h in enumerate(groups[gi]):
            sg, su, sd = chunk_srcs(h)
            for (dst, src) in ((wgbuf[buf], sg), (wubuf[buf], su)):
                emit("gpsimd", lambda g, dst=dst, loc=loc, src=src: g.dma_start(
                    dst.ap()[:, loc * 1024:(loc + 1) * 1024].rearrange(
                        "p (c f) -> p c f", c=NC8),
                    src.rearrange("(c p) f -> p c f", p=128)),
                    waits=([buf_free_wait] if first else []),
                    incs=[(sem, 16)])
                first = False
            emit("gpsimd", lambda g, buf=buf, loc=loc, sd=sd: g.dma_start(
                wdbuf[buf][:, loc * 1024:(loc + 1) * 1024], sd),
                incs=[(sem, 16)])
        return (sem, cnt[sem])

    dma_done = [emit_group_dma(0, ("v", v_acc0)), emit_group_dma(1, ("v", v_acc0))]
    grp_last_t = [0] * len(groups)

    pxg_free = ("v", v_acc0)
    pxu_free = ("v", v_acc0)
    pd_free = {0: ("v", v_acc0), 1: ("v", v_acc0)}
    pcc_free = ("v", v_acc0)
    sgt_free = [("v", 0), ("v", 0)]
    gbuf_free = [("t", 0), ("t", 0)]
    cur_e = -1
    su_slot = 0
    add_done = {}

    for gi, grp in enumerate(groups):
        buf = gi % 2
        dww = dma_done[gi]
        for loc, h in enumerate(grp):
            is_moe = h < NMOE
            if is_moe and (h // NFM) != cur_e:
                cur_e = h // NFM
                for tg in range(NTG):
                    emit("tensor", lambda te, e=cur_e, tg=tg: te.matmul(
                        p_cc[:, :], sele[:, e * 128:(e + 1) * 128],
                        comb_bf[0:E, tg * 512:(tg + 1) * 512],
                        start=True, stop=True),
                        waits=[("v", v_p1_done), pcc_free], incs=[("t", 1)])
                    t_cc = cnt["t"]
                    emit("vector", lambda v, tg=tg: v.tensor_copy(
                        c_cur[:, tg * 512:(tg + 1) * 512], p_cc[:, :]),
                        waits=[("t", t_cc)], incs=[("v", 1)])
                    pcc_free = ("v", cnt["v"])
            v_ccur = cnt["v"]
            for tg in range(NTG):
                for c in range(NC8):
                    emit("tensor", lambda te, buf=buf, loc=loc, c=c, tg=tg: te.matmul(
                        p_xg[:, :],
                        wgbuf[buf][:, loc * 1024 + c * 128: loc * 1024 + c * 128 + 128],
                        xnh_fm[:, c * T + tg * 512: c * T + tg * 512 + 512],
                        start=(c == 0), stop=(c == NC8 - 1)),
                        waits=([dww, pxg_free] if c == 0 else []),
                        incs=([("t", 1)] if c == NC8 - 1 else []))
                t_xg = cnt["t"]
                for c in range(NC8):
                    emit("tensor", lambda te, buf=buf, loc=loc, c=c, tg=tg: te.matmul(
                        p_xu[:, :],
                        wubuf[buf][:, loc * 1024 + c * 128: loc * 1024 + c * 128 + 128],
                        xnh_fm[:, c * T + tg * 512: c * T + tg * 512 + 512],
                        start=(c == 0), stop=(c == NC8 - 1)),
                        waits=([pxu_free] if c == 0 else []),
                        incs=([("t", 1)] if c == NC8 - 1 else []))
                t_xu = cnt["t"]
                ss = su_slot % 2
                su_slot += 1
                emit("scalar", lambda s, ss=ss: s.activation(sgt[ss][:], p_xg[:, :], AF.Silu),
                     waits=[("t", t_xg), sgt_free[ss]], incs=[("s", 1)])
                pxg_free = ("s", cnt["s"])
                s_silu = cnt["s"]
                gslice = gbuf[buf][:, loc * T + tg * 512: loc * T + tg * 512 + 512]
                emit("vector", lambda v, gs=gslice, ss=ss: v.tensor_tensor(
                    gs, sgt[ss][:], p_xu[:, :], ALU.mult),
                    waits=[("t", t_xu), ("s", s_silu)] +
                          ([gbuf_free[buf]] if (loc == 0 and tg == 0) else []),
                    incs=[("v", 1)])
                pxu_free = ("v", cnt["v"])
                sgt_free[ss] = ("v", cnt["v"])
                if is_moe:
                    emit("vector", lambda v, gs=gslice, tg=tg: v.tensor_tensor(
                        gs, gs, c_cur[:, tg * 512:(tg + 1) * 512], ALU.mult),
                        waits=[("v", v_ccur)], incs=[("v", 1)])
        v_g_grp = cnt["v"]
        for tg in range(NTG):
            for dc in range(NC8):
                slot = (tg * NC8 + dc) % 2
                pbank = p_d0 if slot == 0 else p_d1
                for li in range(len(grp)):
                    emit("tensor", lambda te, buf=buf, li=li, dc=dc, tg=tg, pb=pbank,
                         first=(li == 0), last=(li == len(grp) - 1): te.matmul(
                             pb[:, :],
                             wdbuf[buf][:, li * 1024 + dc * 128: li * 1024 + dc * 128 + 128],
                             gbuf[buf][:, li * T + tg * 512: li * T + tg * 512 + 512],
                             start=first, stop=last),
                        waits=([("v", v_g_grp), pd_free[slot]] if li == 0 else []),
                        incs=([("t", 1)] if li == len(grp) - 1 else []))
                t_dn = cnt["t"]
                emit("vector", lambda v, dc=dc, tg=tg, pb=pbank: v.tensor_tensor(
                    acc[:, dc * T + tg * 512: dc * T + tg * 512 + 512],
                    acc[:, dc * T + tg * 512: dc * T + tg * 512 + 512],
                    pb[:, :], ALU.add),
                    waits=[("t", t_dn)], incs=[("v", 1)])
                if gi == len(groups) - 1:
                    add_done[(tg, dc)] = cnt["v"]
                pd_free[slot] = ("v", cnt["v"])
        grp_last_t[gi] = cnt["t"]
        gbuf_free[buf] = ("t", cnt["t"])
        if gi + 2 < len(groups):
            dma_done.append(emit_group_dma(gi + 2, ("t", grp_last_t[gi])))
    v_moe_done = cnt["v"]

    st3p.close()
    st3.close()

    st4 = ExitStack()
    x4 = [sb(f"x4_{b}", [128, D], F32, st4, side="right") for b in range(2)]
    o4 = [sb(f"o4_{b}", [128, D], F32, st4, side="right") for b in range(2)]
    st4p = ExitStack()
    p_o = [pt(f"p_o{b}", [128, 512], F32, st4p) for b in range(2)]

    x4_free = [("v", 0)] * 2
    o4_free = [("doa", 0), ("dob", 0)]
    po_free = {0: ("v", v_moe_done), 1: ("v", v_moe_done)}
    dx4 = []
    X4SEMS = ("dxa", "dxb")
    for i in range(2):
        sem = X4SEMS[i % 2]
        emit("sync", lambda sy, i=i: sy.dma_start(x4[i][:, :], x_h[i * 128:(i + 1) * 128, :]),
             waits=[(sem, cnt[sem])], incs=[(sem, 16)])
        dx4.append((sem, cnt[sem]))
    for i in range(NT):
        b = i % 2
        ob = i % 2
        for c8 in range(NC8):
            slot = (i * NC8 + c8) % 2
            emit("tensor", lambda te, c8=c8, i=i, slot=slot: te.transpose(
                p_o[slot][:, 0:128], acc[:, c8 * T + i * 128: c8 * T + i * 128 + 128],
                ident_f[:]),
                waits=[("v", add_done[(i // 4, c8)]), po_free[slot]], incs=[("t", 1)])
            t_tr = cnt["t"]
            emit("vector", lambda v, c8=c8, b=b, ob=ob, slot=slot: v.tensor_tensor(
                o4[ob][:, c8 * 128:(c8 + 1) * 128], x4[b][:, c8 * 128:(c8 + 1) * 128],
                p_o[slot][:, 0:128], ALU.add),
                waits=[("t", t_tr), dx4[i], o4_free[ob]], incs=[("v", 1)])
            po_free[slot] = ("v", cnt["v"])
        v_out = cnt["v"]
        x4_free[b] = ("v", v_out)
        if i + 2 < NT:
            sem = X4SEMS[(i + 2) % 2]
            emit("sync", lambda sy, i=i, b=b: sy.dma_start(
                x4[b][:, :], x_h[(i + 2) * 128:(i + 3) * 128, :]),
                waits=[x4_free[b], (sem, cnt[sem])], incs=[(sem, 16)])
            dx4.append((sem, cnt[sem]))
        dosem = "doa" if ob == 0 else "dob"
        emit("sync", lambda sy, i=i, ob=ob: sy.dma_start(
            out_h[i * 128:(i + 1) * 128, :], o4[ob][:, :]),
            waits=[("v", v_out)], incs=[(dosem, 16)])
        o4_free[ob] = (dosem, cnt[dosem])
    doa_total, dob_total = cnt["doa"], cnt["dob"]

    with ExitStack() as st_sem:
        sems = {nm: st_sem.enter_context(nc.semaphore(nm)) for nm in cnt}
        with nc.Block() as block:
            def runner(engname):
                ss = {"vector": "v", "scalar": "s"}.get(engname)
                def run(e):
                    for k, (fn, waits, incs) in enumerate(ops[engname]):
                        if ss is not None and k > 0:
                            e.wait_ge(sems[ss], k)
                        for sname, val in waits:
                            e.wait_ge(sems[sname], val)
                        inst = fn(e)
                        for sname, amt in incs:
                            inst.then_inc(sems[sname], amt)
                    if engname == "sync":
                        e.wait_ge(sems["doa"], doa_total)
                        e.wait_ge(sems["dob"], dob_total)
                return run
            block.tensor(runner("tensor"))
            block.vector(runner("vector"))
            block.scalar(runner("scalar"))
            block.gpsimd(runner("gpsimd"))
            block.sync(runner("sync"))

    st4p.close()
    st4.close()
    st_acc.close()
    so.close()
    return nc


_NC_CACHE = None


def kernel(**inputs):
    global _NC_CACHE
    if _NC_CACHE is None:
        _NC_CACHE = build_nc()
    nc = _NC_CACHE
    x = np.ascontiguousarray(np.asarray(inputs["L0_attn_x"], dtype=np.float32))
    common = {
        "gate_w": np.ascontiguousarray(inputs["L0_ffn_gate_w"], dtype=np.float32),
        "w_gate": np.ascontiguousarray(inputs["L0_ffn_w_gate"], dtype=np.float32),
        "w_up": np.ascontiguousarray(inputs["L0_ffn_w_up"], dtype=np.float32),
        "w_down": np.ascontiguousarray(inputs["L0_ffn_w_down"], dtype=np.float32),
        "sw_gate": np.ascontiguousarray(inputs["L0_ffn_sw_gate"], dtype=np.float32),
        "sw_up": np.ascontiguousarray(inputs["L0_ffn_sw_up"], dtype=np.float32),
        "sw_down": np.ascontiguousarray(inputs["L0_ffn_sw_down"], dtype=np.float32),
    }
    for p in "qkvo":
        common[f"basis_{p}"] = np.ascontiguousarray(inputs[f"L0_attn_basis_{p}"], dtype=np.float32)
        common[f"phase_{p}"] = np.ascontiguousarray(inputs[f"L0_attn_phase_{p}"], dtype=np.float32)
        common[f"amp_{p}"] = np.ascontiguousarray(inputs[f"L0_attn_amp_{p}"], dtype=np.float32)
    in_maps = [dict(common, x=np.ascontiguousarray(x[i])) for i in range(8)]
    res = run_bass_kernel_spmd(nc, in_maps, core_ids=list(range(8)))
    return np.stack([r["out"] for r in res.results], axis=0)


if __name__ == "__main__":
    build_nc()
    print("built ok")


# version 12
# speedup vs baseline: 1.0229x; 1.0004x over previous
import numpy as np
from contextlib import ExitStack

import concourse.bass as bass
import concourse.mybir as mybir
from concourse.bass_utils import run_bass_kernel_spmd

F32 = mybir.dt.float32
BF16 = mybir.dt.bfloat16
I32 = mybir.dt.int32
AF = mybir.ActivationFunctionType
ALU = mybir.AluOpType
AX = mybir.AxisListType.X

T = 2048
D = 1024
NC8 = 8
NT = 16
NTG = 4
NHEAD = 16
HARM = 32
E = 8
NFM = 11
NMOE = E * NFM
NFS = 22
NHID = NMOE + NFS
GSZ = 4
BIG = 100.0
EPS_LN = 1e-5
TWO_PI = 6.283185307179586
HALF_PI = 1.5707963267948966
INV_2PI = 1.0 / TWO_PI
PI_LO = 3.1415925

ENGS = ("tensor", "vector", "scalar", "gpsimd", "sync")


def build_nc():
    nc = bass.Bass("TRN2", target_bir_lowering=False)

    x_h = nc.dram_tensor("x", [T, D], F32, kind="ExternalInput")
    basis = {p: nc.dram_tensor(f"basis_{p}", [HARM, D], F32, kind="ExternalInput")
             for p in "qkvo"}
    phase = {p: nc.dram_tensor(f"phase_{p}", [D, HARM], F32, kind="ExternalInput")
             for p in "qkvo"}
    amp = {p: nc.dram_tensor(f"amp_{p}", [D, HARM], F32, kind="ExternalInput")
           for p in "qkvo"}
    gate_w_h = nc.dram_tensor("gate_w", [E, D], F32, kind="ExternalInput")
    w_gate_h = nc.dram_tensor("w_gate", [E, D, 1408], F32, kind="ExternalInput")
    w_up_h = nc.dram_tensor("w_up", [E, D, 1408], F32, kind="ExternalInput")
    w_down_h = nc.dram_tensor("w_down", [E, 1408, D], F32, kind="ExternalInput")
    sw_gate_h = nc.dram_tensor("sw_gate", [D, 2816], F32, kind="ExternalInput")
    sw_up_h = nc.dram_tensor("sw_up", [D, 2816], F32, kind="ExternalInput")
    sw_down_h = nc.dram_tensor("sw_down", [2816, D], F32, kind="ExternalInput")
    out_h = nc.dram_tensor("out", [T, D], F32, kind="ExternalOutput")

    ops = {e: [] for e in ENGS}
    cnt = {s: 0 for s in ("t", "v", "s", "g", "dp", "dwa", "dwb",
                          "dxa", "dxb", "doa", "dob", "dr0", "dr1")}

    SELF_SEM = {"vector": "v", "scalar": "s"}

    def emit(eng, fn, waits=(), incs=()):
        for sname, val in waits:
            assert val <= cnt[sname], f"wait-ahead: {sname} {val} > {cnt[sname]}"
        incs = list(incs)
        ss = SELF_SEM.get(eng)
        if ss is not None and not any(n == ss for n, _ in incs):
            incs.append((ss, 1))
        rec = []
        for sname, amt in incs:
            cnt[sname] += amt
            rec.append((sname, amt))
        ops[eng].append((fn, tuple(waits), tuple(rec)))

    def sb(name, shape, dtype, stack, side=None):
        if side is None:
            return stack.enter_context(nc.sbuf_tensor(name, shape, dtype))
        return stack.enter_context(nc.sbuf_tensor(name, shape, dtype, side=side))

    def pt(name, shape, dtype, stack):
        return stack.enter_context(nc.psum_tensor(name, shape, dtype))

    so = ExitStack()
    ident_bf = sb("ident_bf", [128, 128], BF16, so)
    ident_f = sb("ident_f", [128, 128], F32, so)
    st_iota = ExitStack()
    iota2d = sb("iota2d", [128, 128], F32, st_iota, side="right")
    onesrow = sb("onesrow", [1, 128], BF16, so)
    epsln = sb("epsln", [128, 1], F32, so)
    selc = sb("selc", [16, NC8 * 128], BF16, so)
    sele = sb("sele", [E, E * 128], BF16, so)
    xnh_fm = sb("xnh_fm", [128, NC8 * T], BF16, so)
    comb_bf = sb("comb_bf", [E, T], BF16, so)
    bias128 = sb("bias128", [128, 1], F32, so)
    stw = ExitStack()
    basT = {p: sb(f"basT_{p}", [128, NC8 * HARM], BF16, stw) for p in "qkvo"}
    wT = {p: sb(f"wT_{p}", [HARM, D], BF16, stw) for p in "qkvo"}
    gwTh = sb("gwTh", [128, NC8 * 8], BF16, stw)
    gwTl = sb("gwTl", [128, NC8 * 8], BF16, stw)

    emit("gpsimd", lambda g: g.iota(iota2d[:, :], [[1, 128]], channel_multiplier=-1,
                                    allow_small_or_imprecise_dtypes=True),
         incs=[("g", 1)])
    g_iota = cnt["g"]
    emit("vector", lambda v: v.tensor_single_scalar(ident_bf[:], iota2d[:], 0.0, ALU.is_equal),
         waits=[("g", g_iota)], incs=[("v", 1)])
    emit("vector", lambda v: v.tensor_single_scalar(ident_f[:], iota2d[:], 0.0, ALU.is_equal),
         incs=[("v", 1)])
    emit("vector", lambda v: v.memset(onesrow[:], 1.0), incs=[("v", 1)])
    emit("vector", lambda v: v.memset(epsln[:], EPS_LN), incs=[("v", 1)])
    v_const = cnt["v"]
    st_iota.close()

    stp = ExitStack()
    bas_bf = {p: sb(f"bas_bf_{p}", [HARM, D], BF16, stp, side="right") for p in "qkvo"}
    ph_f = {p: sb(f"ph_f_{p}", [128, NC8 * HARM], F32, stp, side="right") for p in "qkvo"}
    am_f = {p: sb(f"am_f_{p}", [128, NC8 * HARM], F32, stp, side="right") for p in "qkvo"}
    gw_f = sb("gw_f", [E, D], F32, stp, side="right")
    wfull = {p: sb(f"wfull_{p}", [128, NC8 * HARM], BF16, stp, side="right") for p in "qkvo"}
    cw_t = sb("cw_t", [128, NC8 * HARM], F32, stp, side="right")
    cw_k = sb("cw_k", [128, NC8 * HARM], I32, stp, side="right")
    cw_f = sb("cw_f", [128, NC8 * HARM], F32, stp, side="right")
    gwh = sb("gwh", [E, D], BF16, stp, side="right")
    gwl = sb("gwl", [E, D], BF16, stp, side="right")
    gwt = sb("gwt", [E, D], F32, stp, side="right")
    scr_sel = sb("scr_sel", [16, 1024], F32, stp, side="right")

    emit("gpsimd", lambda g: g.iota(
        scr_sel.ap().rearrange("p (g j) -> p g j", g=16),
        [[1, 16], [0, 64]], channel_multiplier=-1,
        allow_small_or_imprecise_dtypes=True), incs=[("g", 1)])
    g_sel = cnt["g"]
    emit("vector", lambda v: v.tensor_single_scalar(selc[:, :], scr_sel[:, :],
                                                    0.0, ALU.is_equal),
         waits=[("g", g_sel)], incs=[("v", 1)])
    v_sel1 = cnt["v"]
    emit("gpsimd", lambda g: g.iota(
        scr_sel[0:8, :].rearrange("p (f j) -> p f j", f=8),
        [[1, 8], [0, 128]], channel_multiplier=-1,
        allow_small_or_imprecise_dtypes=True),
        waits=[("v", v_sel1)], incs=[("g", 1)])
    g_sel2 = cnt["g"]
    emit("vector", lambda v: v.tensor_single_scalar(sele[:, :], scr_sel[0:8, :],
                                                    0.0, ALU.is_equal),
         waits=[("g", g_sel2)], incs=[("v", 1)])

    for p in "qkvo":
        emit("gpsimd", lambda g, p=p: g.dma_start(bas_bf[p][:], basis[p][:, :]),
             incs=[("dp", 16)])
        emit("gpsimd", lambda g, p=p: g.dma_start(
            ph_f[p].ap().rearrange("p (c h) -> p c h", c=NC8),
            phase[p].ap().rearrange("(c p) h -> p c h", p=128)),
            incs=[("dp", 16)])
        emit("gpsimd", lambda g, p=p: g.dma_start(
            am_f[p].ap().rearrange("p (c h) -> p c h", c=NC8),
            amp[p].ap().rearrange("(c p) h -> p c h", p=128)),
            incs=[("dp", 16)])
    emit("gpsimd", lambda g: g.dma_start(gw_f[:], gate_w_h[:, :]), incs=[("dp", 16)])
    dp_params = cnt["dp"]

    for p in "qkvo":
        P = ph_f[p]
        emit("vector", lambda v, P=P: v.tensor_scalar(
            cw_t[:], P[:], HALF_PI, INV_2PI, ALU.add, ALU.mult),
            waits=[("dp", dp_params)], incs=[("v", 1)])
        emit("vector", lambda v: v.tensor_copy(cw_k[:], cw_t[:]), incs=[("v", 1)])
        emit("vector", lambda v: v.tensor_copy(cw_f[:], cw_k[:]), incs=[("v", 1)])
        emit("vector", lambda v, P=P: v.tensor_single_scalar(cw_t[:], P[:], HALF_PI, ALU.add),
             incs=[("v", 1)])
        emit("vector", lambda v: v.scalar_tensor_tensor(
            cw_t[:], cw_f[:], -TWO_PI, cw_t[:], ALU.mult, ALU.add), incs=[("v", 1)])
        emit("vector", lambda v: v.tensor_single_scalar(cw_f[:], cw_t[:], PI_LO, ALU.is_gt),
             incs=[("v", 1)])
        emit("vector", lambda v: v.scalar_tensor_tensor(
            cw_t[:], cw_f[:], -TWO_PI, cw_t[:], ALU.mult, ALU.add), incs=[("v", 1)])
        emit("vector", lambda v: v.tensor_single_scalar(cw_f[:], cw_t[:], -PI_LO, ALU.is_lt),
             incs=[("v", 1)])
        emit("vector", lambda v: v.scalar_tensor_tensor(
            cw_t[:], cw_f[:], TWO_PI, cw_t[:], ALU.mult, ALU.add), incs=[("v", 1)])
        emit("vector", lambda v: v.tensor_single_scalar(cw_t[:], cw_t[:], PI_LO, ALU.min),
             incs=[("v", 1)])
        emit("vector", lambda v: v.tensor_single_scalar(cw_t[:], cw_t[:], -PI_LO, ALU.max),
             incs=[("v", 1)])
        v_red = cnt["v"]
        emit("scalar", lambda s: s.activation(cw_f[:], cw_t[:], AF.Sin, scale=-1.0),
             waits=[("v", v_red)], incs=[("s", 1)])
        s_sin = cnt["s"]
        emit("vector", lambda v, p=p: v.tensor_tensor(wfull[p][:], cw_f[:], am_f[p][:], ALU.mult),
             waits=[("s", s_sin)], incs=[("v", 1)])
    emit("vector", lambda v: v.tensor_copy(gwh[:], gw_f[:]),
         waits=[("dp", dp_params)], incs=[("v", 1)])
    emit("vector", lambda v: v.tensor_tensor(gwt[:], gw_f[:], gwh[:], ALU.subtract),
         incs=[("v", 1)])
    emit("vector", lambda v: v.tensor_copy(gwl[:], gwt[:]), incs=[("v", 1)])
    v_params = cnt["v"]

    stpp = ExitStack()
    p_tr = [pt(f"p_tr{b}", [128, 1024], BF16, stpp) for b in range(2)]
    tr_free = {0: ("v", 0), 1: ("v", 0)}

    def bf_transpose(src_ap, dst_ap, idn):
        slot = bf_transpose.flip
        bf_transpose.flip ^= 1
        ps = p_tr[slot]
        emit("tensor", lambda te, s=src_ap, i=idn, ps=ps: te.transpose(
            ps[0:s.shape[1], 0:s.shape[0]], s, i),
            waits=[tr_free[slot], ("v", v_params)], incs=[("t", 1)])
        t_now = cnt["t"]
        emit("vector", lambda v, d=dst_ap, ps=ps, s=src_ap: v.tensor_copy(
            d, ps[0:s.shape[1], 0:s.shape[0]]),
            waits=[("t", t_now)], incs=[("v", 1)])
        tr_free[slot] = ("v", cnt["v"])
    bf_transpose.flip = 0

    for p in "qkvo":
        for c in range(NC8):
            bf_transpose(bas_bf[p][:, c * 128:(c + 1) * 128],
                         basT[p][:, c * HARM:(c + 1) * HARM],
                         ident_bf[0:HARM, 0:HARM])
        for c in range(NC8):
            bf_transpose(wfull[p][:, c * HARM:(c + 1) * HARM],
                         wT[p][:, c * 128:(c + 1) * 128],
                         ident_bf[:, :])
    for c in range(NC8):
        bf_transpose(gwh[:, c * 128:(c + 1) * 128], gwTh[:, c * 8:(c + 1) * 8],
                     ident_bf[0:E, 0:E])
        bf_transpose(gwl[:, c * 128:(c + 1) * 128], gwTl[:, c * 8:(c + 1) * 8],
                     ident_bf[0:E, 0:E])
    v_prologue = cnt["v"]
    stpp.close()
    stp.close()

    st_qkv = ExitStack()
    q_fm = sb("q_fm", [128, NC8 * T], BF16, st_qkv, side="right")
    k_fm = sb("k_fm", [128, NC8 * T], BF16, st_qkv, side="right")
    v_aug = sb("v_aug", [128, NT * NHEAD * 65], BF16, st_qkv, side="right")
    st1 = ExitStack()
    res_sm = {p: sb(f"res_sm_{p}", [HARM, 512], BF16, st1) for p in "qkv"}
    _vs0 = sb("v_stage0", [128, 512], BF16, st1)
    v_stage = [_vs0, _vs0]
    x_tm = [sb(f"x_tm{b}", [128, D], F32, st1) for b in range(2)]
    xn_tm = sb("xn_tm", [128, D], F32, st1)
    xnh_tm = [sb(f"xnh_tm{b}", [128, D], BF16, st1) for b in range(2)]
    xnl_tm = [sb(f"xnl_tm{b}", [128, D], BF16, st1) for b in range(2)]
    sqd = sb("sqd", [128, D], BF16, st1)
    xnl_t = sb("xnl_t", [128, D], BF16, st1)
    m_t = sb("m_t", [128, 1], F32, st1)
    ss_t = sb("ss_t", [128, 1], F32, st1)
    rstd_t = sb("rstd_t", [128, 1], F32, st1)
    lg_all = sb("lg_all", [128, 128], F32, st1)
    s_all = sb("s_all", [128, 128], F32, st1)
    eq_all = sb("eq_all", [128, 128], F32, st1)
    tmp_all = eq_all
    mk1 = sb("mk1", [128, 128], F32, st1)
    mk2 = sb("mk2", [128, 128], F32, st1)
    idxB_all = sb("idxB_all", [128, 128], F32, st1)
    comb_all = lg_all
    m16 = sb("m16", [128, 16], F32, st1)
    m2_16 = sb("m2_16", [128, 16], F32, st1)
    z16 = sb("z16", [128, 16], F32, st1)
    a16 = z16
    rw16 = sb("rw16", [128, 16], F32, st1)

    def v3(th):
        return th.ap().rearrange("p (i e) -> p i e", e=8)

    def b3(th16):
        return bass.AP(th16, 0, [[16, 128], [1, 16], [0, 8]])

    emit("gpsimd", lambda g: g.iota(v3(idxB_all), [[0, 16], [1, 8]], base=int(BIG),
                                    channel_multiplier=0,
                                    allow_small_or_imprecise_dtypes=True),
         incs=[("g", 1)])
    g_idx = cnt["g"]
    v_idx_all = 0

    st1p = ExitStack()
    p_t1 = [pt(f"p_t1{b}", [128, 1024], BF16, st1p) for b in range(2)]
    p_g8 = pt("p_g8", [128, 512], F32, st1p)
    p_cb = pt("p_cb", [128, 512], F32, st1p)
    p_pj = [pt(f"p_pj{b}", [128, 512], F32, st1p) for b in range(2)]

    xt_free = [("v", 0), ("v", 0)]
    t1_free = {0: [("v", v_prologue)], 1: [("v", v_prologue)]}
    hilo_free = [("t", 0), ("t", 0)]
    pg_free = ("s", 0)
    pcb_free = ("v", 0)
    pj_free = [("s", 0), ("s", 0)]
    rsm_free = {p: ("t", 0) for p in "qkv"}
    vstage_free = [("t", 0)]
    pjflip = [0]

    def emit_proj_tg(tg, s_fm_ready):
        for pi, p in enumerate("qkv"):
            slot = pi % 2
            for c in range(NC8):
                emit("tensor", lambda te, p=p, c=c, tg=tg, pb=p_pj[slot]: te.matmul(
                    pb[0:HARM, :], basT[p][:, c * HARM:(c + 1) * HARM],
                    xnh_fm[:, c * T + tg * 512: c * T + tg * 512 + 512],
                    start=(c == 0), stop=(c == NC8 - 1)),
                    waits=([("s", s_fm_ready), pj_free[slot], rsm_free[p]]
                           if c == 0 else []),
                    incs=([("t", 1)] if c == NC8 - 1 else []))
            t_res = cnt["t"]
            emit("scalar", lambda sc_, p=p, pb=p_pj[slot]: sc_.activation(
                res_sm[p][:, :], pb[0:HARM, :], AF.Copy),
                waits=[("t", t_res)], incs=[("s", 1)])
            pj_free[slot] = ("s", cnt["s"])
        s_res_all = cnt["s"]
        for p in "qkv":
            for c8 in range(NC8):
                slot = pjflip[0]
                pjflip[0] ^= 1
                emit("tensor", lambda te, p=p, c8=c8, pb=p_pj[slot]: te.matmul(
                    pb[:, :], wT[p][:, c8 * 128:(c8 + 1) * 128],
                    res_sm[p][:, :], start=True, stop=True),
                    waits=[("s", s_res_all), pj_free[slot]], incs=[("t", 1)])
                t_pr = cnt["t"]
                rsm_free[p] = ("t", t_pr)
                if p == "q" or p == "k":
                    dst = q_fm if p == "q" else k_fm
                    emit("scalar", lambda sc_, dst=dst, c8=c8, tg=tg, pb=p_pj[slot]:
                         sc_.activation(
                             dst[:, c8 * T + tg * 512: c8 * T + tg * 512 + 512],
                             pb[:, :], AF.Copy),
                         waits=[("t", t_pr)], incs=[("s", 1)])
                    pj_free[slot] = ("s", cnt["s"])
                else:
                    vs = 0
                    emit("scalar", lambda sc_, vs=vs, pb=p_pj[slot]: sc_.activation(
                        v_stage[vs][:, :], pb[:, :], AF.Copy),
                        waits=[("t", t_pr), vstage_free[0]], incs=[("s", 1)])
                    pj_free[slot] = ("s", cnt["s"])
                    s_vst = cnt["s"]
                    for tt in range(4):
                        itok = tg * 4 + tt
                        slot2 = tt % 2
                        ps = p_t1[slot2]
                        emit("tensor", lambda te, vs=vs, tt=tt, ps=ps: te.transpose(
                            ps[:, 0:128], v_stage[vs][:, tt * 128:(tt + 1) * 128],
                            ident_bf[:]),
                            waits=[("s", s_vst)] + t1_free[slot2], incs=[("t", 1)])
                        t_vt = cnt["t"]
                        for half in range(2):
                            h = 2 * c8 + half
                            emit("vector", lambda v_, itok=itok, h=h, half=half, ps=ps:
                                 v_.tensor_copy(
                                     v_aug[:, itok * NHEAD * 65 + h * 65:
                                           itok * NHEAD * 65 + h * 65 + 64],
                                     ps[:, half * 64: half * 64 + 64]),
                                 waits=[("t", t_vt)], incs=[("v", 1)])
                        t1_free[slot2] = [("v", cnt["v"])]
                    vstage_free[0] = ("t", cnt["t"])
    dx_needed = []
    v_rstd_read = 0
    s_sq_prev = 0
    t_xn_read = 0

    for i in range(2):
        sem = "dxa" if i % 2 == 0 else "dxb"
        emit("sync", lambda sy, i=i: sy.dma_start(
            x_tm[i][:, :], x_h[i * 128:(i + 1) * 128, :]), incs=[(sem, 16)])
        dx_needed.append((sem, cnt[sem]))

    for i in range(NT):
        b = i % 2
        dxw = dx_needed[i]
        emit("vector", lambda v, b=b: v.reduce_sum(m_t[:], x_tm[b][:], axis=AX),
             waits=[dxw], incs=[("v", 1)])
        emit("vector", lambda v: v.tensor_single_scalar(m_t[:], m_t[:], 1.0 / D, ALU.mult),
             incs=[("v", 1)])
        emit("vector", lambda v, b=b: v.tensor_single_scalar(
            xn_tm[:], x_tm[b][:], m_t[:], ALU.subtract),
            waits=[("s", s_sq_prev)], incs=[("v", 1)])
        v_xc = cnt["v"]
        xt_free[b] = ("v", v_xc)
        if i + 2 < NT:
            sem = "dxa" if (i + 2) % 2 == 0 else "dxb"
            emit("sync", lambda sy, i=i, b=b: sy.dma_start(
                x_tm[b][:, :], x_h[(i + 2) * 128:(i + 3) * 128, :]),
                waits=[xt_free[b]], incs=[(sem, 16)])
            dx_needed.append((sem, cnt[sem]))
        emit("scalar", lambda s: s.activation(sqd[:], xn_tm[:], AF.Square,
                                              accum_out=ss_t[:]),
             waits=[("v", v_xc)], incs=[("s", 1)])
        s_sq_prev = cnt["s"]
        emit("scalar", lambda s: s.activation(rstd_t[:], ss_t[:], AF.Sqrt,
                                              scale=1.0 / D, bias=epsln[:]),
             waits=[("v", max(v_const, v_rstd_read))], incs=[("s", 1)])
        s_rs = cnt["s"]
        emit("vector", lambda v: v.reciprocal(rstd_t[:], rstd_t[:]),
             waits=[("s", s_rs)], incs=[("v", 1)])
        emit("vector", lambda v: v.tensor_single_scalar(xn_tm[:], xn_tm[:], rstd_t[:], ALU.mult),
             incs=[("v", 1)])
        v_rstd_read = cnt["v"]
        hb = i % 2
        emit("vector", lambda v, hb=hb: v.tensor_copy(xnh_tm[hb][:], xn_tm[:]),
             waits=[hilo_free[hb]], incs=[("v", 1)])
        emit("vector", lambda v, hb=hb: v.tensor_tensor(
            xnl_tm[hb][:], xn_tm[:], xnh_tm[hb][:], ALU.subtract), incs=[("v", 1)])
        v_hl = cnt["v"]
        for si, src in enumerate((xnh_tm[hb], xnl_tm[hb])):
            for c in range(NC8):
                slot = c % 2
                ps = p_t1[slot]
                emit("tensor", lambda te, src=src, c=c, ps=ps: te.transpose(
                    ps[:, 0:128], src[:, c * 128:(c + 1) * 128], ident_bf[:]),
                    waits=t1_free[slot] + [("v", v_hl)], incs=[("t", 1)])
                t_now = cnt["t"]
                if si == 0:
                    emit("scalar", lambda sc_, c=c, i=i, ps=ps: sc_.activation(
                        xnh_fm[:, c * T + i * 128: c * T + i * 128 + 128],
                        ps[:, 0:128], AF.Copy),
                        waits=[("t", t_now)], incs=[("s", 1)])
                else:
                    emit("scalar", lambda sc_, c=c, ps=ps: sc_.activation(
                        xnl_t[:, c * 128:(c + 1) * 128], ps[:, 0:128], AF.Copy),
                        waits=[("t", t_now)], incs=[("s", 1)])
                t1_free[slot] = [("s", cnt["s"])]
        hilo_free[hb] = ("t", cnt["t"])
        t_xn_read = cnt["t"]
        s_fm_tile = cnt["s"]
        v_fm_tile = cnt["v"]
        nmm = 0
        for (xsel, gsel) in (("h", gwTh), ("h", gwTl), ("l", gwTh)):
            for c in range(NC8):
                first, last = nmm == 0, nmm == 23
                if xsel == "h":
                    xap = xnh_fm[:, c * T + i * 128: c * T + i * 128 + 128]
                else:
                    xap = xnl_t[:, c * 128:(c + 1) * 128]
                emit("tensor", lambda te, xap=xap, gsel=gsel, c=c,
                     first=first, last=last: te.matmul(
                         p_g8[:, 0:8], xap, gsel[:, c * 8:(c + 1) * 8],
                         start=first, stop=last),
                     waits=([("v", v_fm_tile), ("s", s_fm_tile), pg_free] if first else []),
                     incs=([("t", 1)] if last else []))
                nmm += 1
        t_g8 = cnt["t"]
        emit("vector", lambda v, i=i: v.tensor_copy(lg_all[:, i * 8:(i + 1) * 8],
                                                    p_g8[:, 0:8]),
             waits=[("t", t_g8)], incs=[("v", 1)])
        pg_free = ("v", cnt["v"])
        if i % 4 == 3:
            emit_proj_tg(i // 4, s_fm_tile)

    emit("vector", lambda v: v.tensor_reduce(m16[:], v3(lg_all), AX, ALU.max),
         incs=[("v", 1)])
    emit("vector", lambda v: v.tensor_tensor(v3(s_all), v3(lg_all), b3(m16),
                                             ALU.subtract), incs=[("v", 1)])
    v_sub = cnt["v"]
    emit("scalar", lambda s: s.activation(s_all[:], s_all[:], AF.Exp),
         waits=[("v", v_sub)], incs=[("s", 1)])
    s_expall = cnt["s"]
    emit("vector", lambda v: v.tensor_reduce(z16[:], v3(s_all), AX, ALU.add),
         waits=[("s", s_expall)], incs=[("v", 1)])
    emit("vector", lambda v: v.reciprocal(z16[:], z16[:]), incs=[("v", 1)])
    emit("vector", lambda v: v.tensor_tensor(v3(s_all), v3(s_all), b3(z16), ALU.mult),
         incs=[("v", 1)])
    emit("vector", lambda v: v.tensor_reduce(m16[:], v3(s_all), AX, ALU.max),
         incs=[("v", 1)])
    emit("vector", lambda v: v.tensor_tensor(v3(eq_all), v3(s_all), b3(m16),
                                             ALU.is_equal), incs=[("v", 1)])
    emit("vector", lambda v: v.scalar_tensor_tensor(tmp_all[:], eq_all[:], -BIG,
                                                    idxB_all[:], ALU.mult, ALU.add),
         waits=[("g", g_idx)], incs=[("v", 1)])
    emit("vector", lambda v: v.tensor_reduce(a16[:], v3(tmp_all), AX, ALU.min),
         incs=[("v", 1)])
    emit("vector", lambda v: v.tensor_tensor(v3(mk1), v3(tmp_all), b3(a16),
                                             ALU.is_equal), incs=[("v", 1)])
    emit("vector", lambda v: v.tensor_tensor(tmp_all[:], s_all[:], mk1[:], ALU.mult),
         incs=[("v", 1)])
    emit("vector", lambda v: v.tensor_tensor(s_all[:], s_all[:], tmp_all[:],
                                             ALU.subtract), incs=[("v", 1)])
    emit("vector", lambda v: v.tensor_reduce(m2_16[:], v3(s_all), AX, ALU.max),
         incs=[("v", 1)])
    emit("vector", lambda v: v.tensor_tensor(v3(eq_all), v3(s_all), b3(m2_16),
                                             ALU.is_equal), incs=[("v", 1)])
    emit("vector", lambda v: v.scalar_tensor_tensor(tmp_all[:], eq_all[:], -BIG,
                                                    idxB_all[:], ALU.mult, ALU.add),
         incs=[("v", 1)])
    emit("vector", lambda v: v.tensor_reduce(a16[:], v3(tmp_all), AX, ALU.min),
         incs=[("v", 1)])
    emit("vector", lambda v: v.tensor_tensor(v3(mk2), v3(tmp_all), b3(a16),
                                             ALU.is_equal), incs=[("v", 1)])
    emit("vector", lambda v: v.tensor_tensor(rw16[:], m16[:], m2_16[:], ALU.add),
         incs=[("v", 1)])
    emit("vector", lambda v: v.tensor_single_scalar(rw16[:], rw16[:], 1e-20, ALU.add),
         incs=[("v", 1)])
    emit("vector", lambda v: v.reciprocal(rw16[:], rw16[:]), incs=[("v", 1)])
    emit("vector", lambda v: v.tensor_single_scalar(rw16[:], rw16[:], 2.5, ALU.mult),
         incs=[("v", 1)])
    emit("vector", lambda v: v.tensor_tensor(v3(mk1), v3(mk1), b3(m16), ALU.mult),
         incs=[("v", 1)])
    emit("vector", lambda v: v.tensor_tensor(v3(mk2), v3(mk2), b3(m2_16), ALU.mult),
         incs=[("v", 1)])
    emit("vector", lambda v: v.tensor_tensor(comb_all[:], mk1[:], mk2[:], ALU.add),
         incs=[("v", 1)])
    emit("vector", lambda v: v.tensor_tensor(v3(comb_all), v3(comb_all), b3(rw16),
                                             ALU.mult), incs=[("v", 1)])
    v_comb = cnt["v"]
    for i in range(NT):
        emit("tensor", lambda te, i=i: te.transpose(
            p_cb[0:8, 0:128], comb_all[:, i * 8:(i + 1) * 8], ident_f[:]),
            waits=[("v", v_comb), pcb_free], incs=[("t", 1)])
        t_cb = cnt["t"]
        emit("vector", lambda v, i=i: v.tensor_copy(comb_bf[:, i * 128:(i + 1) * 128],
                                                    p_cb[0:8, 0:128]),
             waits=[("t", t_cb)], incs=[("v", 1)])
        pcb_free = ("v", cnt["v"])

    v_p1_done = cnt["v"]
    s_p1_done = cnt["s"]
    st1p.close()
    st1.close()

    st2o = ExitStack()
    o_heads = sb("o_heads", [128, NC8 * T], BF16, st2o)
    recip16 = sb("recip16", [NHEAD, T], BF16, st2o)
    res_bf = sb("res_bf", [HARM, T], BF16, st2o)
    mrow = sb("mrow", [1, 272], F32, st2o)
    mval = sb("mval", [1, 1], BF16, st2o)
    rrow = sb("rrow", [1, 512], F32, st2o)
    rrow_bf = [sb(f"rrow_bf{b}", [1, 512], BF16, st2o) for b in range(2)]

    st2 = ExitStack()
    zrow = [sb(f"zrow{b}", [1, 512], BF16, st2o) for b in range(2)]
    pTs = sb("pTs", [128, 4096], BF16, st2)
    colmax = sb("colmax", [128, 1], F32, st2)

    st2p = ExitStack()
    p_a = pt("p_a", [128, 512], F32, st2p)
    p_b = pt("p_b", [128, 512], F32, st2p)
    p_c = pt("p_c", [128, 512], F32, st2p)

    pa_free = [("v", v_p1_done), ("s", s_p1_done)]
    pbc_free = {0: [("v", v_p1_done), ("s", s_p1_done)],
                1: [("v", v_p1_done), ("s", s_p1_done)]}

    emit("vector", lambda v: v.memset(
        v_aug.ap().rearrange("p (i h c) -> p i h c", i=NT, h=NHEAD)[:, :, :, 64:65], 1.0),
        incs=[("v", 1)])
    v_vaug_ones = cnt["v"]
    stx = ExitStack()
    p_st = pt("p_st", [128, 2048], F32, stx)

    s_qk_done = s_p1_done
    emit("vector", lambda v: v.tensor_reduce(colmax[:], q_fm[:], AX, ALU.max,
                                             apply_absolute_value=True),
         waits=[("s", s_qk_done)], incs=[("v", 1)])
    v_cm = cnt["v"]
    emit("tensor", lambda te: te.transpose(p_a[0:1, 0:128], colmax[:], ident_f[:]),
         waits=[("v", v_cm), pa_free[0]], incs=[("t", 1)])
    t_cm = cnt["t"]
    emit("vector", lambda v: v.tensor_copy(mrow[:, 0:128], p_a[0:1, 0:128]),
         waits=[("t", t_cm)], incs=[("v", 1)])
    emit("vector", lambda v: v.reduce_max(mrow[:, 256:257], mrow[:, 0:128], axis=AX),
         incs=[("v", 1)])
    emit("vector", lambda v: v.tensor_reduce(colmax[:], k_fm[:], AX, ALU.max,
                                             apply_absolute_value=True), incs=[("v", 1)])
    v_cm2 = cnt["v"]
    emit("tensor", lambda te: te.transpose(p_a[0:1, 0:128], colmax[:], ident_f[:]),
         waits=[("v", v_cm2)], incs=[("t", 1)])
    t_cm2 = cnt["t"]
    emit("vector", lambda v: v.tensor_copy(mrow[:, 128:256], p_a[0:1, 0:128]),
         waits=[("t", t_cm2)], incs=[("v", 1)])
    pa_free[0] = ("v", cnt["v"])
    emit("vector", lambda v: v.reduce_max(mrow[:, 257:258], mrow[:, 128:256], axis=AX),
         incs=[("v", 1)])
    emit("vector", lambda v: v.tensor_tensor(mrow[:, 258:259], mrow[:, 256:257],
                                             mrow[:, 257:258], ALU.mult), incs=[("v", 1)])
    emit("vector", lambda v: v.tensor_copy(mval[:], mrow[:, 258:259]), incs=[("v", 1)])
    v_mval = cnt["v"]
    emit("tensor", lambda te: te.matmul(p_a[:, 0:1], onesrow[:], mval[:],
                                        start=True, stop=True),
         waits=[("v", v_mval)], incs=[("t", 1)])
    t_bb = cnt["t"]
    emit("vector", lambda v: v.tensor_single_scalar(bias128[:], p_a[:, 0:1], -8.0, ALU.mult),
         waits=[("t", t_bb)], incs=[("v", 1)])
    pa_free[0] = ("v", cnt["v"])
    v_bias = cnt["v"]

    rrow_free = [("dr0", 0), ("dr1", 0)]
    st_free = [("s", s_p1_done), ("v", v_p1_done)]
    av_free = {0: list(pa_free), 1: list(pbc_free[0])}
    pTs_free = {0: ("t", 0), 1: ("t", 0)}
    pairs = [(2 * hp, 2 * hp + 1, qb) for hp in range(NHEAD // 2) for qb in range(NTG)]
    for mi, (hA, hB, qb) in enumerate(pairs):
        exp_done = [0] * 8
        for pair in range(10):
            if pair >= 2:
                p2 = pair - 2
                pp2 = p2 % 2
                for sidx, h in ((0, hA), (1, hB)):
                    avbank = p_a if sidx == 0 else p_b
                    for j in (0, 1):
                        kc = 2 * p2 + j
                        emit("tensor", lambda te, h=h, kc=kc, sidx=sidx, j=j, pp2=pp2,
                             pb=avbank,
                             first=(kc == 0), last=(kc == 15): te.matmul(
                                 pb[0:65, :],
                                 v_aug[:, kc * NHEAD * 65 + h * 65: kc * NHEAD * 65 + h * 65 + 65],
                                 pTs[:, pp2 * 2048 + sidx * 1024 + j * 512:
                                     pp2 * 2048 + sidx * 1024 + j * 512 + 512],
                                 start=first, stop=last),
                            waits=([("s", exp_done[p2])] if (sidx == 0 and j == 0) else []) +
                                  (av_free[sidx] + ([("v", v_vaug_ones)] if mi == 0 else [])
                                   if kc == 0 else []),
                            incs=[("t", 1)])
                pTs_free[pp2] = ("t", cnt["t"])
            if pair < 8:
                pp = pair % 2
                first = True
                for j in (0, 1):
                    for sidx, h in ((0, hA), (1, hB)):
                        c8, off = h // 2, (h % 2) * 64
                        kc = 2 * pair + j
                        emit("tensor", lambda te, c8=c8, off=off, kc=kc, qb=qb,
                             sidx=sidx, j=j: te.matmul(
                                 p_st[:, sidx * 1024 + j * 512: sidx * 1024 + j * 512 + 512],
                                 k_fm[off:off + 64, c8 * T + kc * 128: c8 * T + kc * 128 + 128],
                                 q_fm[off:off + 64, c8 * T + qb * 512: c8 * T + qb * 512 + 512],
                                 start=True, stop=True),
                            waits=(st_free if first else []) +
                                  ([("v", v_bias)] if (mi == 0 and pair == 0 and first)
                                   else []),
                            incs=[("t", 1)])
                        first = False
                t_sp = cnt["t"]
                emit("scalar", lambda s, pp=pp: s.activation(
                    pTs[:, pp * 2048: pp * 2048 + 2048],
                    p_st[:, 0:2048], AF.Exp,
                    scale=0.125, bias=bias128[:]),
                    waits=[("t", t_sp), pTs_free[pp]], incs=[("s", 1)])
                st_free = [("s", cnt["s"])]
                exp_done[pair] = cnt["s"]
        t_av = cnt["t"]
        for sidx, h in ((0, hA), (1, hB)):
            c8, off = h // 2, (h % 2) * 64
            avbank = p_a if sidx == 0 else p_b
            emit("vector", lambda v, pb=avbank, slot=sidx: v.tensor_copy(
                zrow[slot][:, :], pb[64:65, :]),
                waits=[("t", t_av)], incs=[("v", 1)])
            emit("vector", lambda v, qb=qb, c8=c8, off=off, pb=avbank: v.tensor_copy(
                o_heads[off:off + 64, c8 * T + qb * 512: c8 * T + qb * 512 + 512],
                pb[0:64, :]),
                incs=[("v", 1)])
            av_free[sidx] = [("v", cnt["v"])]
        for sidx, h in ((0, hA), (1, hB)):
            qb_ = qb
            emit("vector", lambda v, slot=sidx: v.reciprocal(rrow[:], zrow[slot][:, :]),
                 incs=[("v", 1)])
            emit("vector", lambda v, slot=sidx: v.tensor_copy(
                rrow_bf[slot][:, :], rrow[:]),
                waits=[rrow_free[sidx]], incs=[("v", 1)])
            v_rr = cnt["v"]
            drsem = "dr0" if sidx == 0 else "dr1"
            emit("sync", lambda sy, h=h, qb=qb_, slot=sidx: sy.dma_start(
                recip16[h:h + 1, qb * 512:(qb + 1) * 512], rrow_bf[slot][:, :]),
                waits=[("v", v_rr)], incs=[(drsem, 16)])
            rrow_free[sidx] = (drsem, cnt[drsem])
    v_recip_all = cnt["v"]
    dr0_all, dr1_all = cnt["dr0"], cnt["dr1"]

    pbc2_free = list(av_free[1]) + list(pbc_free[1])
    for c8 in range(NC8):
        for tg in range(NTG):
            emit("tensor", lambda te, c8=c8, tg=tg: te.matmul(
                p_b[:, :], selc[:, c8 * 128:(c8 + 1) * 128],
                recip16[:, tg * 512:(tg + 1) * 512], start=True, stop=True),
                waits=[("v", v_recip_all), ("dr0", dr0_all), ("dr1", dr1_all)]
                      + list(pbc2_free), incs=[("t", 1)])
            t_bc = cnt["t"]
            emit("vector", lambda v, c8=c8, tg=tg: v.tensor_tensor(
                o_heads[:, c8 * T + tg * 512: c8 * T + tg * 512 + 512],
                o_heads[:, c8 * T + tg * 512: c8 * T + tg * 512 + 512],
                p_b[:, :], ALU.mult),
                waits=[("t", t_bc)], incs=[("v", 1)])
            pbc2_free = [("v", cnt["v"])]
    v_onorm = cnt["v"]

    st2.close()
    st_qkv.close()
    st_acc = ExitStack()
    acc = sb("acc", [128, NC8 * T], F32, st_acc, side="right")

    pa_free2 = [av_free[0][0]]
    pbc_free2 = {0: ("v", v_onorm), 1: ("v", v_onorm)}
    for tg in range(NTG):
        for c in range(NC8):
            emit("tensor", lambda te, c=c, tg=tg: te.matmul(
                p_a[0:HARM, :], basT["o"][:, c * HARM:(c + 1) * HARM],
                o_heads[:, c * T + tg * 512: c * T + tg * 512 + 512],
                start=(c == 0), stop=(c == NC8 - 1)),
                waits=([("v", v_onorm), pa_free2[0], av_free[0][-1]] if c == 0 else []),
                incs=([("t", 1)] if c == NC8 - 1 else []))
        t_ro = cnt["t"]
        emit("scalar", lambda s, tg=tg: s.activation(
            res_bf[:, tg * 512: tg * 512 + 512], p_a[0:HARM, :], AF.Copy),
            waits=[("t", t_ro)], incs=[("s", 1)])
        pa_free2[0] = ("s", cnt["s"])
    s_reso = cnt["s"]
    for c8 in range(NC8):
        for tg in range(NTG):
            slot = (c8 * NTG + tg) % 2
            pbank = p_b if slot == 0 else p_c
            emit("tensor", lambda te, c8=c8, tg=tg, pb=pbank: te.matmul(
                pb[:, :], wT["o"][:, c8 * 128:(c8 + 1) * 128],
                res_bf[:, tg * 512: tg * 512 + 512], start=True, stop=True),
                waits=[("s", s_reso), pbc_free2[slot]], incs=[("t", 1)])
            t_ao = cnt["t"]
            emit("vector", lambda v, c8=c8, tg=tg, pb=pbank: v.tensor_copy(
                acc[:, c8 * T + tg * 512: c8 * T + tg * 512 + 512], pb[:, :]),
                waits=[("t", t_ao)], incs=[("v", 1)])
            pbc_free2[slot] = ("v", cnt["v"])
    v_acc0 = cnt["v"]

    st2o.close()
    stx.close()
    st2p.close()
    stw.close()

    st3 = ExitStack()
    wgbuf = [sb(f"wgbuf{b}", [128, GSZ * 1024], BF16, st3) for b in range(2)]
    wubuf = [sb(f"wubuf{b}", [128, GSZ * 1024], BF16, st3) for b in range(2)]
    wdbuf = [sb(f"wdbuf{b}", [128, GSZ * 1024], BF16, st3) for b in range(2)]
    gbuf = [sb(f"gbuf{b}", [128, GSZ * T], BF16, st3) for b in range(2)]
    c_cur = sb("c_cur", [128, T], F32, st3)
    sgt = [sb(f"sgt{b}", [128, 512], BF16, st3) for b in range(2)]

    st3p = ExitStack()
    p_xg = pt("p_xg", [128, 512], F32, st3p)
    p_xu = pt("p_xu", [128, 512], F32, st3p)
    p_d0 = pt("p_d0", [128, 512], F32, st3p)
    p_d1 = pt("p_d1", [128, 512], F32, st3p)
    p_cc = pt("p_cc", [128, 512], F32, st3p)

    groups = [list(range(g, min(g + GSZ, NMOE))) for g in range(0, NMOE, GSZ)]
    groups += [list(range(g, min(g + GSZ, NHID))) for g in range(NMOE, NHID, GSZ)]

    def chunk_srcs(h):
        if h < NMOE:
            e, fi = h // NFM, h % NFM
            return (w_gate_h[e, :, fi * 128:(fi + 1) * 128],
                    w_up_h[e, :, fi * 128:(fi + 1) * 128],
                    w_down_h[e, fi * 128:(fi + 1) * 128, :])
        fi = h - NMOE
        return (sw_gate_h[:, fi * 128:(fi + 1) * 128],
                sw_up_h[:, fi * 128:(fi + 1) * 128],
                sw_down_h[fi * 128:(fi + 1) * 128, :])

    def emit_group_dma(gi, buf_free_wait):
        buf = gi % 2
        sem = "dwa" if buf == 0 else "dwb"
        first = True
        for loc, h in enumerate(groups[gi]):
            sg, su, sd = chunk_srcs(h)
            for (dst, src) in ((wgbuf[buf], sg), (wubuf[buf], su)):
                emit("gpsimd", lambda g, dst=dst, loc=loc, src=src: g.dma_start(
                    dst.ap()[:, loc * 1024:(loc + 1) * 1024].rearrange(
                        "p (c f) -> p c f", c=NC8),
                    src.rearrange("(c p) f -> p c f", p=128)),
                    waits=([buf_free_wait] if first else []),
                    incs=[(sem, 16)])
                first = False
            emit("gpsimd", lambda g, buf=buf, loc=loc, sd=sd: g.dma_start(
                wdbuf[buf][:, loc * 1024:(loc + 1) * 1024], sd),
                incs=[(sem, 16)])
        return (sem, cnt[sem])

    dma_done = [emit_group_dma(0, ("v", v_acc0)), emit_group_dma(1, ("v", v_acc0))]
    grp_last_t = [0] * len(groups)

    pxg_free = ("v", v_acc0)
    pxu_free = ("v", v_acc0)
    pd_free = {0: ("v", v_acc0), 1: ("v", v_acc0)}
    pcc_free = ("v", v_acc0)
    sgt_free = [("v", 0), ("v", 0)]
    gbuf_free = [("t", 0), ("t", 0)]
    cur_e = -1
    su_slot = 0
    add_done = {}
    v_g_tg = [0] * NTG

    for gi, grp in enumerate(groups):
        buf = gi % 2
        dww = dma_done[gi]
        for loc, h in enumerate(grp):
            is_moe = h < NMOE
            if is_moe and (h // NFM) != cur_e:
                cur_e = h // NFM
                for tg in range(NTG):
                    emit("tensor", lambda te, e=cur_e, tg=tg: te.matmul(
                        p_cc[:, :], sele[:, e * 128:(e + 1) * 128],
                        comb_bf[0:E, tg * 512:(tg + 1) * 512],
                        start=True, stop=True),
                        waits=[("v", v_p1_done), pcc_free], incs=[("t", 1)])
                    t_cc = cnt["t"]
                    emit("vector", lambda v, tg=tg: v.tensor_copy(
                        c_cur[:, tg * 512:(tg + 1) * 512], p_cc[:, :]),
                        waits=[("t", t_cc)], incs=[("v", 1)])
                    pcc_free = ("v", cnt["v"])
            v_ccur = cnt["v"]
            for tg in range(NTG):
                for c in range(NC8):
                    emit("tensor", lambda te, buf=buf, loc=loc, c=c, tg=tg: te.matmul(
                        p_xg[:, :],
                        wgbuf[buf][:, loc * 1024 + c * 128: loc * 1024 + c * 128 + 128],
                        xnh_fm[:, c * T + tg * 512: c * T + tg * 512 + 512],
                        start=(c == 0), stop=(c == NC8 - 1)),
                        waits=([dww, pxg_free] if c == 0 else []),
                        incs=([("t", 1)] if c == NC8 - 1 else []))
                t_xg = cnt["t"]
                for c in range(NC8):
                    emit("tensor", lambda te, buf=buf, loc=loc, c=c, tg=tg: te.matmul(
                        p_xu[:, :],
                        wubuf[buf][:, loc * 1024 + c * 128: loc * 1024 + c * 128 + 128],
                        xnh_fm[:, c * T + tg * 512: c * T + tg * 512 + 512],
                        start=(c == 0), stop=(c == NC8 - 1)),
                        waits=([pxu_free] if c == 0 else []),
                        incs=([("t", 1)] if c == NC8 - 1 else []))
                t_xu = cnt["t"]
                ss = su_slot % 2
                su_slot += 1
                emit("scalar", lambda s, ss=ss: s.activation(sgt[ss][:], p_xg[:, :], AF.Silu),
                     waits=[("t", t_xg), sgt_free[ss]], incs=[("s", 1)])
                pxg_free = ("s", cnt["s"])
                s_silu = cnt["s"]
                gslice = gbuf[buf][:, loc * T + tg * 512: loc * T + tg * 512 + 512]
                emit("vector", lambda v, gs=gslice, ss=ss: v.tensor_tensor(
                    gs, sgt[ss][:], p_xu[:, :], ALU.mult),
                    waits=[("t", t_xu), ("s", s_silu)] +
                          ([gbuf_free[buf]] if (loc == 0 and tg == 0) else []),
                    incs=[("v", 1)])
                pxu_free = ("v", cnt["v"])
                sgt_free[ss] = ("v", cnt["v"])
                if is_moe:
                    emit("vector", lambda v, gs=gslice, tg=tg: v.tensor_tensor(
                        gs, gs, c_cur[:, tg * 512:(tg + 1) * 512], ALU.mult),
                        waits=[("v", v_ccur)], incs=[("v", 1)])
                v_g_tg[tg] = cnt["v"]
        for tg in range(NTG):
            for dc in range(NC8):
                slot = (tg * NC8 + dc) % 2
                pbank = p_d0 if slot == 0 else p_d1
                for li in range(len(grp)):
                    emit("tensor", lambda te, buf=buf, li=li, dc=dc, tg=tg, pb=pbank,
                         first=(li == 0), last=(li == len(grp) - 1): te.matmul(
                             pb[:, :],
                             wdbuf[buf][:, li * 1024 + dc * 128: li * 1024 + dc * 128 + 128],
                             gbuf[buf][:, li * T + tg * 512: li * T + tg * 512 + 512],
                             start=first, stop=last),
                        waits=([("v", v_g_tg[tg]), pd_free[slot]] if li == 0 else []),
                        incs=([("t", 1)] if li == len(grp) - 1 else []))
                t_dn = cnt["t"]
                emit("vector", lambda v, dc=dc, tg=tg, pb=pbank: v.tensor_tensor(
                    acc[:, dc * T + tg * 512: dc * T + tg * 512 + 512],
                    acc[:, dc * T + tg * 512: dc * T + tg * 512 + 512],
                    pb[:, :], ALU.add),
                    waits=[("t", t_dn)], incs=[("v", 1)])
                if gi == len(groups) - 1:
                    add_done[(tg, dc)] = cnt["v"]
                pd_free[slot] = ("v", cnt["v"])
        grp_last_t[gi] = cnt["t"]
        gbuf_free[buf] = ("t", cnt["t"])
        if gi + 2 < len(groups):
            dma_done.append(emit_group_dma(gi + 2, ("t", grp_last_t[gi])))
    v_moe_done = cnt["v"]

    st3p.close()
    st3.close()

    st4 = ExitStack()
    x4 = [sb(f"x4_{b}", [128, D], F32, st4, side="right") for b in range(2)]
    o4 = [sb(f"o4_{b}", [128, D], F32, st4, side="right") for b in range(2)]
    st4p = ExitStack()
    p_o = [pt(f"p_o{b}", [128, 512], F32, st4p) for b in range(2)]

    x4_free = [("v", 0)] * 2
    o4_free = [("doa", 0), ("dob", 0)]
    po_free = {0: ("v", v_moe_done), 1: ("v", v_moe_done)}
    dx4 = []
    X4SEMS = ("dxa", "dxb")
    for i in range(2):
        sem = X4SEMS[i % 2]
        emit("sync", lambda sy, i=i: sy.dma_start(x4[i][:, :], x_h[i * 128:(i + 1) * 128, :]),
             waits=[(sem, cnt[sem])], incs=[(sem, 16)])
        dx4.append((sem, cnt[sem]))
    for i in range(NT):
        b = i % 2
        ob = i % 2
        for c8 in range(NC8):
            slot = (i * NC8 + c8) % 2
            emit("tensor", lambda te, c8=c8, i=i, slot=slot: te.transpose(
                p_o[slot][:, 0:128], acc[:, c8 * T + i * 128: c8 * T + i * 128 + 128],
                ident_f[:]),
                waits=[("v", add_done[(i // 4, c8)]), po_free[slot]], incs=[("t", 1)])
            t_tr = cnt["t"]
            emit("vector", lambda v, c8=c8, b=b, ob=ob, slot=slot: v.tensor_tensor(
                o4[ob][:, c8 * 128:(c8 + 1) * 128], x4[b][:, c8 * 128:(c8 + 1) * 128],
                p_o[slot][:, 0:128], ALU.add),
                waits=[("t", t_tr), dx4[i], o4_free[ob]], incs=[("v", 1)])
            po_free[slot] = ("v", cnt["v"])
        v_out = cnt["v"]
        x4_free[b] = ("v", v_out)
        if i + 2 < NT:
            sem = X4SEMS[(i + 2) % 2]
            emit("sync", lambda sy, i=i, b=b: sy.dma_start(
                x4[b][:, :], x_h[(i + 2) * 128:(i + 3) * 128, :]),
                waits=[x4_free[b], (sem, cnt[sem])], incs=[(sem, 16)])
            dx4.append((sem, cnt[sem]))
        dosem = "doa" if ob == 0 else "dob"
        emit("sync", lambda sy, i=i, ob=ob: sy.dma_start(
            out_h[i * 128:(i + 1) * 128, :], o4[ob][:, :]),
            waits=[("v", v_out)], incs=[(dosem, 16)])
        o4_free[ob] = (dosem, cnt[dosem])
    doa_total, dob_total = cnt["doa"], cnt["dob"]

    with ExitStack() as st_sem:
        sems = {nm: st_sem.enter_context(nc.semaphore(nm)) for nm in cnt}
        with nc.Block() as block:
            def runner(engname):
                ss = {"vector": "v", "scalar": "s"}.get(engname)
                def run(e):
                    for k, (fn, waits, incs) in enumerate(ops[engname]):
                        if ss is not None and k > 0:
                            e.wait_ge(sems[ss], k)
                        for sname, val in waits:
                            e.wait_ge(sems[sname], val)
                        inst = fn(e)
                        for sname, amt in incs:
                            inst.then_inc(sems[sname], amt)
                    if engname == "sync":
                        e.wait_ge(sems["doa"], doa_total)
                        e.wait_ge(sems["dob"], dob_total)
                return run
            block.tensor(runner("tensor"))
            block.vector(runner("vector"))
            block.scalar(runner("scalar"))
            block.gpsimd(runner("gpsimd"))
            block.sync(runner("sync"))

    st4p.close()
    st4.close()
    st_acc.close()
    so.close()
    return nc


_NC_CACHE = None


def kernel(**inputs):
    global _NC_CACHE
    if _NC_CACHE is None:
        _NC_CACHE = build_nc()
    nc = _NC_CACHE
    x = np.ascontiguousarray(np.asarray(inputs["L0_attn_x"], dtype=np.float32))
    common = {
        "gate_w": np.ascontiguousarray(inputs["L0_ffn_gate_w"], dtype=np.float32),
        "w_gate": np.ascontiguousarray(inputs["L0_ffn_w_gate"], dtype=np.float32),
        "w_up": np.ascontiguousarray(inputs["L0_ffn_w_up"], dtype=np.float32),
        "w_down": np.ascontiguousarray(inputs["L0_ffn_w_down"], dtype=np.float32),
        "sw_gate": np.ascontiguousarray(inputs["L0_ffn_sw_gate"], dtype=np.float32),
        "sw_up": np.ascontiguousarray(inputs["L0_ffn_sw_up"], dtype=np.float32),
        "sw_down": np.ascontiguousarray(inputs["L0_ffn_sw_down"], dtype=np.float32),
    }
    for p in "qkvo":
        common[f"basis_{p}"] = np.ascontiguousarray(inputs[f"L0_attn_basis_{p}"], dtype=np.float32)
        common[f"phase_{p}"] = np.ascontiguousarray(inputs[f"L0_attn_phase_{p}"], dtype=np.float32)
        common[f"amp_{p}"] = np.ascontiguousarray(inputs[f"L0_attn_amp_{p}"], dtype=np.float32)
    in_maps = [dict(common, x=np.ascontiguousarray(x[i])) for i in range(8)]
    res = run_bass_kernel_spmd(nc, in_maps, core_ids=list(range(8)))
    return np.stack([r["out"] for r in res.results], axis=0)


if __name__ == "__main__":
    build_nc()
    print("built ok")


# version 13
# speedup vs baseline: 1.0285x; 1.0055x over previous
import numpy as np
from contextlib import ExitStack

import concourse.bass as bass
import concourse.mybir as mybir
from concourse.bass_utils import run_bass_kernel_spmd

F32 = mybir.dt.float32
BF16 = mybir.dt.bfloat16
I32 = mybir.dt.int32
AF = mybir.ActivationFunctionType
ALU = mybir.AluOpType
AX = mybir.AxisListType.X

T = 2048
D = 1024
NC8 = 8
NT = 16
NTG = 4
NHEAD = 16
HARM = 32
E = 8
NFM = 11
NMOE = E * NFM
NFS = 22
NHID = NMOE + NFS
GSZ = 4
BIG = 100.0
EPS_LN = 1e-5
TWO_PI = 6.283185307179586
HALF_PI = 1.5707963267948966
INV_2PI = 1.0 / TWO_PI
PI_LO = 3.1415925

ENGS = ("tensor", "vector", "scalar", "gpsimd", "sync")


def build_nc():
    nc = bass.Bass("TRN2", target_bir_lowering=False)

    x_h = nc.dram_tensor("x", [T, D], F32, kind="ExternalInput")
    basis = {p: nc.dram_tensor(f"basis_{p}", [HARM, D], F32, kind="ExternalInput")
             for p in "qkvo"}
    phase = {p: nc.dram_tensor(f"phase_{p}", [D, HARM], F32, kind="ExternalInput")
             for p in "qkvo"}
    amp = {p: nc.dram_tensor(f"amp_{p}", [D, HARM], F32, kind="ExternalInput")
           for p in "qkvo"}
    gate_w_h = nc.dram_tensor("gate_w", [E, D], F32, kind="ExternalInput")
    w_gate_h = nc.dram_tensor("w_gate", [E, D, 1408], F32, kind="ExternalInput")
    w_up_h = nc.dram_tensor("w_up", [E, D, 1408], F32, kind="ExternalInput")
    w_down_h = nc.dram_tensor("w_down", [E, 1408, D], F32, kind="ExternalInput")
    sw_gate_h = nc.dram_tensor("sw_gate", [D, 2816], F32, kind="ExternalInput")
    sw_up_h = nc.dram_tensor("sw_up", [D, 2816], F32, kind="ExternalInput")
    sw_down_h = nc.dram_tensor("sw_down", [2816, D], F32, kind="ExternalInput")
    out_h = nc.dram_tensor("out", [T, D], F32, kind="ExternalOutput")

    ops = {e: [] for e in ENGS}
    cnt = {s: 0 for s in ("t", "v", "s", "g", "dp", "dwa", "dwb",
                          "dxa", "dxb", "doa", "dob", "dr0", "dr1")}

    SELF_SEM = {"vector": "v", "scalar": "s"}

    def emit(eng, fn, waits=(), incs=()):
        for sname, val in waits:
            assert val <= cnt[sname], f"wait-ahead: {sname} {val} > {cnt[sname]}"
        incs = list(incs)
        ss = SELF_SEM.get(eng)
        if ss is not None and not any(n == ss for n, _ in incs):
            incs.append((ss, 1))
        rec = []
        for sname, amt in incs:
            cnt[sname] += amt
            rec.append((sname, amt))
        ops[eng].append((fn, tuple(waits), tuple(rec)))

    def sb(name, shape, dtype, stack, side=None):
        if side is None:
            return stack.enter_context(nc.sbuf_tensor(name, shape, dtype))
        return stack.enter_context(nc.sbuf_tensor(name, shape, dtype, side=side))

    def pt(name, shape, dtype, stack):
        return stack.enter_context(nc.psum_tensor(name, shape, dtype))

    so = ExitStack()
    ident_bf = sb("ident_bf", [128, 128], BF16, so)
    ident_f = sb("ident_f", [128, 128], F32, so)
    st_iota = ExitStack()
    iota2d = sb("iota2d", [128, 128], F32, st_iota, side="right")
    onesrow = sb("onesrow", [1, 128], BF16, so)
    epsln = sb("epsln", [128, 1], F32, so)
    selc = sb("selc", [16, NC8 * 128], BF16, so)
    sele = sb("sele", [E, E * 128], BF16, so)
    xnh_fm = sb("xnh_fm", [128, NC8 * T], BF16, so)
    comb_bf = sb("comb_bf", [E, T], BF16, so)
    bias128 = sb("bias128", [128, 1], F32, so)
    stw = ExitStack()
    basT = {p: sb(f"basT_{p}", [128, NC8 * HARM], BF16, stw) for p in "qkvo"}
    wT = {p: sb(f"wT_{p}", [HARM, D], BF16, stw) for p in "qkvo"}
    gwTh = sb("gwTh", [128, NC8 * 8], BF16, stw)
    gwTl = sb("gwTl", [128, NC8 * 8], BF16, stw)

    emit("gpsimd", lambda g: g.iota(iota2d[:, :], [[1, 128]], channel_multiplier=-1,
                                    allow_small_or_imprecise_dtypes=True),
         incs=[("g", 1)])
    g_iota = cnt["g"]
    emit("vector", lambda v: v.tensor_single_scalar(ident_bf[:], iota2d[:], 0.0, ALU.is_equal),
         waits=[("g", g_iota)], incs=[("v", 1)])
    emit("vector", lambda v: v.tensor_single_scalar(ident_f[:], iota2d[:], 0.0, ALU.is_equal),
         incs=[("v", 1)])
    emit("vector", lambda v: v.memset(onesrow[:], 1.0), incs=[("v", 1)])
    emit("vector", lambda v: v.memset(epsln[:], EPS_LN), incs=[("v", 1)])
    v_const = cnt["v"]
    st_iota.close()

    stp = ExitStack()
    bas_bf = {p: sb(f"bas_bf_{p}", [HARM, D], BF16, stp, side="right") for p in "qkvo"}
    ph_f = {p: sb(f"ph_f_{p}", [128, NC8 * HARM], F32, stp, side="right") for p in "qkvo"}
    am_f = {p: sb(f"am_f_{p}", [128, NC8 * HARM], F32, stp, side="right") for p in "qkvo"}
    gw_f = sb("gw_f", [E, D], F32, stp, side="right")
    wfull = {p: sb(f"wfull_{p}", [128, NC8 * HARM], BF16, stp, side="right") for p in "qkvo"}
    cw_t = sb("cw_t", [128, NC8 * HARM], F32, stp, side="right")
    cw_k = sb("cw_k", [128, NC8 * HARM], I32, stp, side="right")
    cw_f = sb("cw_f", [128, NC8 * HARM], F32, stp, side="right")
    gwh = sb("gwh", [E, D], BF16, stp, side="right")
    gwl = sb("gwl", [E, D], BF16, stp, side="right")
    gwt = sb("gwt", [E, D], F32, stp, side="right")
    scr_sel = sb("scr_sel", [16, 1024], F32, stp, side="right")

    emit("gpsimd", lambda g: g.iota(
        scr_sel.ap().rearrange("p (g j) -> p g j", g=16),
        [[1, 16], [0, 64]], channel_multiplier=-1,
        allow_small_or_imprecise_dtypes=True), incs=[("g", 1)])
    g_sel = cnt["g"]
    emit("vector", lambda v: v.tensor_single_scalar(selc[:, :], scr_sel[:, :],
                                                    0.0, ALU.is_equal),
         waits=[("g", g_sel)], incs=[("v", 1)])
    v_sel1 = cnt["v"]
    emit("gpsimd", lambda g: g.iota(
        scr_sel[0:8, :].rearrange("p (f j) -> p f j", f=8),
        [[1, 8], [0, 128]], channel_multiplier=-1,
        allow_small_or_imprecise_dtypes=True),
        waits=[("v", v_sel1)], incs=[("g", 1)])
    g_sel2 = cnt["g"]
    emit("vector", lambda v: v.tensor_single_scalar(sele[:, :], scr_sel[0:8, :],
                                                    0.0, ALU.is_equal),
         waits=[("g", g_sel2)], incs=[("v", 1)])

    for p in "qkvo":
        emit("gpsimd", lambda g, p=p: g.dma_start(bas_bf[p][:], basis[p][:, :]),
             incs=[("dp", 16)])
        emit("gpsimd", lambda g, p=p: g.dma_start(
            ph_f[p].ap().rearrange("p (c h) -> p c h", c=NC8),
            phase[p].ap().rearrange("(c p) h -> p c h", p=128)),
            incs=[("dp", 16)])
        emit("gpsimd", lambda g, p=p: g.dma_start(
            am_f[p].ap().rearrange("p (c h) -> p c h", c=NC8),
            amp[p].ap().rearrange("(c p) h -> p c h", p=128)),
            incs=[("dp", 16)])
    emit("gpsimd", lambda g: g.dma_start(gw_f[:], gate_w_h[:, :]), incs=[("dp", 16)])
    dp_params = cnt["dp"]

    for p in "qkvo":
        P = ph_f[p]
        emit("vector", lambda v, P=P: v.tensor_scalar(
            cw_t[:], P[:], HALF_PI, INV_2PI, ALU.add, ALU.mult),
            waits=[("dp", dp_params)], incs=[("v", 1)])
        emit("vector", lambda v: v.tensor_copy(cw_k[:], cw_t[:]), incs=[("v", 1)])
        emit("vector", lambda v: v.tensor_copy(cw_f[:], cw_k[:]), incs=[("v", 1)])
        emit("vector", lambda v, P=P: v.tensor_single_scalar(cw_t[:], P[:], HALF_PI, ALU.add),
             incs=[("v", 1)])
        emit("vector", lambda v: v.scalar_tensor_tensor(
            cw_t[:], cw_f[:], -TWO_PI, cw_t[:], ALU.mult, ALU.add), incs=[("v", 1)])
        emit("vector", lambda v: v.tensor_single_scalar(cw_f[:], cw_t[:], PI_LO, ALU.is_gt),
             incs=[("v", 1)])
        emit("vector", lambda v: v.scalar_tensor_tensor(
            cw_t[:], cw_f[:], -TWO_PI, cw_t[:], ALU.mult, ALU.add), incs=[("v", 1)])
        emit("vector", lambda v: v.tensor_single_scalar(cw_f[:], cw_t[:], -PI_LO, ALU.is_lt),
             incs=[("v", 1)])
        emit("vector", lambda v: v.scalar_tensor_tensor(
            cw_t[:], cw_f[:], TWO_PI, cw_t[:], ALU.mult, ALU.add), incs=[("v", 1)])
        emit("vector", lambda v: v.tensor_single_scalar(cw_t[:], cw_t[:], PI_LO, ALU.min),
             incs=[("v", 1)])
        emit("vector", lambda v: v.tensor_single_scalar(cw_t[:], cw_t[:], -PI_LO, ALU.max),
             incs=[("v", 1)])
        v_red = cnt["v"]
        emit("scalar", lambda s: s.activation(cw_f[:], cw_t[:], AF.Sin, scale=-1.0),
             waits=[("v", v_red)], incs=[("s", 1)])
        s_sin = cnt["s"]
        emit("vector", lambda v, p=p: v.tensor_tensor(wfull[p][:], cw_f[:], am_f[p][:], ALU.mult),
             waits=[("s", s_sin)], incs=[("v", 1)])
    emit("vector", lambda v: v.tensor_copy(gwh[:], gw_f[:]),
         waits=[("dp", dp_params)], incs=[("v", 1)])
    emit("vector", lambda v: v.tensor_tensor(gwt[:], gw_f[:], gwh[:], ALU.subtract),
         incs=[("v", 1)])
    emit("vector", lambda v: v.tensor_copy(gwl[:], gwt[:]), incs=[("v", 1)])
    v_params = cnt["v"]

    stpp = ExitStack()
    p_tr = [pt(f"p_tr{b}", [128, 1024], BF16, stpp) for b in range(2)]
    tr_free = {0: ("v", 0), 1: ("v", 0)}

    def bf_transpose(src_ap, dst_ap, idn):
        slot = bf_transpose.flip
        bf_transpose.flip ^= 1
        ps = p_tr[slot]
        emit("tensor", lambda te, s=src_ap, i=idn, ps=ps: te.transpose(
            ps[0:s.shape[1], 0:s.shape[0]], s, i),
            waits=[tr_free[slot], ("v", v_params)], incs=[("t", 1)])
        t_now = cnt["t"]
        emit("vector", lambda v, d=dst_ap, ps=ps, s=src_ap: v.tensor_copy(
            d, ps[0:s.shape[1], 0:s.shape[0]]),
            waits=[("t", t_now)], incs=[("v", 1)])
        tr_free[slot] = ("v", cnt["v"])
    bf_transpose.flip = 0

    for p in "qkvo":
        for c in range(NC8):
            bf_transpose(bas_bf[p][:, c * 128:(c + 1) * 128],
                         basT[p][:, c * HARM:(c + 1) * HARM],
                         ident_bf[0:HARM, 0:HARM])
        for c in range(NC8):
            bf_transpose(wfull[p][:, c * HARM:(c + 1) * HARM],
                         wT[p][:, c * 128:(c + 1) * 128],
                         ident_bf[:, :])
    for c in range(NC8):
        bf_transpose(gwh[:, c * 128:(c + 1) * 128], gwTh[:, c * 8:(c + 1) * 8],
                     ident_bf[0:E, 0:E])
        bf_transpose(gwl[:, c * 128:(c + 1) * 128], gwTl[:, c * 8:(c + 1) * 8],
                     ident_bf[0:E, 0:E])
    v_prologue = cnt["v"]
    stpp.close()
    stp.close()

    st_qkv = ExitStack()
    q_fm = sb("q_fm", [128, NC8 * T], BF16, st_qkv, side="right")
    k_fm = sb("k_fm", [128, NC8 * T], BF16, st_qkv, side="right")
    v_aug = sb("v_aug", [128, NT * NHEAD * 65], BF16, st_qkv, side="right")
    st1 = ExitStack()
    res_sm = {p: sb(f"res_sm_{p}", [HARM, 512], BF16, st1) for p in "qkv"}
    _vs0 = sb("v_stage0", [128, 512], BF16, st1)
    v_stage = [_vs0, _vs0]
    x_tm = [sb(f"x_tm{b}", [128, D], F32, st1) for b in range(2)]
    xn_tm = sb("xn_tm", [128, D], F32, st1)
    xnh_tm = [sb(f"xnh_tm{b}", [128, D], BF16, st1) for b in range(2)]
    xnl_tm = [sb(f"xnl_tm{b}", [128, D], BF16, st1) for b in range(2)]
    sqd = sb("sqd", [128, D], BF16, st1)
    xnl_t = sb("xnl_t", [128, D], BF16, st1)
    m_t = sb("m_t", [128, 1], F32, st1)
    ss_t = sb("ss_t", [128, 1], F32, st1)
    rstd_t = sb("rstd_t", [128, 1], F32, st1)
    lg_all = sb("lg_all", [128, 128], F32, st1)
    s_all = sb("s_all", [128, 128], F32, st1)
    eq_all = sb("eq_all", [128, 128], F32, st1)
    tmp_all = eq_all
    mk1 = sb("mk1", [128, 128], F32, st1)
    mk2 = sb("mk2", [128, 128], F32, st1)
    idxB_all = sb("idxB_all", [128, 128], F32, st1)
    comb_all = lg_all
    m16 = sb("m16", [128, 16], F32, st1)
    m2_16 = sb("m2_16", [128, 16], F32, st1)
    z16 = sb("z16", [128, 16], F32, st1)
    a16 = z16
    rw16 = sb("rw16", [128, 16], F32, st1)

    def v3(th):
        return th.ap().rearrange("p (i e) -> p i e", e=8)

    def b3(th16):
        return bass.AP(th16, 0, [[16, 128], [1, 16], [0, 8]])

    emit("gpsimd", lambda g: g.iota(v3(idxB_all), [[0, 16], [1, 8]], base=int(BIG),
                                    channel_multiplier=0,
                                    allow_small_or_imprecise_dtypes=True),
         incs=[("g", 1)])
    g_idx = cnt["g"]
    v_idx_all = 0

    st1p = ExitStack()
    p_t1 = [pt(f"p_t1{b}", [128, 1024], BF16, st1p) for b in range(2)]
    p_g8 = pt("p_g8", [128, 512], F32, st1p)
    p_cb = pt("p_cb", [128, 512], F32, st1p)
    p_pj = [pt(f"p_pj{b}", [128, 512], F32, st1p) for b in range(4)]

    xt_free = [("v", 0), ("v", 0)]
    t1_free = {0: [("v", v_prologue)], 1: [("v", v_prologue)]}
    hilo_free = [("t", 0), ("t", 0)]
    pg_free = ("s", 0)
    pcb_free = ("v", 0)
    pj_free = [("s", 0), ("s", 0), ("s", 0), ("s", 0)]
    rsm_free = {p: ("t", 0) for p in "qkv"}
    vstage_free = [("t", 0)]
    pjflip = [0]

    def emit_proj_tg(tg, s_fm_ready):
        for pi, p in enumerate("qkv"):
            slot = pi % 2
            for c in range(NC8):
                emit("tensor", lambda te, p=p, c=c, tg=tg, pb=p_pj[slot]: te.matmul(
                    pb[0:HARM, :], basT[p][:, c * HARM:(c + 1) * HARM],
                    xnh_fm[:, c * T + tg * 512: c * T + tg * 512 + 512],
                    start=(c == 0), stop=(c == NC8 - 1)),
                    waits=([("s", s_fm_ready), pj_free[slot], rsm_free[p]]
                           if c == 0 else []),
                    incs=([("t", 1)] if c == NC8 - 1 else []))
            t_res = cnt["t"]
            emit("scalar", lambda sc_, p=p, pb=p_pj[slot]: sc_.activation(
                res_sm[p][:, :], pb[0:HARM, :], AF.Copy),
                waits=[("t", t_res)], incs=[("s", 1)])
            pj_free[slot] = ("s", cnt["s"])
        s_res_all = cnt["s"]
        for p in "qkv":
            for c8 in range(NC8):
                slot = pjflip[0]
                pjflip[0] = (pjflip[0] + 1) % 4
                emit("tensor", lambda te, p=p, c8=c8, pb=p_pj[slot]: te.matmul(
                    pb[:, :], wT[p][:, c8 * 128:(c8 + 1) * 128],
                    res_sm[p][:, :], start=True, stop=True),
                    waits=[("s", s_res_all), pj_free[slot]], incs=[("t", 1)])
                t_pr = cnt["t"]
                rsm_free[p] = ("t", t_pr)
                if p == "q" or p == "k":
                    dst = q_fm if p == "q" else k_fm
                    if slot % 2 == 0:
                        emit("scalar", lambda sc_, dst=dst, c8=c8, tg=tg, pb=p_pj[slot]:
                             sc_.activation(
                                 dst[:, c8 * T + tg * 512: c8 * T + tg * 512 + 512],
                                 pb[:, :], AF.Copy),
                             waits=[("t", t_pr)], incs=[("s", 1)])
                        pj_free[slot] = ("s", cnt["s"])
                    else:
                        emit("vector", lambda v_, dst=dst, c8=c8, tg=tg, pb=p_pj[slot]:
                             v_.tensor_copy(
                                 dst[:, c8 * T + tg * 512: c8 * T + tg * 512 + 512],
                                 pb[:, :]),
                             waits=[("t", t_pr)], incs=[("v", 1)])
                        pj_free[slot] = ("v", cnt["v"])
                else:
                    vs = 0
                    emit("scalar", lambda sc_, vs=vs, pb=p_pj[slot]: sc_.activation(
                        v_stage[vs][:, :], pb[:, :], AF.Copy),
                        waits=[("t", t_pr), vstage_free[0]], incs=[("s", 1)])
                    pj_free[slot] = ("s", cnt["s"])
                    s_vst = cnt["s"]
                    for tt in range(4):
                        itok = tg * 4 + tt
                        slot2 = tt % 2
                        ps = p_t1[slot2]
                        emit("tensor", lambda te, vs=vs, tt=tt, ps=ps: te.transpose(
                            ps[:, 0:128], v_stage[vs][:, tt * 128:(tt + 1) * 128],
                            ident_bf[:]),
                            waits=[("s", s_vst)] + t1_free[slot2], incs=[("t", 1)])
                        t_vt = cnt["t"]
                        for half in range(2):
                            h = 2 * c8 + half
                            emit("vector", lambda v_, itok=itok, h=h, half=half, ps=ps:
                                 v_.tensor_copy(
                                     v_aug[:, itok * NHEAD * 65 + h * 65:
                                           itok * NHEAD * 65 + h * 65 + 64],
                                     ps[:, half * 64: half * 64 + 64]),
                                 waits=[("t", t_vt)], incs=[("v", 1)])
                        t1_free[slot2] = [("v", cnt["v"])]
                    vstage_free[0] = ("t", cnt["t"])
    dx_needed = []
    v_rstd_read = 0
    s_sq_prev = 0
    t_xn_read = 0

    for i in range(2):
        sem = "dxa" if i % 2 == 0 else "dxb"
        emit("sync", lambda sy, i=i: sy.dma_start(
            x_tm[i][:, :], x_h[i * 128:(i + 1) * 128, :]), incs=[(sem, 16)])
        dx_needed.append((sem, cnt[sem]))

    for i in range(NT):
        b = i % 2
        dxw = dx_needed[i]
        emit("vector", lambda v, b=b: v.reduce_sum(m_t[:], x_tm[b][:], axis=AX),
             waits=[dxw], incs=[("v", 1)])
        emit("vector", lambda v: v.tensor_single_scalar(m_t[:], m_t[:], 1.0 / D, ALU.mult),
             incs=[("v", 1)])
        emit("vector", lambda v, b=b: v.tensor_single_scalar(
            xn_tm[:], x_tm[b][:], m_t[:], ALU.subtract),
            waits=[("s", s_sq_prev)], incs=[("v", 1)])
        v_xc = cnt["v"]
        xt_free[b] = ("v", v_xc)
        if i + 2 < NT:
            sem = "dxa" if (i + 2) % 2 == 0 else "dxb"
            emit("sync", lambda sy, i=i, b=b: sy.dma_start(
                x_tm[b][:, :], x_h[(i + 2) * 128:(i + 3) * 128, :]),
                waits=[xt_free[b]], incs=[(sem, 16)])
            dx_needed.append((sem, cnt[sem]))
        emit("scalar", lambda s: s.activation(sqd[:], xn_tm[:], AF.Square,
                                              accum_out=ss_t[:]),
             waits=[("v", v_xc)], incs=[("s", 1)])
        s_sq_prev = cnt["s"]
        emit("scalar", lambda s: s.activation(rstd_t[:], ss_t[:], AF.Sqrt,
                                              scale=1.0 / D, bias=epsln[:]),
             waits=[("v", max(v_const, v_rstd_read))], incs=[("s", 1)])
        s_rs = cnt["s"]
        emit("vector", lambda v: v.reciprocal(rstd_t[:], rstd_t[:]),
             waits=[("s", s_rs)], incs=[("v", 1)])
        emit("vector", lambda v: v.tensor_single_scalar(xn_tm[:], xn_tm[:], rstd_t[:], ALU.mult),
             incs=[("v", 1)])
        v_rstd_read = cnt["v"]
        hb = i % 2
        emit("vector", lambda v, hb=hb: v.tensor_copy(xnh_tm[hb][:], xn_tm[:]),
             waits=[hilo_free[hb]], incs=[("v", 1)])
        emit("vector", lambda v, hb=hb: v.tensor_tensor(
            xnl_tm[hb][:], xn_tm[:], xnh_tm[hb][:], ALU.subtract), incs=[("v", 1)])
        v_hl = cnt["v"]
        for si, src in enumerate((xnh_tm[hb], xnl_tm[hb])):
            for c in range(NC8):
                slot = c % 2
                ps = p_t1[slot]
                emit("tensor", lambda te, src=src, c=c, ps=ps: te.transpose(
                    ps[:, 0:128], src[:, c * 128:(c + 1) * 128], ident_bf[:]),
                    waits=t1_free[slot] + [("v", v_hl)], incs=[("t", 1)])
                t_now = cnt["t"]
                if si == 0:
                    emit("scalar", lambda sc_, c=c, i=i, ps=ps: sc_.activation(
                        xnh_fm[:, c * T + i * 128: c * T + i * 128 + 128],
                        ps[:, 0:128], AF.Copy),
                        waits=[("t", t_now)], incs=[("s", 1)])
                else:
                    emit("scalar", lambda sc_, c=c, ps=ps: sc_.activation(
                        xnl_t[:, c * 128:(c + 1) * 128], ps[:, 0:128], AF.Copy),
                        waits=[("t", t_now)], incs=[("s", 1)])
                t1_free[slot] = [("s", cnt["s"])]
        hilo_free[hb] = ("t", cnt["t"])
        t_xn_read = cnt["t"]
        s_fm_tile = cnt["s"]
        v_fm_tile = cnt["v"]
        nmm = 0
        for (xsel, gsel) in (("h", gwTh), ("h", gwTl), ("l", gwTh)):
            for c in range(NC8):
                first, last = nmm == 0, nmm == 23
                if xsel == "h":
                    xap = xnh_fm[:, c * T + i * 128: c * T + i * 128 + 128]
                else:
                    xap = xnl_t[:, c * 128:(c + 1) * 128]
                emit("tensor", lambda te, xap=xap, gsel=gsel, c=c,
                     first=first, last=last: te.matmul(
                         p_g8[:, 0:8], xap, gsel[:, c * 8:(c + 1) * 8],
                         start=first, stop=last),
                     waits=([("v", v_fm_tile), ("s", s_fm_tile), pg_free] if first else []),
                     incs=([("t", 1)] if last else []))
                nmm += 1
        t_g8 = cnt["t"]
        emit("vector", lambda v, i=i: v.tensor_copy(lg_all[:, i * 8:(i + 1) * 8],
                                                    p_g8[:, 0:8]),
             waits=[("t", t_g8)], incs=[("v", 1)])
        pg_free = ("v", cnt["v"])
        if i % 4 == 3:
            emit_proj_tg(i // 4, s_fm_tile)

    emit("vector", lambda v: v.tensor_reduce(m16[:], v3(lg_all), AX, ALU.max),
         incs=[("v", 1)])
    emit("vector", lambda v: v.tensor_tensor(v3(s_all), v3(lg_all), b3(m16),
                                             ALU.subtract), incs=[("v", 1)])
    v_sub = cnt["v"]
    emit("scalar", lambda s: s.activation(s_all[:], s_all[:], AF.Exp),
         waits=[("v", v_sub)], incs=[("s", 1)])
    s_expall = cnt["s"]
    emit("vector", lambda v: v.tensor_reduce(z16[:], v3(s_all), AX, ALU.add),
         waits=[("s", s_expall)], incs=[("v", 1)])
    emit("vector", lambda v: v.reciprocal(z16[:], z16[:]), incs=[("v", 1)])
    emit("vector", lambda v: v.tensor_tensor(v3(s_all), v3(s_all), b3(z16), ALU.mult),
         incs=[("v", 1)])
    emit("vector", lambda v: v.tensor_reduce(m16[:], v3(s_all), AX, ALU.max),
         incs=[("v", 1)])
    emit("vector", lambda v: v.tensor_tensor(v3(eq_all), v3(s_all), b3(m16),
                                             ALU.is_equal), incs=[("v", 1)])
    emit("vector", lambda v: v.scalar_tensor_tensor(tmp_all[:], eq_all[:], -BIG,
                                                    idxB_all[:], ALU.mult, ALU.add),
         waits=[("g", g_idx)], incs=[("v", 1)])
    emit("vector", lambda v: v.tensor_reduce(a16[:], v3(tmp_all), AX, ALU.min),
         incs=[("v", 1)])
    emit("vector", lambda v: v.tensor_tensor(v3(mk1), v3(tmp_all), b3(a16),
                                             ALU.is_equal), incs=[("v", 1)])
    emit("vector", lambda v: v.tensor_tensor(tmp_all[:], s_all[:], mk1[:], ALU.mult),
         incs=[("v", 1)])
    emit("vector", lambda v: v.tensor_tensor(s_all[:], s_all[:], tmp_all[:],
                                             ALU.subtract), incs=[("v", 1)])
    emit("vector", lambda v: v.tensor_reduce(m2_16[:], v3(s_all), AX, ALU.max),
         incs=[("v", 1)])
    emit("vector", lambda v: v.tensor_tensor(v3(eq_all), v3(s_all), b3(m2_16),
                                             ALU.is_equal), incs=[("v", 1)])
    emit("vector", lambda v: v.scalar_tensor_tensor(tmp_all[:], eq_all[:], -BIG,
                                                    idxB_all[:], ALU.mult, ALU.add),
         incs=[("v", 1)])
    emit("vector", lambda v: v.tensor_reduce(a16[:], v3(tmp_all), AX, ALU.min),
         incs=[("v", 1)])
    emit("vector", lambda v: v.tensor_tensor(v3(mk2), v3(tmp_all), b3(a16),
                                             ALU.is_equal), incs=[("v", 1)])
    emit("vector", lambda v: v.tensor_tensor(rw16[:], m16[:], m2_16[:], ALU.add),
         incs=[("v", 1)])
    emit("vector", lambda v: v.tensor_single_scalar(rw16[:], rw16[:], 1e-20, ALU.add),
         incs=[("v", 1)])
    emit("vector", lambda v: v.reciprocal(rw16[:], rw16[:]), incs=[("v", 1)])
    emit("vector", lambda v: v.tensor_single_scalar(rw16[:], rw16[:], 2.5, ALU.mult),
         incs=[("v", 1)])
    emit("vector", lambda v: v.tensor_tensor(v3(mk1), v3(mk1), b3(m16), ALU.mult),
         incs=[("v", 1)])
    emit("vector", lambda v: v.tensor_tensor(v3(mk2), v3(mk2), b3(m2_16), ALU.mult),
         incs=[("v", 1)])
    emit("vector", lambda v: v.tensor_tensor(comb_all[:], mk1[:], mk2[:], ALU.add),
         incs=[("v", 1)])
    emit("vector", lambda v: v.tensor_tensor(v3(comb_all), v3(comb_all), b3(rw16),
                                             ALU.mult), incs=[("v", 1)])
    v_comb = cnt["v"]
    for i in range(NT):
        emit("tensor", lambda te, i=i: te.transpose(
            p_cb[0:8, 0:128], comb_all[:, i * 8:(i + 1) * 8], ident_f[:]),
            waits=[("v", v_comb), pcb_free], incs=[("t", 1)])
        t_cb = cnt["t"]
        emit("vector", lambda v, i=i: v.tensor_copy(comb_bf[:, i * 128:(i + 1) * 128],
                                                    p_cb[0:8, 0:128]),
             waits=[("t", t_cb)], incs=[("v", 1)])
        pcb_free = ("v", cnt["v"])

    v_p1_done = cnt["v"]
    s_p1_done = cnt["s"]
    st1p.close()
    st1.close()

    st2o = ExitStack()
    o_heads = sb("o_heads", [128, NC8 * T], BF16, st2o)
    recip16 = sb("recip16", [NHEAD, T], BF16, st2o)
    res_bf = sb("res_bf", [HARM, T], BF16, st2o)
    mrow = sb("mrow", [1, 272], F32, st2o)
    mval = sb("mval", [1, 1], BF16, st2o)
    rrow = sb("rrow", [1, 512], F32, st2o)
    rrow_bf = [sb(f"rrow_bf{b}", [1, 512], BF16, st2o) for b in range(2)]

    st2 = ExitStack()
    zrow = [sb(f"zrow{b}", [1, 512], BF16, st2o) for b in range(2)]
    pTs = sb("pTs", [128, 4096], BF16, st2)
    colmax = sb("colmax", [128, 1], F32, st2)

    st2p = ExitStack()
    p_a = pt("p_a", [128, 512], F32, st2p)
    p_b = pt("p_b", [128, 512], F32, st2p)
    p_c = pt("p_c", [128, 512], F32, st2p)

    pa_free = [("v", v_p1_done), ("s", s_p1_done)]
    pbc_free = {0: [("v", v_p1_done), ("s", s_p1_done)],
                1: [("v", v_p1_done), ("s", s_p1_done)]}

    emit("vector", lambda v: v.memset(
        v_aug.ap().rearrange("p (i h c) -> p i h c", i=NT, h=NHEAD)[:, :, :, 64:65], 1.0),
        incs=[("v", 1)])
    v_vaug_ones = cnt["v"]
    stx = ExitStack()
    p_st = pt("p_st", [128, 2048], F32, stx)

    s_qk_done = s_p1_done
    emit("vector", lambda v: v.tensor_reduce(colmax[:], q_fm[:], AX, ALU.max,
                                             apply_absolute_value=True),
         waits=[("s", s_qk_done)], incs=[("v", 1)])
    v_cm = cnt["v"]
    emit("tensor", lambda te: te.transpose(p_a[0:1, 0:128], colmax[:], ident_f[:]),
         waits=[("v", v_cm), pa_free[0]], incs=[("t", 1)])
    t_cm = cnt["t"]
    emit("vector", lambda v: v.tensor_copy(mrow[:, 0:128], p_a[0:1, 0:128]),
         waits=[("t", t_cm)], incs=[("v", 1)])
    emit("vector", lambda v: v.reduce_max(mrow[:, 256:257], mrow[:, 0:128], axis=AX),
         incs=[("v", 1)])
    emit("vector", lambda v: v.tensor_reduce(colmax[:], k_fm[:], AX, ALU.max,
                                             apply_absolute_value=True), incs=[("v", 1)])
    v_cm2 = cnt["v"]
    emit("tensor", lambda te: te.transpose(p_a[0:1, 0:128], colmax[:], ident_f[:]),
         waits=[("v", v_cm2)], incs=[("t", 1)])
    t_cm2 = cnt["t"]
    emit("vector", lambda v: v.tensor_copy(mrow[:, 128:256], p_a[0:1, 0:128]),
         waits=[("t", t_cm2)], incs=[("v", 1)])
    pa_free[0] = ("v", cnt["v"])
    emit("vector", lambda v: v.reduce_max(mrow[:, 257:258], mrow[:, 128:256], axis=AX),
         incs=[("v", 1)])
    emit("vector", lambda v: v.tensor_tensor(mrow[:, 258:259], mrow[:, 256:257],
                                             mrow[:, 257:258], ALU.mult), incs=[("v", 1)])
    emit("vector", lambda v: v.tensor_copy(mval[:], mrow[:, 258:259]), incs=[("v", 1)])
    v_mval = cnt["v"]
    emit("tensor", lambda te: te.matmul(p_a[:, 0:1], onesrow[:], mval[:],
                                        start=True, stop=True),
         waits=[("v", v_mval)], incs=[("t", 1)])
    t_bb = cnt["t"]
    emit("vector", lambda v: v.tensor_single_scalar(bias128[:], p_a[:, 0:1], -8.0, ALU.mult),
         waits=[("t", t_bb)], incs=[("v", 1)])
    pa_free[0] = ("v", cnt["v"])
    v_bias = cnt["v"]

    rrow_free = [("dr0", 0), ("dr1", 0)]
    st_free = [("s", s_p1_done), ("v", v_p1_done)]
    av_free = {0: list(pa_free), 1: list(pbc_free[0])}
    pTs_free = {0: ("t", 0), 1: ("t", 0)}
    pairs = [(2 * hp, 2 * hp + 1, qb) for hp in range(NHEAD // 2) for qb in range(NTG)]
    for mi, (hA, hB, qb) in enumerate(pairs):
        exp_done = [0] * 8
        for pair in range(10):
            if pair >= 2:
                p2 = pair - 2
                pp2 = p2 % 2
                for sidx, h in ((0, hA), (1, hB)):
                    avbank = p_a if sidx == 0 else p_b
                    for j in (0, 1):
                        kc = 2 * p2 + j
                        emit("tensor", lambda te, h=h, kc=kc, sidx=sidx, j=j, pp2=pp2,
                             pb=avbank,
                             first=(kc == 0), last=(kc == 15): te.matmul(
                                 pb[0:65, :],
                                 v_aug[:, kc * NHEAD * 65 + h * 65: kc * NHEAD * 65 + h * 65 + 65],
                                 pTs[:, pp2 * 2048 + sidx * 1024 + j * 512:
                                     pp2 * 2048 + sidx * 1024 + j * 512 + 512],
                                 start=first, stop=last),
                            waits=([("s", exp_done[p2])] if (sidx == 0 and j == 0) else []) +
                                  (av_free[sidx] + ([("v", v_vaug_ones)] if mi == 0 else [])
                                   if kc == 0 else []),
                            incs=[("t", 1)])
                pTs_free[pp2] = ("t", cnt["t"])
            if pair < 8:
                pp = pair % 2
                first = True
                for j in (0, 1):
                    for sidx, h in ((0, hA), (1, hB)):
                        c8, off = h // 2, (h % 2) * 64
                        kc = 2 * pair + j
                        emit("tensor", lambda te, c8=c8, off=off, kc=kc, qb=qb,
                             sidx=sidx, j=j: te.matmul(
                                 p_st[:, sidx * 1024 + j * 512: sidx * 1024 + j * 512 + 512],
                                 k_fm[off:off + 64, c8 * T + kc * 128: c8 * T + kc * 128 + 128],
                                 q_fm[off:off + 64, c8 * T + qb * 512: c8 * T + qb * 512 + 512],
                                 start=True, stop=True),
                            waits=(st_free if first else []) +
                                  ([("v", v_bias)] if (mi == 0 and pair == 0 and first)
                                   else []),
                            incs=[("t", 1)])
                        first = False
                t_sp = cnt["t"]
                emit("scalar", lambda s, pp=pp: s.activation(
                    pTs[:, pp * 2048: pp * 2048 + 2048],
                    p_st[:, 0:2048], AF.Exp,
                    scale=0.125, bias=bias128[:]),
                    waits=[("t", t_sp), pTs_free[pp]], incs=[("s", 1)])
                st_free = [("s", cnt["s"])]
                exp_done[pair] = cnt["s"]
        t_av = cnt["t"]
        for sidx, h in ((0, hA), (1, hB)):
            c8, off = h // 2, (h % 2) * 64
            avbank = p_a if sidx == 0 else p_b
            emit("vector", lambda v, pb=avbank, slot=sidx: v.tensor_copy(
                zrow[slot][:, :], pb[64:65, :]),
                waits=[("t", t_av)], incs=[("v", 1)])
            emit("vector", lambda v, qb=qb, c8=c8, off=off, pb=avbank: v.tensor_copy(
                o_heads[off:off + 64, c8 * T + qb * 512: c8 * T + qb * 512 + 512],
                pb[0:64, :]),
                incs=[("v", 1)])
            av_free[sidx] = [("v", cnt["v"])]
        for sidx, h in ((0, hA), (1, hB)):
            qb_ = qb
            emit("vector", lambda v, slot=sidx: v.reciprocal(rrow[:], zrow[slot][:, :]),
                 incs=[("v", 1)])
            emit("vector", lambda v, slot=sidx: v.tensor_copy(
                rrow_bf[slot][:, :], rrow[:]),
                waits=[rrow_free[sidx]], incs=[("v", 1)])
            v_rr = cnt["v"]
            drsem = "dr0" if sidx == 0 else "dr1"
            emit("sync", lambda sy, h=h, qb=qb_, slot=sidx: sy.dma_start(
                recip16[h:h + 1, qb * 512:(qb + 1) * 512], rrow_bf[slot][:, :]),
                waits=[("v", v_rr)], incs=[(drsem, 16)])
            rrow_free[sidx] = (drsem, cnt[drsem])
    v_recip_all = cnt["v"]
    dr0_all, dr1_all = cnt["dr0"], cnt["dr1"]

    pbc2_free = list(av_free[1]) + list(pbc_free[1])
    for c8 in range(NC8):
        for tg in range(NTG):
            emit("tensor", lambda te, c8=c8, tg=tg: te.matmul(
                p_b[:, :], selc[:, c8 * 128:(c8 + 1) * 128],
                recip16[:, tg * 512:(tg + 1) * 512], start=True, stop=True),
                waits=[("v", v_recip_all), ("dr0", dr0_all), ("dr1", dr1_all)]
                      + list(pbc2_free), incs=[("t", 1)])
            t_bc = cnt["t"]
            emit("vector", lambda v, c8=c8, tg=tg: v.tensor_tensor(
                o_heads[:, c8 * T + tg * 512: c8 * T + tg * 512 + 512],
                o_heads[:, c8 * T + tg * 512: c8 * T + tg * 512 + 512],
                p_b[:, :], ALU.mult),
                waits=[("t", t_bc)], incs=[("v", 1)])
            pbc2_free = [("v", cnt["v"])]
    v_onorm = cnt["v"]

    st2.close()
    st_qkv.close()
    st_acc = ExitStack()
    acc = sb("acc", [128, NC8 * T], F32, st_acc, side="right")

    pa_free2 = [av_free[0][0]]
    pbc_free2 = {0: ("v", v_onorm), 1: ("v", v_onorm)}
    for tg in range(NTG):
        for c in range(NC8):
            emit("tensor", lambda te, c=c, tg=tg: te.matmul(
                p_a[0:HARM, :], basT["o"][:, c * HARM:(c + 1) * HARM],
                o_heads[:, c * T + tg * 512: c * T + tg * 512 + 512],
                start=(c == 0), stop=(c == NC8 - 1)),
                waits=([("v", v_onorm), pa_free2[0], av_free[0][-1]] if c == 0 else []),
                incs=([("t", 1)] if c == NC8 - 1 else []))
        t_ro = cnt["t"]
        emit("scalar", lambda s, tg=tg: s.activation(
            res_bf[:, tg * 512: tg * 512 + 512], p_a[0:HARM, :], AF.Copy),
            waits=[("t", t_ro)], incs=[("s", 1)])
        pa_free2[0] = ("s", cnt["s"])
    s_reso = cnt["s"]
    for c8 in range(NC8):
        for tg in range(NTG):
            slot = (c8 * NTG + tg) % 2
            pbank = p_b if slot == 0 else p_c
            emit("tensor", lambda te, c8=c8, tg=tg, pb=pbank: te.matmul(
                pb[:, :], wT["o"][:, c8 * 128:(c8 + 1) * 128],
                res_bf[:, tg * 512: tg * 512 + 512], start=True, stop=True),
                waits=[("s", s_reso), pbc_free2[slot]], incs=[("t", 1)])
            t_ao = cnt["t"]
            emit("vector", lambda v, c8=c8, tg=tg, pb=pbank: v.tensor_copy(
                acc[:, c8 * T + tg * 512: c8 * T + tg * 512 + 512], pb[:, :]),
                waits=[("t", t_ao)], incs=[("v", 1)])
            pbc_free2[slot] = ("v", cnt["v"])
    v_acc0 = cnt["v"]

    st2o.close()
    stx.close()
    st2p.close()
    stw.close()

    st3 = ExitStack()
    wgbuf = [sb(f"wgbuf{b}", [128, GSZ * 1024], BF16, st3) for b in range(2)]
    wubuf = [sb(f"wubuf{b}", [128, GSZ * 1024], BF16, st3) for b in range(2)]
    wdbuf = [sb(f"wdbuf{b}", [128, GSZ * 1024], BF16, st3) for b in range(2)]
    gbuf = [sb(f"gbuf{b}", [128, GSZ * T], BF16, st3) for b in range(2)]
    c_cur = sb("c_cur", [128, T], F32, st3)
    sgt = [sb(f"sgt{b}", [128, 512], BF16, st3) for b in range(2)]

    st3p = ExitStack()
    p_xg = pt("p_xg", [128, 512], F32, st3p)
    p_xu = pt("p_xu", [128, 512], F32, st3p)
    p_d0 = pt("p_d0", [128, 512], F32, st3p)
    p_d1 = pt("p_d1", [128, 512], F32, st3p)
    p_cc = pt("p_cc", [128, 512], F32, st3p)

    groups = [list(range(g, min(g + GSZ, NMOE))) for g in range(0, NMOE, GSZ)]
    groups += [list(range(g, min(g + GSZ, NHID))) for g in range(NMOE, NHID, GSZ)]

    def chunk_srcs(h):
        if h < NMOE:
            e, fi = h // NFM, h % NFM
            return (w_gate_h[e, :, fi * 128:(fi + 1) * 128],
                    w_up_h[e, :, fi * 128:(fi + 1) * 128],
                    w_down_h[e, fi * 128:(fi + 1) * 128, :])
        fi = h - NMOE
        return (sw_gate_h[:, fi * 128:(fi + 1) * 128],
                sw_up_h[:, fi * 128:(fi + 1) * 128],
                sw_down_h[fi * 128:(fi + 1) * 128, :])

    def emit_group_dma(gi, buf_free_wait):
        buf = gi % 2
        sem = "dwa" if buf == 0 else "dwb"
        first = True
        for loc, h in enumerate(groups[gi]):
            sg, su, sd = chunk_srcs(h)
            for (dst, src) in ((wgbuf[buf], sg), (wubuf[buf], su)):
                emit("gpsimd", lambda g, dst=dst, loc=loc, src=src: g.dma_start(
                    dst.ap()[:, loc * 1024:(loc + 1) * 1024].rearrange(
                        "p (c f) -> p c f", c=NC8),
                    src.rearrange("(c p) f -> p c f", p=128)),
                    waits=([buf_free_wait] if first else []),
                    incs=[(sem, 16)])
                first = False
            emit("gpsimd", lambda g, buf=buf, loc=loc, sd=sd: g.dma_start(
                wdbuf[buf][:, loc * 1024:(loc + 1) * 1024], sd),
                incs=[(sem, 16)])
        return (sem, cnt[sem])

    dma_done = [emit_group_dma(0, ("v", v_acc0)), emit_group_dma(1, ("v", v_acc0))]
    grp_last_t = [0] * len(groups)

    pxg_free = ("v", v_acc0)
    pxu_free = ("v", v_acc0)
    pd_free = {0: ("v", v_acc0), 1: ("v", v_acc0)}
    pcc_free = ("v", v_acc0)
    sgt_free = [("v", 0), ("v", 0)]
    gbuf_free = [("t", 0), ("t", 0)]
    cur_e = -1
    su_slot = 0
    add_done = {}
    v_g_tg = [0] * NTG

    for gi, grp in enumerate(groups):
        buf = gi % 2
        dww = dma_done[gi]
        for loc, h in enumerate(grp):
            is_moe = h < NMOE
            if is_moe and (h // NFM) != cur_e:
                cur_e = h // NFM
                for tg in range(NTG):
                    emit("tensor", lambda te, e=cur_e, tg=tg: te.matmul(
                        p_cc[:, :], sele[:, e * 128:(e + 1) * 128],
                        comb_bf[0:E, tg * 512:(tg + 1) * 512],
                        start=True, stop=True),
                        waits=[("v", v_p1_done), pcc_free], incs=[("t", 1)])
                    t_cc = cnt["t"]
                    emit("vector", lambda v, tg=tg: v.tensor_copy(
                        c_cur[:, tg * 512:(tg + 1) * 512], p_cc[:, :]),
                        waits=[("t", t_cc)], incs=[("v", 1)])
                    pcc_free = ("v", cnt["v"])
            v_ccur = cnt["v"]
            for tg in range(NTG):
                for c in range(NC8):
                    emit("tensor", lambda te, buf=buf, loc=loc, c=c, tg=tg: te.matmul(
                        p_xg[:, :],
                        wgbuf[buf][:, loc * 1024 + c * 128: loc * 1024 + c * 128 + 128],
                        xnh_fm[:, c * T + tg * 512: c * T + tg * 512 + 512],
                        start=(c == 0), stop=(c == NC8 - 1)),
                        waits=([dww, pxg_free] if c == 0 else []),
                        incs=([("t", 1)] if c == NC8 - 1 else []))
                t_xg = cnt["t"]
                for c in range(NC8):
                    emit("tensor", lambda te, buf=buf, loc=loc, c=c, tg=tg: te.matmul(
                        p_xu[:, :],
                        wubuf[buf][:, loc * 1024 + c * 128: loc * 1024 + c * 128 + 128],
                        xnh_fm[:, c * T + tg * 512: c * T + tg * 512 + 512],
                        start=(c == 0), stop=(c == NC8 - 1)),
                        waits=([pxu_free] if c == 0 else []),
                        incs=([("t", 1)] if c == NC8 - 1 else []))
                t_xu = cnt["t"]
                ss = su_slot % 2
                su_slot += 1
                emit("scalar", lambda s, ss=ss: s.activation(sgt[ss][:], p_xg[:, :], AF.Silu),
                     waits=[("t", t_xg), sgt_free[ss]], incs=[("s", 1)])
                pxg_free = ("s", cnt["s"])
                s_silu = cnt["s"]
                gslice = gbuf[buf][:, loc * T + tg * 512: loc * T + tg * 512 + 512]
                emit("vector", lambda v, gs=gslice, ss=ss: v.tensor_tensor(
                    gs, sgt[ss][:], p_xu[:, :], ALU.mult),
                    waits=[("t", t_xu), ("s", s_silu)] +
                          ([gbuf_free[buf]] if (loc == 0 and tg == 0) else []),
                    incs=[("v", 1)])
                pxu_free = ("v", cnt["v"])
                sgt_free[ss] = ("v", cnt["v"])
                if is_moe:
                    emit("vector", lambda v, gs=gslice, tg=tg: v.tensor_tensor(
                        gs, gs, c_cur[:, tg * 512:(tg + 1) * 512], ALU.mult),
                        waits=[("v", v_ccur)], incs=[("v", 1)])
                v_g_tg[tg] = cnt["v"]
        for tg in range(NTG):
            for dc in range(NC8):
                slot = (tg * NC8 + dc) % 2
                pbank = p_d0 if slot == 0 else p_d1
                for li in range(len(grp)):
                    emit("tensor", lambda te, buf=buf, li=li, dc=dc, tg=tg, pb=pbank,
                         first=(li == 0), last=(li == len(grp) - 1): te.matmul(
                             pb[:, :],
                             wdbuf[buf][:, li * 1024 + dc * 128: li * 1024 + dc * 128 + 128],
                             gbuf[buf][:, li * T + tg * 512: li * T + tg * 512 + 512],
                             start=first, stop=last),
                        waits=([("v", v_g_tg[tg]), pd_free[slot]] if li == 0 else []),
                        incs=([("t", 1)] if li == len(grp) - 1 else []))
                t_dn = cnt["t"]
                emit("vector", lambda v, dc=dc, tg=tg, pb=pbank: v.tensor_tensor(
                    acc[:, dc * T + tg * 512: dc * T + tg * 512 + 512],
                    acc[:, dc * T + tg * 512: dc * T + tg * 512 + 512],
                    pb[:, :], ALU.add),
                    waits=[("t", t_dn)], incs=[("v", 1)])
                if gi == len(groups) - 1:
                    add_done[(tg, dc)] = cnt["v"]
                pd_free[slot] = ("v", cnt["v"])
        grp_last_t[gi] = cnt["t"]
        gbuf_free[buf] = ("t", cnt["t"])
        if gi + 2 < len(groups):
            dma_done.append(emit_group_dma(gi + 2, ("t", grp_last_t[gi])))
    v_moe_done = cnt["v"]

    st3p.close()
    st3.close()

    st4 = ExitStack()
    x4 = [sb(f"x4_{b}", [128, D], F32, st4, side="right") for b in range(2)]
    o4 = [sb(f"o4_{b}", [128, D], F32, st4, side="right") for b in range(2)]
    st4p = ExitStack()
    p_o = [pt(f"p_o{b}", [128, 512], F32, st4p) for b in range(2)]

    x4_free = [("v", 0)] * 2
    o4_free = [("doa", 0), ("dob", 0)]
    po_free = {0: ("v", v_moe_done), 1: ("v", v_moe_done)}
    dx4 = []
    X4SEMS = ("dxa", "dxb")
    for i in range(2):
        sem = X4SEMS[i % 2]
        emit("sync", lambda sy, i=i: sy.dma_start(x4[i][:, :], x_h[i * 128:(i + 1) * 128, :]),
             waits=[(sem, cnt[sem])], incs=[(sem, 16)])
        dx4.append((sem, cnt[sem]))
    for i in range(NT):
        b = i % 2
        ob = i % 2
        for c8 in range(NC8):
            slot = (i * NC8 + c8) % 2
            emit("tensor", lambda te, c8=c8, i=i, slot=slot: te.transpose(
                p_o[slot][:, 0:128], acc[:, c8 * T + i * 128: c8 * T + i * 128 + 128],
                ident_f[:]),
                waits=[("v", add_done[(i // 4, c8)]), po_free[slot]], incs=[("t", 1)])
            t_tr = cnt["t"]
            emit("vector", lambda v, c8=c8, b=b, ob=ob, slot=slot: v.tensor_tensor(
                o4[ob][:, c8 * 128:(c8 + 1) * 128], x4[b][:, c8 * 128:(c8 + 1) * 128],
                p_o[slot][:, 0:128], ALU.add),
                waits=[("t", t_tr), dx4[i], o4_free[ob]], incs=[("v", 1)])
            po_free[slot] = ("v", cnt["v"])
        v_out = cnt["v"]
        x4_free[b] = ("v", v_out)
        if i + 2 < NT:
            sem = X4SEMS[(i + 2) % 2]
            emit("sync", lambda sy, i=i, b=b: sy.dma_start(
                x4[b][:, :], x_h[(i + 2) * 128:(i + 3) * 128, :]),
                waits=[x4_free[b], (sem, cnt[sem])], incs=[(sem, 16)])
            dx4.append((sem, cnt[sem]))
        dosem = "doa" if ob == 0 else "dob"
        emit("sync", lambda sy, i=i, ob=ob: sy.dma_start(
            out_h[i * 128:(i + 1) * 128, :], o4[ob][:, :]),
            waits=[("v", v_out)], incs=[(dosem, 16)])
        o4_free[ob] = (dosem, cnt[dosem])
    doa_total, dob_total = cnt["doa"], cnt["dob"]

    with ExitStack() as st_sem:
        sems = {nm: st_sem.enter_context(nc.semaphore(nm)) for nm in cnt}
        with nc.Block() as block:
            def runner(engname):
                ss = {"vector": "v", "scalar": "s"}.get(engname)
                def run(e):
                    for k, (fn, waits, incs) in enumerate(ops[engname]):
                        if ss is not None and k > 0:
                            e.wait_ge(sems[ss], k)
                        for sname, val in waits:
                            e.wait_ge(sems[sname], val)
                        inst = fn(e)
                        for sname, amt in incs:
                            inst.then_inc(sems[sname], amt)
                    if engname == "sync":
                        e.wait_ge(sems["doa"], doa_total)
                        e.wait_ge(sems["dob"], dob_total)
                return run
            block.tensor(runner("tensor"))
            block.vector(runner("vector"))
            block.scalar(runner("scalar"))
            block.gpsimd(runner("gpsimd"))
            block.sync(runner("sync"))

    st4p.close()
    st4.close()
    st_acc.close()
    so.close()
    return nc


_NC_CACHE = None


def kernel(**inputs):
    global _NC_CACHE
    if _NC_CACHE is None:
        _NC_CACHE = build_nc()
    nc = _NC_CACHE
    x = np.ascontiguousarray(np.asarray(inputs["L0_attn_x"], dtype=np.float32))
    common = {
        "gate_w": np.ascontiguousarray(inputs["L0_ffn_gate_w"], dtype=np.float32),
        "w_gate": np.ascontiguousarray(inputs["L0_ffn_w_gate"], dtype=np.float32),
        "w_up": np.ascontiguousarray(inputs["L0_ffn_w_up"], dtype=np.float32),
        "w_down": np.ascontiguousarray(inputs["L0_ffn_w_down"], dtype=np.float32),
        "sw_gate": np.ascontiguousarray(inputs["L0_ffn_sw_gate"], dtype=np.float32),
        "sw_up": np.ascontiguousarray(inputs["L0_ffn_sw_up"], dtype=np.float32),
        "sw_down": np.ascontiguousarray(inputs["L0_ffn_sw_down"], dtype=np.float32),
    }
    for p in "qkvo":
        common[f"basis_{p}"] = np.ascontiguousarray(inputs[f"L0_attn_basis_{p}"], dtype=np.float32)
        common[f"phase_{p}"] = np.ascontiguousarray(inputs[f"L0_attn_phase_{p}"], dtype=np.float32)
        common[f"amp_{p}"] = np.ascontiguousarray(inputs[f"L0_attn_amp_{p}"], dtype=np.float32)
    in_maps = [dict(common, x=np.ascontiguousarray(x[i])) for i in range(8)]
    res = run_bass_kernel_spmd(nc, in_maps, core_ids=list(range(8)))
    return np.stack([r["out"] for r in res.results], axis=0)


if __name__ == "__main__":
    build_nc()
    print("built ok")


# version 14
# speedup vs baseline: 1.0594x; 1.0300x over previous
import numpy as np
from contextlib import ExitStack

import concourse.bass as bass
import concourse.mybir as mybir
from concourse.bass_utils import run_bass_kernel_spmd

F32 = mybir.dt.float32
BF16 = mybir.dt.bfloat16
I32 = mybir.dt.int32
AF = mybir.ActivationFunctionType
ALU = mybir.AluOpType
AX = mybir.AxisListType.X

T = 2048
D = 1024
NC8 = 8
NT = 16
NTG = 4
NHEAD = 16
HARM = 32
E = 8
NFM = 11
NMOE = E * NFM
NFS = 22
NHID = NMOE + NFS
GSZ = 4
BIG = 100.0
EPS_LN = 1e-5
TWO_PI = 6.283185307179586
HALF_PI = 1.5707963267948966
INV_2PI = 1.0 / TWO_PI
PI_LO = 3.1415925

ENGS = ("tensor", "vector", "scalar", "gpsimd", "sync")


def build_nc():
    nc = bass.Bass("TRN2", target_bir_lowering=False)

    x_h = nc.dram_tensor("x", [T, D], F32, kind="ExternalInput")
    basis = {p: nc.dram_tensor(f"basis_{p}", [HARM, D], F32, kind="ExternalInput")
             for p in "qkvo"}
    phase = {p: nc.dram_tensor(f"phase_{p}", [D, HARM], F32, kind="ExternalInput")
             for p in "qkvo"}
    amp = {p: nc.dram_tensor(f"amp_{p}", [D, HARM], F32, kind="ExternalInput")
           for p in "qkvo"}
    gate_w_h = nc.dram_tensor("gate_w", [E, D], F32, kind="ExternalInput")
    w_gate_h = nc.dram_tensor("w_gate", [E, D, 1408], F32, kind="ExternalInput")
    w_up_h = nc.dram_tensor("w_up", [E, D, 1408], F32, kind="ExternalInput")
    w_down_h = nc.dram_tensor("w_down", [E, 1408, D], F32, kind="ExternalInput")
    sw_gate_h = nc.dram_tensor("sw_gate", [D, 2816], F32, kind="ExternalInput")
    sw_up_h = nc.dram_tensor("sw_up", [D, 2816], F32, kind="ExternalInput")
    sw_down_h = nc.dram_tensor("sw_down", [2816, D], F32, kind="ExternalInput")
    out_h = nc.dram_tensor("out", [T, D], F32, kind="ExternalOutput")

    ops = {e: [] for e in ENGS}
    cnt = {s: 0 for s in ("t", "v", "s", "g", "dp", "dwa", "dwb",
                          "dxa", "dxb", "doa", "dob", "dr0", "dr1")}

    SELF_SEM = {"vector": "v", "scalar": "s"}

    def emit(eng, fn, waits=(), incs=()):
        for sname, val in waits:
            assert val <= cnt[sname], f"wait-ahead: {sname} {val} > {cnt[sname]}"
        incs = list(incs)
        ss = SELF_SEM.get(eng)
        if ss is not None and not any(n == ss for n, _ in incs):
            incs.append((ss, 1))
        rec = []
        for sname, amt in incs:
            cnt[sname] += amt
            rec.append((sname, amt))
        ops[eng].append((fn, tuple(waits), tuple(rec)))

    def sb(name, shape, dtype, stack, side=None):
        if side is None:
            return stack.enter_context(nc.sbuf_tensor(name, shape, dtype))
        return stack.enter_context(nc.sbuf_tensor(name, shape, dtype, side=side))

    def pt(name, shape, dtype, stack):
        return stack.enter_context(nc.psum_tensor(name, shape, dtype))

    so = ExitStack()
    ident_bf = sb("ident_bf", [128, 128], BF16, so)
    ident_f = sb("ident_f", [128, 128], F32, so)
    st_iota = ExitStack()
    iota2d = sb("iota2d", [128, 128], F32, st_iota, side="right")
    onesrow = sb("onesrow", [1, 128], BF16, so)
    epsln = sb("epsln", [128, 1], F32, so)
    selc = sb("selc", [16, NC8 * 128], BF16, so)
    sele = sb("sele", [E, E * 128], BF16, so)
    xnh_fm = sb("xnh_fm", [128, NC8 * T], BF16, so)
    comb_bf = sb("comb_bf", [E, T], BF16, so)
    bias128 = sb("bias128", [128, 1], F32, so)
    stw = ExitStack()
    basT = {p: sb(f"basT_{p}", [128, NC8 * HARM], BF16, stw) for p in "qkvo"}
    wT = {p: sb(f"wT_{p}", [HARM, D], BF16, stw) for p in "qkvo"}
    gwTh = sb("gwTh", [128, NC8 * 8], BF16, stw)
    gwTl = sb("gwTl", [128, NC8 * 8], BF16, stw)

    emit("gpsimd", lambda g: g.iota(iota2d[:, :], [[1, 128]], channel_multiplier=-1,
                                    allow_small_or_imprecise_dtypes=True),
         incs=[("g", 1)])
    g_iota = cnt["g"]
    emit("vector", lambda v: v.tensor_single_scalar(ident_bf[:], iota2d[:], 0.0, ALU.is_equal),
         waits=[("g", g_iota)], incs=[("v", 1)])
    emit("vector", lambda v: v.tensor_single_scalar(ident_f[:], iota2d[:], 0.0, ALU.is_equal),
         incs=[("v", 1)])
    emit("vector", lambda v: v.memset(onesrow[:], 1.0), incs=[("v", 1)])
    emit("vector", lambda v: v.memset(epsln[:], EPS_LN), incs=[("v", 1)])
    v_const = cnt["v"]
    st_iota.close()

    stp = ExitStack()
    bas_bf = {p: sb(f"bas_bf_{p}", [HARM, D], BF16, stp, side="right") for p in "qkvo"}
    ph_f = {p: sb(f"ph_f_{p}", [128, NC8 * HARM], F32, stp, side="right") for p in "qkvo"}
    am_f = {p: sb(f"am_f_{p}", [128, NC8 * HARM], F32, stp, side="right") for p in "qkvo"}
    gw_f = sb("gw_f", [E, D], F32, stp, side="right")
    wfull = {p: sb(f"wfull_{p}", [128, NC8 * HARM], BF16, stp, side="right") for p in "qkvo"}
    cw_t = sb("cw_t", [128, NC8 * HARM], F32, stp, side="right")
    cw_k = sb("cw_k", [128, NC8 * HARM], I32, stp, side="right")
    cw_f = sb("cw_f", [128, NC8 * HARM], F32, stp, side="right")
    gwh = sb("gwh", [E, D], BF16, stp, side="right")
    gwl = sb("gwl", [E, D], BF16, stp, side="right")
    gwt = sb("gwt", [E, D], F32, stp, side="right")
    scr_sel = sb("scr_sel", [16, 1024], F32, stp, side="right")

    emit("gpsimd", lambda g: g.iota(
        scr_sel.ap().rearrange("p (g j) -> p g j", g=16),
        [[1, 16], [0, 64]], channel_multiplier=-1,
        allow_small_or_imprecise_dtypes=True), incs=[("g", 1)])
    g_sel = cnt["g"]
    emit("vector", lambda v: v.tensor_single_scalar(selc[:, :], scr_sel[:, :],
                                                    0.0, ALU.is_equal),
         waits=[("g", g_sel)], incs=[("v", 1)])
    v_sel1 = cnt["v"]
    emit("gpsimd", lambda g: g.iota(
        scr_sel[0:8, :].rearrange("p (f j) -> p f j", f=8),
        [[1, 8], [0, 128]], channel_multiplier=-1,
        allow_small_or_imprecise_dtypes=True),
        waits=[("v", v_sel1)], incs=[("g", 1)])
    g_sel2 = cnt["g"]
    emit("vector", lambda v: v.tensor_single_scalar(sele[:, :], scr_sel[0:8, :],
                                                    0.0, ALU.is_equal),
         waits=[("g", g_sel2)], incs=[("v", 1)])

    for p in "qkvo":
        emit("gpsimd", lambda g, p=p: g.dma_start(bas_bf[p][:], basis[p][:, :]),
             incs=[("dp", 16)])
        emit("gpsimd", lambda g, p=p: g.dma_start(
            ph_f[p].ap().rearrange("p (c h) -> p c h", c=NC8),
            phase[p].ap().rearrange("(c p) h -> p c h", p=128)),
            incs=[("dp", 16)])
        emit("gpsimd", lambda g, p=p: g.dma_start(
            am_f[p].ap().rearrange("p (c h) -> p c h", c=NC8),
            amp[p].ap().rearrange("(c p) h -> p c h", p=128)),
            incs=[("dp", 16)])
    emit("gpsimd", lambda g: g.dma_start(gw_f[:], gate_w_h[:, :]), incs=[("dp", 16)])
    dp_params = cnt["dp"]

    for p in "qkvo":
        P = ph_f[p]
        emit("vector", lambda v, P=P: v.tensor_scalar(
            cw_t[:], P[:], HALF_PI, INV_2PI, ALU.add, ALU.mult),
            waits=[("dp", dp_params)], incs=[("v", 1)])
        emit("vector", lambda v: v.tensor_copy(cw_k[:], cw_t[:]), incs=[("v", 1)])
        emit("vector", lambda v: v.tensor_copy(cw_f[:], cw_k[:]), incs=[("v", 1)])
        emit("vector", lambda v, P=P: v.tensor_single_scalar(cw_t[:], P[:], HALF_PI, ALU.add),
             incs=[("v", 1)])
        emit("vector", lambda v: v.scalar_tensor_tensor(
            cw_t[:], cw_f[:], -TWO_PI, cw_t[:], ALU.mult, ALU.add), incs=[("v", 1)])
        emit("vector", lambda v: v.tensor_single_scalar(cw_f[:], cw_t[:], PI_LO, ALU.is_gt),
             incs=[("v", 1)])
        emit("vector", lambda v: v.scalar_tensor_tensor(
            cw_t[:], cw_f[:], -TWO_PI, cw_t[:], ALU.mult, ALU.add), incs=[("v", 1)])
        emit("vector", lambda v: v.tensor_single_scalar(cw_f[:], cw_t[:], -PI_LO, ALU.is_lt),
             incs=[("v", 1)])
        emit("vector", lambda v: v.scalar_tensor_tensor(
            cw_t[:], cw_f[:], TWO_PI, cw_t[:], ALU.mult, ALU.add), incs=[("v", 1)])
        emit("vector", lambda v: v.tensor_single_scalar(cw_t[:], cw_t[:], PI_LO, ALU.min),
             incs=[("v", 1)])
        emit("vector", lambda v: v.tensor_single_scalar(cw_t[:], cw_t[:], -PI_LO, ALU.max),
             incs=[("v", 1)])
        v_red = cnt["v"]
        emit("scalar", lambda s: s.activation(cw_f[:], cw_t[:], AF.Sin, scale=-1.0),
             waits=[("v", v_red)], incs=[("s", 1)])
        s_sin = cnt["s"]
        emit("vector", lambda v, p=p: v.tensor_tensor(wfull[p][:], cw_f[:], am_f[p][:], ALU.mult),
             waits=[("s", s_sin)], incs=[("v", 1)])
    emit("vector", lambda v: v.tensor_copy(gwh[:], gw_f[:]),
         waits=[("dp", dp_params)], incs=[("v", 1)])
    emit("vector", lambda v: v.tensor_tensor(gwt[:], gw_f[:], gwh[:], ALU.subtract),
         incs=[("v", 1)])
    emit("vector", lambda v: v.tensor_copy(gwl[:], gwt[:]), incs=[("v", 1)])
    v_params = cnt["v"]

    stpp = ExitStack()
    p_tr = [pt(f"p_tr{b}", [128, 1024], BF16, stpp) for b in range(2)]
    tr_free = {0: ("v", 0), 1: ("v", 0)}

    def bf_transpose(src_ap, dst_ap, idn):
        slot = bf_transpose.flip
        bf_transpose.flip ^= 1
        ps = p_tr[slot]
        emit("tensor", lambda te, s=src_ap, i=idn, ps=ps: te.transpose(
            ps[0:s.shape[1], 0:s.shape[0]], s, i),
            waits=[tr_free[slot], ("v", v_params)], incs=[("t", 1)])
        t_now = cnt["t"]
        emit("vector", lambda v, d=dst_ap, ps=ps, s=src_ap: v.tensor_copy(
            d, ps[0:s.shape[1], 0:s.shape[0]]),
            waits=[("t", t_now)], incs=[("v", 1)])
        tr_free[slot] = ("v", cnt["v"])
    bf_transpose.flip = 0

    for p in "qkvo":
        for c in range(NC8):
            bf_transpose(bas_bf[p][:, c * 128:(c + 1) * 128],
                         basT[p][:, c * HARM:(c + 1) * HARM],
                         ident_bf[0:HARM, 0:HARM])
        for c in range(NC8):
            bf_transpose(wfull[p][:, c * HARM:(c + 1) * HARM],
                         wT[p][:, c * 128:(c + 1) * 128],
                         ident_bf[:, :])
    for c in range(NC8):
        bf_transpose(gwh[:, c * 128:(c + 1) * 128], gwTh[:, c * 8:(c + 1) * 8],
                     ident_bf[0:E, 0:E])
        bf_transpose(gwl[:, c * 128:(c + 1) * 128], gwTl[:, c * 8:(c + 1) * 8],
                     ident_bf[0:E, 0:E])
    v_prologue = cnt["v"]
    stpp.close()
    stp.close()

    st_qkv = ExitStack()
    q_fm = sb("q_fm", [128, NC8 * T], BF16, st_qkv, side="right")
    k_fm = sb("k_fm", [128, NC8 * T], BF16, st_qkv, side="right")
    v_aug = sb("v_aug", [128, NT * NHEAD * 65], BF16, st_qkv, side="right")
    st1 = ExitStack()
    res_sm = {p: sb(f"res_sm_{p}", [HARM, 512], BF16, st1) for p in "qkv"}
    x_tm = [sb(f"x_tm{b}", [128, D], F32, st1) for b in range(2)]
    xn_tm = sb("xn_tm", [128, D], F32, st1)
    xnh_tm = [sb(f"xnh_tm{b}", [128, D], BF16, st1) for b in range(2)]
    xnl_tm = [sb(f"xnl_tm{b}", [128, D], BF16, st1) for b in range(2)]
    sqd = sb("sqd", [128, D], BF16, st1)
    xnl_t = sb("xnl_t", [128, D], BF16, st1)
    m_t = sb("m_t", [128, 1], F32, st1)
    ss_t = sb("ss_t", [128, 1], F32, st1)
    rstd_t = sb("rstd_t", [128, 1], F32, st1)
    lg_all = sb("lg_all", [128, 128], F32, st1)
    s_all = sb("s_all", [128, 128], F32, st1)
    eq_all = sb("eq_all", [128, 128], F32, st1)
    tmp_all = eq_all
    mk1 = sb("mk1", [128, 128], F32, st1)
    mk2 = sb("mk2", [128, 128], F32, st1)
    idxB_all = sb("idxB_all", [128, 128], F32, st1)
    comb_all = lg_all
    m16 = sb("m16", [128, 16], F32, st1)
    m2_16 = sb("m2_16", [128, 16], F32, st1)
    z16 = sb("z16", [128, 16], F32, st1)
    a16 = z16
    rw16 = sb("rw16", [128, 16], F32, st1)

    def v3(th):
        return th.ap().rearrange("p (i e) -> p i e", e=8)

    def b3(th16):
        return bass.AP(th16, 0, [[16, 128], [1, 16], [0, 8]])

    emit("gpsimd", lambda g: g.iota(v3(idxB_all), [[0, 16], [1, 8]], base=int(BIG),
                                    channel_multiplier=0,
                                    allow_small_or_imprecise_dtypes=True),
         incs=[("g", 1)])
    g_idx = cnt["g"]
    v_idx_all = 0

    st1p = ExitStack()
    p_t1 = [pt(f"p_t1{b}", [128, 1024], BF16, st1p) for b in range(2)]
    p_g8 = pt("p_g8", [128, 512], F32, st1p)
    p_cb = pt("p_cb", [128, 512], F32, st1p)
    p_pj = [pt(f"p_pj{b}", [128, 512], F32, st1p) for b in range(4)]

    xt_free = [("v", 0), ("v", 0)]
    t1_free = {0: [("v", v_prologue)], 1: [("v", v_prologue)]}
    hilo_free = [("t", 0), ("t", 0)]
    pg_free = ("s", 0)
    pcb_free = ("v", 0)
    pj_free = [("s", 0), ("s", 0), ("s", 0), ("s", 0)]
    rsm_free = {p: ("t", 0) for p in "qkv"}
    pjflip = [0]

    def emit_proj_tg(tg, s_fm_ready):
        for pi, p in enumerate("qkv"):
            slot = pi % 2
            for c in range(NC8):
                emit("tensor", lambda te, p=p, c=c, tg=tg, pb=p_pj[slot]: te.matmul(
                    pb[0:HARM, :], basT[p][:, c * HARM:(c + 1) * HARM],
                    xnh_fm[:, c * T + tg * 512: c * T + tg * 512 + 512],
                    start=(c == 0), stop=(c == NC8 - 1)),
                    waits=([("s", s_fm_ready), pj_free[slot], rsm_free[p]]
                           if c == 0 else []),
                    incs=([("t", 1)] if c == NC8 - 1 else []))
            t_res = cnt["t"]
            emit("scalar", lambda sc_, p=p, pb=p_pj[slot]: sc_.activation(
                res_sm[p][:, :], pb[0:HARM, :], AF.Copy),
                waits=[("t", t_res)], incs=[("s", 1)])
            pj_free[slot] = ("s", cnt["s"])
        s_res_all = cnt["s"]
        for p in "qk":
            for c8 in range(NC8):
                slot = pjflip[0]
                pjflip[0] = (pjflip[0] + 1) % 4
                emit("tensor", lambda te, p=p, c8=c8, pb=p_pj[slot]: te.matmul(
                    pb[:, :], wT[p][:, c8 * 128:(c8 + 1) * 128],
                    res_sm[p][:, :], start=True, stop=True),
                    waits=[("s", s_res_all), pj_free[slot]], incs=[("t", 1)])
                t_pr = cnt["t"]
                rsm_free[p] = ("t", t_pr)
                dst = q_fm if p == "q" else k_fm
                if slot % 2 == 0:
                    emit("scalar", lambda sc_, dst=dst, c8=c8, tg=tg, pb=p_pj[slot]:
                         sc_.activation(
                             dst[:, c8 * T + tg * 512: c8 * T + tg * 512 + 512],
                             pb[:, :], AF.Copy),
                         waits=[("t", t_pr)], incs=[("s", 1)])
                    pj_free[slot] = ("s", cnt["s"])
                else:
                    emit("vector", lambda v_, dst=dst, c8=c8, tg=tg, pb=p_pj[slot]:
                         v_.tensor_copy(
                             dst[:, c8 * T + tg * 512: c8 * T + tg * 512 + 512],
                             pb[:, :]),
                         waits=[("t", t_pr)], incs=[("v", 1)])
                    pj_free[slot] = ("v", cnt["v"])
        for tt in range(4):
            itok = tg * 4 + tt
            for dhalf in range(2):
                slot = pjflip[0]
                pjflip[0] = (pjflip[0] + 1) % 4
                emit("tensor", lambda te, tt=tt, dhalf=dhalf, pb=p_pj[slot]: te.matmul(
                    pb[:, :], res_sm["v"][:, tt * 128:(tt + 1) * 128],
                    wT["v"][:, dhalf * 512:(dhalf + 1) * 512],
                    start=True, stop=True),
                    waits=[("s", s_res_all), pj_free[slot]], incs=[("t", 1)])
                t_pr = cnt["t"]
                rsm_free["v"] = ("t", t_pr)
                vdst = bass.AP(v_aug, itok * NHEAD * 65 + dhalf * 8 * 65,
                               [[NHEAD * 65 * NT, 128], [65, 8], [1, 64]])
                psrc = (lambda pb=p_pj[slot]: bass.AP(
                    pb, 0, [[512, 128], [64, 8], [1, 64]]))()
                if slot % 2 == 0:
                    emit("scalar", lambda sc_, vdst=vdst, psrc=psrc:
                         sc_.activation(vdst, psrc, AF.Copy),
                         waits=[("t", t_pr)], incs=[("s", 1)])
                    pj_free[slot] = ("s", cnt["s"])
                else:
                    emit("vector", lambda v_, vdst=vdst, psrc=psrc:
                         v_.tensor_copy(vdst, psrc),
                         waits=[("t", t_pr)], incs=[("v", 1)])
                    pj_free[slot] = ("v", cnt["v"])
    dx_needed = []
    v_rstd_read = 0
    s_sq_prev = 0
    t_xn_read = 0

    for i in range(2):
        sem = "dxa" if i % 2 == 0 else "dxb"
        emit("sync", lambda sy, i=i: sy.dma_start(
            x_tm[i][:, :], x_h[i * 128:(i + 1) * 128, :]), incs=[(sem, 16)])
        dx_needed.append((sem, cnt[sem]))

    for i in range(NT):
        b = i % 2
        dxw = dx_needed[i]
        emit("vector", lambda v, b=b: v.reduce_sum(m_t[:], x_tm[b][:], axis=AX),
             waits=[dxw], incs=[("v", 1)])
        emit("vector", lambda v: v.tensor_single_scalar(m_t[:], m_t[:], 1.0 / D, ALU.mult),
             incs=[("v", 1)])
        emit("vector", lambda v, b=b: v.tensor_single_scalar(
            xn_tm[:], x_tm[b][:], m_t[:], ALU.subtract),
            waits=[("s", s_sq_prev)], incs=[("v", 1)])
        v_xc = cnt["v"]
        xt_free[b] = ("v", v_xc)
        if i + 2 < NT:
            sem = "dxa" if (i + 2) % 2 == 0 else "dxb"
            emit("sync", lambda sy, i=i, b=b: sy.dma_start(
                x_tm[b][:, :], x_h[(i + 2) * 128:(i + 3) * 128, :]),
                waits=[xt_free[b]], incs=[(sem, 16)])
            dx_needed.append((sem, cnt[sem]))
        emit("scalar", lambda s: s.activation(sqd[:], xn_tm[:], AF.Square,
                                              accum_out=ss_t[:]),
             waits=[("v", v_xc)], incs=[("s", 1)])
        s_sq_prev = cnt["s"]
        emit("scalar", lambda s: s.activation(rstd_t[:], ss_t[:], AF.Sqrt,
                                              scale=1.0 / D, bias=epsln[:]),
             waits=[("v", max(v_const, v_rstd_read))], incs=[("s", 1)])
        s_rs = cnt["s"]
        emit("vector", lambda v: v.reciprocal(rstd_t[:], rstd_t[:]),
             waits=[("s", s_rs)], incs=[("v", 1)])
        emit("vector", lambda v: v.tensor_single_scalar(xn_tm[:], xn_tm[:], rstd_t[:], ALU.mult),
             incs=[("v", 1)])
        v_rstd_read = cnt["v"]
        hb = i % 2
        emit("vector", lambda v, hb=hb: v.tensor_copy(xnh_tm[hb][:], xn_tm[:]),
             waits=[hilo_free[hb]], incs=[("v", 1)])
        emit("vector", lambda v, hb=hb: v.tensor_tensor(
            xnl_tm[hb][:], xn_tm[:], xnh_tm[hb][:], ALU.subtract), incs=[("v", 1)])
        v_hl = cnt["v"]
        for si, src in enumerate((xnh_tm[hb], xnl_tm[hb])):
            for c in range(NC8):
                slot = c % 2
                ps = p_t1[slot]
                emit("tensor", lambda te, src=src, c=c, ps=ps: te.transpose(
                    ps[:, 0:128], src[:, c * 128:(c + 1) * 128], ident_bf[:]),
                    waits=t1_free[slot] + [("v", v_hl)], incs=[("t", 1)])
                t_now = cnt["t"]
                if si == 0:
                    emit("scalar", lambda sc_, c=c, i=i, ps=ps: sc_.activation(
                        xnh_fm[:, c * T + i * 128: c * T + i * 128 + 128],
                        ps[:, 0:128], AF.Copy),
                        waits=[("t", t_now)], incs=[("s", 1)])
                    t1_free[slot] = [("s", cnt["s"])]
                else:
                    emit("vector", lambda v_, c=c, ps=ps: v_.tensor_copy(
                        xnl_t[:, c * 128:(c + 1) * 128], ps[:, 0:128]),
                        waits=[("t", t_now)], incs=[("v", 1)])
                    t1_free[slot] = [("v", cnt["v"])]
        hilo_free[hb] = ("t", cnt["t"])
        t_xn_read = cnt["t"]
        s_fm_tile = cnt["s"]
        v_fm_tile = cnt["v"]
        nmm = 0
        for (xsel, gsel) in (("h", gwTh), ("h", gwTl), ("l", gwTh)):
            for c in range(NC8):
                first, last = nmm == 0, nmm == 23
                if xsel == "h":
                    xap = xnh_fm[:, c * T + i * 128: c * T + i * 128 + 128]
                else:
                    xap = xnl_t[:, c * 128:(c + 1) * 128]
                emit("tensor", lambda te, xap=xap, gsel=gsel, c=c,
                     first=first, last=last: te.matmul(
                         p_g8[:, 0:8], xap, gsel[:, c * 8:(c + 1) * 8],
                         start=first, stop=last),
                     waits=([("v", v_fm_tile), ("s", s_fm_tile), pg_free] if first else []),
                     incs=([("t", 1)] if last else []))
                nmm += 1
        t_g8 = cnt["t"]
        emit("vector", lambda v, i=i: v.tensor_copy(lg_all[:, i * 8:(i + 1) * 8],
                                                    p_g8[:, 0:8]),
             waits=[("t", t_g8)], incs=[("v", 1)])
        pg_free = ("v", cnt["v"])
        if i % 4 == 3:
            emit_proj_tg(i // 4, s_fm_tile)

    emit("vector", lambda v: v.tensor_reduce(m16[:], v3(lg_all), AX, ALU.max),
         incs=[("v", 1)])
    emit("vector", lambda v: v.tensor_tensor(v3(s_all), v3(lg_all), b3(m16),
                                             ALU.subtract), incs=[("v", 1)])
    v_sub = cnt["v"]
    emit("scalar", lambda s: s.activation(s_all[:], s_all[:], AF.Exp),
         waits=[("v", v_sub)], incs=[("s", 1)])
    s_expall = cnt["s"]
    emit("vector", lambda v: v.tensor_reduce(z16[:], v3(s_all), AX, ALU.add),
         waits=[("s", s_expall)], incs=[("v", 1)])
    emit("vector", lambda v: v.reciprocal(z16[:], z16[:]), incs=[("v", 1)])
    emit("vector", lambda v: v.tensor_tensor(v3(s_all), v3(s_all), b3(z16), ALU.mult),
         incs=[("v", 1)])
    emit("vector", lambda v: v.tensor_reduce(m16[:], v3(s_all), AX, ALU.max),
         incs=[("v", 1)])
    emit("vector", lambda v: v.tensor_tensor(v3(eq_all), v3(s_all), b3(m16),
                                             ALU.is_equal), incs=[("v", 1)])
    emit("vector", lambda v: v.scalar_tensor_tensor(tmp_all[:], eq_all[:], -BIG,
                                                    idxB_all[:], ALU.mult, ALU.add),
         waits=[("g", g_idx)], incs=[("v", 1)])
    emit("vector", lambda v: v.tensor_reduce(a16[:], v3(tmp_all), AX, ALU.min),
         incs=[("v", 1)])
    emit("vector", lambda v: v.tensor_tensor(v3(mk1), v3(tmp_all), b3(a16),
                                             ALU.is_equal), incs=[("v", 1)])
    emit("vector", lambda v: v.tensor_tensor(tmp_all[:], s_all[:], mk1[:], ALU.mult),
         incs=[("v", 1)])
    emit("vector", lambda v: v.tensor_tensor(s_all[:], s_all[:], tmp_all[:],
                                             ALU.subtract), incs=[("v", 1)])
    emit("vector", lambda v: v.tensor_reduce(m2_16[:], v3(s_all), AX, ALU.max),
         incs=[("v", 1)])
    emit("vector", lambda v: v.tensor_tensor(v3(eq_all), v3(s_all), b3(m2_16),
                                             ALU.is_equal), incs=[("v", 1)])
    emit("vector", lambda v: v.scalar_tensor_tensor(tmp_all[:], eq_all[:], -BIG,
                                                    idxB_all[:], ALU.mult, ALU.add),
         incs=[("v", 1)])
    emit("vector", lambda v: v.tensor_reduce(a16[:], v3(tmp_all), AX, ALU.min),
         incs=[("v", 1)])
    emit("vector", lambda v: v.tensor_tensor(v3(mk2), v3(tmp_all), b3(a16),
                                             ALU.is_equal), incs=[("v", 1)])
    emit("vector", lambda v: v.tensor_tensor(rw16[:], m16[:], m2_16[:], ALU.add),
         incs=[("v", 1)])
    emit("vector", lambda v: v.tensor_single_scalar(rw16[:], rw16[:], 1e-20, ALU.add),
         incs=[("v", 1)])
    emit("vector", lambda v: v.reciprocal(rw16[:], rw16[:]), incs=[("v", 1)])
    emit("vector", lambda v: v.tensor_single_scalar(rw16[:], rw16[:], 2.5, ALU.mult),
         incs=[("v", 1)])
    emit("vector", lambda v: v.tensor_tensor(v3(mk1), v3(mk1), b3(m16), ALU.mult),
         incs=[("v", 1)])
    emit("vector", lambda v: v.tensor_tensor(v3(mk2), v3(mk2), b3(m2_16), ALU.mult),
         incs=[("v", 1)])
    emit("vector", lambda v: v.tensor_tensor(comb_all[:], mk1[:], mk2[:], ALU.add),
         incs=[("v", 1)])
    emit("vector", lambda v: v.tensor_tensor(v3(comb_all), v3(comb_all), b3(rw16),
                                             ALU.mult), incs=[("v", 1)])
    v_comb = cnt["v"]
    for i in range(NT):
        emit("tensor", lambda te, i=i: te.transpose(
            p_cb[0:8, 0:128], comb_all[:, i * 8:(i + 1) * 8], ident_f[:]),
            waits=[("v", v_comb), pcb_free], incs=[("t", 1)])
        t_cb = cnt["t"]
        emit("vector", lambda v, i=i: v.tensor_copy(comb_bf[:, i * 128:(i + 1) * 128],
                                                    p_cb[0:8, 0:128]),
             waits=[("t", t_cb)], incs=[("v", 1)])
        pcb_free = ("v", cnt["v"])

    v_p1_done = cnt["v"]
    s_p1_done = cnt["s"]
    st1p.close()
    st1.close()

    st2o = ExitStack()
    o_heads = sb("o_heads", [128, NC8 * T], BF16, st2o)
    recip16 = sb("recip16", [NHEAD, T], BF16, st2o)
    res_bf = sb("res_bf", [HARM, T], BF16, st2o)
    mrow = sb("mrow", [1, 272], F32, st2o)
    mval = sb("mval", [1, 1], BF16, st2o)
    rrow = sb("rrow", [1, 512], F32, st2o)
    rrow_bf = [sb(f"rrow_bf{b}", [1, 512], BF16, st2o) for b in range(2)]

    st2 = ExitStack()
    zrow = [sb(f"zrow{b}", [1, 512], BF16, st2o) for b in range(2)]
    pTs = sb("pTs", [128, 4096], BF16, st2)
    colmax = sb("colmax", [128, 1], F32, st2)

    st2p = ExitStack()
    p_a = pt("p_a", [128, 512], F32, st2p)
    p_b = pt("p_b", [128, 512], F32, st2p)
    p_c = pt("p_c", [128, 512], F32, st2p)

    pa_free = [("v", v_p1_done), ("s", s_p1_done)]
    pbc_free = {0: [("v", v_p1_done), ("s", s_p1_done)],
                1: [("v", v_p1_done), ("s", s_p1_done)]}

    emit("vector", lambda v: v.memset(
        v_aug.ap().rearrange("p (i h c) -> p i h c", i=NT, h=NHEAD)[:, :, :, 64:65], 1.0),
        incs=[("v", 1)])
    v_vaug_ones = cnt["v"]
    stx = ExitStack()
    p_st = pt("p_st", [128, 2048], F32, stx)

    s_qk_done = s_p1_done
    emit("vector", lambda v: v.tensor_reduce(colmax[:], q_fm[:], AX, ALU.max,
                                             apply_absolute_value=True),
         waits=[("s", s_qk_done)], incs=[("v", 1)])
    v_cm = cnt["v"]
    emit("tensor", lambda te: te.transpose(p_a[0:1, 0:128], colmax[:], ident_f[:]),
         waits=[("v", v_cm), pa_free[0]], incs=[("t", 1)])
    t_cm = cnt["t"]
    emit("vector", lambda v: v.tensor_copy(mrow[:, 0:128], p_a[0:1, 0:128]),
         waits=[("t", t_cm)], incs=[("v", 1)])
    emit("vector", lambda v: v.reduce_max(mrow[:, 256:257], mrow[:, 0:128], axis=AX),
         incs=[("v", 1)])
    emit("vector", lambda v: v.tensor_reduce(colmax[:], k_fm[:], AX, ALU.max,
                                             apply_absolute_value=True), incs=[("v", 1)])
    v_cm2 = cnt["v"]
    emit("tensor", lambda te: te.transpose(p_a[0:1, 0:128], colmax[:], ident_f[:]),
         waits=[("v", v_cm2)], incs=[("t", 1)])
    t_cm2 = cnt["t"]
    emit("vector", lambda v: v.tensor_copy(mrow[:, 128:256], p_a[0:1, 0:128]),
         waits=[("t", t_cm2)], incs=[("v", 1)])
    pa_free[0] = ("v", cnt["v"])
    emit("vector", lambda v: v.reduce_max(mrow[:, 257:258], mrow[:, 128:256], axis=AX),
         incs=[("v", 1)])
    emit("vector", lambda v: v.tensor_tensor(mrow[:, 258:259], mrow[:, 256:257],
                                             mrow[:, 257:258], ALU.mult), incs=[("v", 1)])
    emit("vector", lambda v: v.tensor_copy(mval[:], mrow[:, 258:259]), incs=[("v", 1)])
    v_mval = cnt["v"]
    emit("tensor", lambda te: te.matmul(p_a[:, 0:1], onesrow[:], mval[:],
                                        start=True, stop=True),
         waits=[("v", v_mval)], incs=[("t", 1)])
    t_bb = cnt["t"]
    emit("vector", lambda v: v.tensor_single_scalar(bias128[:], p_a[:, 0:1], -8.0, ALU.mult),
         waits=[("t", t_bb)], incs=[("v", 1)])
    pa_free[0] = ("v", cnt["v"])
    v_bias = cnt["v"]

    rrow_free = [("dr0", 0), ("dr1", 0)]
    st_free = [("s", s_p1_done), ("v", v_p1_done)]
    av_free = {0: list(pa_free), 1: list(pbc_free[0])}
    pTs_free = {0: ("t", 0), 1: ("t", 0)}
    pairs = [(2 * hp, 2 * hp + 1, qb) for hp in range(NHEAD // 2) for qb in range(NTG)]
    for mi, (hA, hB, qb) in enumerate(pairs):
        exp_done = [0] * 8
        for pair in range(10):
            if pair >= 2:
                p2 = pair - 2
                pp2 = p2 % 2
                for sidx, h in ((0, hA), (1, hB)):
                    avbank = p_a if sidx == 0 else p_b
                    for j in (0, 1):
                        kc = 2 * p2 + j
                        emit("tensor", lambda te, h=h, kc=kc, sidx=sidx, j=j, pp2=pp2,
                             pb=avbank,
                             first=(kc == 0), last=(kc == 15): te.matmul(
                                 pb[0:65, :],
                                 v_aug[:, kc * NHEAD * 65 + h * 65: kc * NHEAD * 65 + h * 65 + 65],
                                 pTs[:, pp2 * 2048 + sidx * 1024 + j * 512:
                                     pp2 * 2048 + sidx * 1024 + j * 512 + 512],
                                 start=first, stop=last),
                            waits=([("s", exp_done[p2])] if (sidx == 0 and j == 0) else []) +
                                  (av_free[sidx] + ([("v", v_vaug_ones)] if mi == 0 else [])
                                   if kc == 0 else []),
                            incs=[("t", 1)])
                pTs_free[pp2] = ("t", cnt["t"])
            if pair < 8:
                pp = pair % 2
                first = True
                for j in (0, 1):
                    for sidx, h in ((0, hA), (1, hB)):
                        c8, off = h // 2, (h % 2) * 64
                        kc = 2 * pair + j
                        emit("tensor", lambda te, c8=c8, off=off, kc=kc, qb=qb,
                             sidx=sidx, j=j: te.matmul(
                                 p_st[:, sidx * 1024 + j * 512: sidx * 1024 + j * 512 + 512],
                                 k_fm[off:off + 64, c8 * T + kc * 128: c8 * T + kc * 128 + 128],
                                 q_fm[off:off + 64, c8 * T + qb * 512: c8 * T + qb * 512 + 512],
                                 start=True, stop=True),
                            waits=(st_free if first else []) +
                                  ([("v", v_bias)] if (mi == 0 and pair == 0 and first)
                                   else []),
                            incs=[("t", 1)])
                        first = False
                t_sp = cnt["t"]
                emit("scalar", lambda s, pp=pp: s.activation(
                    pTs[:, pp * 2048: pp * 2048 + 2048],
                    p_st[:, 0:2048], AF.Exp,
                    scale=0.125, bias=bias128[:]),
                    waits=[("t", t_sp), pTs_free[pp]], incs=[("s", 1)])
                st_free = [("s", cnt["s"])]
                exp_done[pair] = cnt["s"]
        t_av = cnt["t"]
        for sidx, h in ((0, hA), (1, hB)):
            c8, off = h // 2, (h % 2) * 64
            avbank = p_a if sidx == 0 else p_b
            emit("vector", lambda v, pb=avbank, slot=sidx: v.tensor_copy(
                zrow[slot][:, :], pb[64:65, :]),
                waits=[("t", t_av)], incs=[("v", 1)])
            emit("vector", lambda v, qb=qb, c8=c8, off=off, pb=avbank: v.tensor_copy(
                o_heads[off:off + 64, c8 * T + qb * 512: c8 * T + qb * 512 + 512],
                pb[0:64, :]),
                incs=[("v", 1)])
            av_free[sidx] = [("v", cnt["v"])]
        for sidx, h in ((0, hA), (1, hB)):
            qb_ = qb
            emit("vector", lambda v, slot=sidx: v.reciprocal(rrow[:], zrow[slot][:, :]),
                 incs=[("v", 1)])
            emit("vector", lambda v, slot=sidx: v.tensor_copy(
                rrow_bf[slot][:, :], rrow[:]),
                waits=[rrow_free[sidx]], incs=[("v", 1)])
            v_rr = cnt["v"]
            drsem = "dr0" if sidx == 0 else "dr1"
            emit("sync", lambda sy, h=h, qb=qb_, slot=sidx: sy.dma_start(
                recip16[h:h + 1, qb * 512:(qb + 1) * 512], rrow_bf[slot][:, :]),
                waits=[("v", v_rr)], incs=[(drsem, 16)])
            rrow_free[sidx] = (drsem, cnt[drsem])
    v_recip_all = cnt["v"]
    dr0_all, dr1_all = cnt["dr0"], cnt["dr1"]

    pbc2_free = list(av_free[1]) + list(pbc_free[1])
    for c8 in range(NC8):
        for tg in range(NTG):
            emit("tensor", lambda te, c8=c8, tg=tg: te.matmul(
                p_b[:, :], selc[:, c8 * 128:(c8 + 1) * 128],
                recip16[:, tg * 512:(tg + 1) * 512], start=True, stop=True),
                waits=[("v", v_recip_all), ("dr0", dr0_all), ("dr1", dr1_all)]
                      + list(pbc2_free), incs=[("t", 1)])
            t_bc = cnt["t"]
            emit("vector", lambda v, c8=c8, tg=tg: v.tensor_tensor(
                o_heads[:, c8 * T + tg * 512: c8 * T + tg * 512 + 512],
                o_heads[:, c8 * T + tg * 512: c8 * T + tg * 512 + 512],
                p_b[:, :], ALU.mult),
                waits=[("t", t_bc)], incs=[("v", 1)])
            pbc2_free = [("v", cnt["v"])]
    v_onorm = cnt["v"]

    st2.close()
    st_qkv.close()
    st_acc = ExitStack()
    acc = sb("acc", [128, NC8 * T], F32, st_acc, side="right")

    pa_free2 = [av_free[0][0]]
    pbc_free2 = {0: ("v", v_onorm), 1: ("v", v_onorm)}
    for tg in range(NTG):
        for c in range(NC8):
            emit("tensor", lambda te, c=c, tg=tg: te.matmul(
                p_a[0:HARM, :], basT["o"][:, c * HARM:(c + 1) * HARM],
                o_heads[:, c * T + tg * 512: c * T + tg * 512 + 512],
                start=(c == 0), stop=(c == NC8 - 1)),
                waits=([("v", v_onorm), pa_free2[0], av_free[0][-1]] if c == 0 else []),
                incs=([("t", 1)] if c == NC8 - 1 else []))
        t_ro = cnt["t"]
        emit("scalar", lambda s, tg=tg: s.activation(
            res_bf[:, tg * 512: tg * 512 + 512], p_a[0:HARM, :], AF.Copy),
            waits=[("t", t_ro)], incs=[("s", 1)])
        pa_free2[0] = ("s", cnt["s"])
    s_reso = cnt["s"]
    for c8 in range(NC8):
        for tg in range(NTG):
            slot = (c8 * NTG + tg) % 2
            pbank = p_b if slot == 0 else p_c
            emit("tensor", lambda te, c8=c8, tg=tg, pb=pbank: te.matmul(
                pb[:, :], wT["o"][:, c8 * 128:(c8 + 1) * 128],
                res_bf[:, tg * 512: tg * 512 + 512], start=True, stop=True),
                waits=[("s", s_reso), pbc_free2[slot]], incs=[("t", 1)])
            t_ao = cnt["t"]
            emit("vector", lambda v, c8=c8, tg=tg, pb=pbank: v.tensor_copy(
                acc[:, c8 * T + tg * 512: c8 * T + tg * 512 + 512], pb[:, :]),
                waits=[("t", t_ao)], incs=[("v", 1)])
            pbc_free2[slot] = ("v", cnt["v"])
    v_acc0 = cnt["v"]

    st2o.close()
    stx.close()
    st2p.close()
    stw.close()

    st3 = ExitStack()
    wgbuf = [sb(f"wgbuf{b}", [128, GSZ * 1024], BF16, st3) for b in range(2)]
    wubuf = [sb(f"wubuf{b}", [128, GSZ * 1024], BF16, st3) for b in range(2)]
    wdbuf = [sb(f"wdbuf{b}", [128, GSZ * 1024], BF16, st3) for b in range(2)]
    gbuf = [sb(f"gbuf{b}", [128, GSZ * T], BF16, st3) for b in range(2)]
    c_cur = sb("c_cur", [128, T], F32, st3)
    sgt = [sb(f"sgt{b}", [128, 512], BF16, st3) for b in range(2)]

    st3p = ExitStack()
    p_xg = pt("p_xg", [128, 512], F32, st3p)
    p_xu = pt("p_xu", [128, 512], F32, st3p)
    p_d0 = pt("p_d0", [128, 512], F32, st3p)
    p_d1 = pt("p_d1", [128, 512], F32, st3p)
    p_cc = pt("p_cc", [128, 512], F32, st3p)

    groups = [list(range(g, min(g + GSZ, NMOE))) for g in range(0, NMOE, GSZ)]
    groups += [list(range(g, min(g + GSZ, NHID))) for g in range(NMOE, NHID, GSZ)]

    def chunk_srcs(h):
        if h < NMOE:
            e, fi = h // NFM, h % NFM
            return (w_gate_h[e, :, fi * 128:(fi + 1) * 128],
                    w_up_h[e, :, fi * 128:(fi + 1) * 128],
                    w_down_h[e, fi * 128:(fi + 1) * 128, :])
        fi = h - NMOE
        return (sw_gate_h[:, fi * 128:(fi + 1) * 128],
                sw_up_h[:, fi * 128:(fi + 1) * 128],
                sw_down_h[fi * 128:(fi + 1) * 128, :])

    def emit_group_dma(gi, buf_free_wait):
        buf = gi % 2
        sem = "dwa" if buf == 0 else "dwb"
        first = True
        for loc, h in enumerate(groups[gi]):
            sg, su, sd = chunk_srcs(h)
            for (dst, src) in ((wgbuf[buf], sg), (wubuf[buf], su)):
                emit("gpsimd", lambda g, dst=dst, loc=loc, src=src: g.dma_start(
                    dst.ap()[:, loc * 1024:(loc + 1) * 1024].rearrange(
                        "p (c f) -> p c f", c=NC8),
                    src.rearrange("(c p) f -> p c f", p=128)),
                    waits=([buf_free_wait] if first else []),
                    incs=[(sem, 16)])
                first = False
            emit("gpsimd", lambda g, buf=buf, loc=loc, sd=sd: g.dma_start(
                wdbuf[buf][:, loc * 1024:(loc + 1) * 1024], sd),
                incs=[(sem, 16)])
        return (sem, cnt[sem])

    dma_done = [emit_group_dma(0, ("v", v_acc0)), emit_group_dma(1, ("v", v_acc0))]
    grp_last_t = [0] * len(groups)

    pxg_free = ("v", v_acc0)
    pxu_free = ("v", v_acc0)
    pd_free = {0: ("v", v_acc0), 1: ("v", v_acc0)}
    pcc_free = ("v", v_acc0)
    sgt_free = [("v", 0), ("v", 0)]
    gbuf_free = [("t", 0), ("t", 0)]
    cur_e = -1
    su_slot = 0
    add_done = {}
    v_g_tg = [0] * NTG

    for gi, grp in enumerate(groups):
        buf = gi % 2
        dww = dma_done[gi]
        for loc, h in enumerate(grp):
            is_moe = h < NMOE
            if is_moe and (h // NFM) != cur_e:
                cur_e = h // NFM
                for tg in range(NTG):
                    emit("tensor", lambda te, e=cur_e, tg=tg: te.matmul(
                        p_cc[:, :], sele[:, e * 128:(e + 1) * 128],
                        comb_bf[0:E, tg * 512:(tg + 1) * 512],
                        start=True, stop=True),
                        waits=[("v", v_p1_done), pcc_free], incs=[("t", 1)])
                    t_cc = cnt["t"]
                    emit("vector", lambda v, tg=tg: v.tensor_copy(
                        c_cur[:, tg * 512:(tg + 1) * 512], p_cc[:, :]),
                        waits=[("t", t_cc)], incs=[("v", 1)])
                    pcc_free = ("v", cnt["v"])
            v_ccur = cnt["v"]
            for tg in range(NTG):
                for c in range(NC8):
                    emit("tensor", lambda te, buf=buf, loc=loc, c=c, tg=tg: te.matmul(
                        p_xg[:, :],
                        wgbuf[buf][:, loc * 1024 + c * 128: loc * 1024 + c * 128 + 128],
                        xnh_fm[:, c * T + tg * 512: c * T + tg * 512 + 512],
                        start=(c == 0), stop=(c == NC8 - 1)),
                        waits=([dww, pxg_free] if c == 0 else []),
                        incs=([("t", 1)] if c == NC8 - 1 else []))
                t_xg = cnt["t"]
                for c in range(NC8):
                    emit("tensor", lambda te, buf=buf, loc=loc, c=c, tg=tg: te.matmul(
                        p_xu[:, :],
                        wubuf[buf][:, loc * 1024 + c * 128: loc * 1024 + c * 128 + 128],
                        xnh_fm[:, c * T + tg * 512: c * T + tg * 512 + 512],
                        start=(c == 0), stop=(c == NC8 - 1)),
                        waits=([pxu_free] if c == 0 else []),
                        incs=([("t", 1)] if c == NC8 - 1 else []))
                t_xu = cnt["t"]
                ss = su_slot % 2
                su_slot += 1
                emit("scalar", lambda s, ss=ss: s.activation(sgt[ss][:], p_xg[:, :], AF.Silu),
                     waits=[("t", t_xg), sgt_free[ss]], incs=[("s", 1)])
                pxg_free = ("s", cnt["s"])
                s_silu = cnt["s"]
                gslice = gbuf[buf][:, loc * T + tg * 512: loc * T + tg * 512 + 512]
                emit("vector", lambda v, gs=gslice, ss=ss: v.tensor_tensor(
                    gs, sgt[ss][:], p_xu[:, :], ALU.mult),
                    waits=[("t", t_xu), ("s", s_silu)] +
                          ([gbuf_free[buf]] if (loc == 0 and tg == 0) else []),
                    incs=[("v", 1)])
                pxu_free = ("v", cnt["v"])
                sgt_free[ss] = ("v", cnt["v"])
                if is_moe:
                    emit("vector", lambda v, gs=gslice, tg=tg: v.tensor_tensor(
                        gs, gs, c_cur[:, tg * 512:(tg + 1) * 512], ALU.mult),
                        waits=[("v", v_ccur)], incs=[("v", 1)])
                v_g_tg[tg] = cnt["v"]
        for tg in range(NTG):
            for dc in range(NC8):
                slot = (tg * NC8 + dc) % 2
                pbank = p_d0 if slot == 0 else p_d1
                for li in range(len(grp)):
                    emit("tensor", lambda te, buf=buf, li=li, dc=dc, tg=tg, pb=pbank,
                         first=(li == 0), last=(li == len(grp) - 1): te.matmul(
                             pb[:, :],
                             wdbuf[buf][:, li * 1024 + dc * 128: li * 1024 + dc * 128 + 128],
                             gbuf[buf][:, li * T + tg * 512: li * T + tg * 512 + 512],
                             start=first, stop=last),
                        waits=([("v", v_g_tg[tg]), pd_free[slot]] if li == 0 else []),
                        incs=([("t", 1)] if li == len(grp) - 1 else []))
                t_dn = cnt["t"]
                emit("vector", lambda v, dc=dc, tg=tg, pb=pbank: v.tensor_tensor(
                    acc[:, dc * T + tg * 512: dc * T + tg * 512 + 512],
                    acc[:, dc * T + tg * 512: dc * T + tg * 512 + 512],
                    pb[:, :], ALU.add),
                    waits=[("t", t_dn)], incs=[("v", 1)])
                if gi == len(groups) - 1:
                    add_done[(tg, dc)] = cnt["v"]
                pd_free[slot] = ("v", cnt["v"])
        grp_last_t[gi] = cnt["t"]
        gbuf_free[buf] = ("t", cnt["t"])
        if gi + 2 < len(groups):
            dma_done.append(emit_group_dma(gi + 2, ("t", grp_last_t[gi])))
    v_moe_done = cnt["v"]

    st3p.close()
    st3.close()

    st4 = ExitStack()
    x4 = [sb(f"x4_{b}", [128, D], F32, st4, side="right") for b in range(2)]
    o4 = [sb(f"o4_{b}", [128, D], F32, st4, side="right") for b in range(2)]
    st4p = ExitStack()
    p_o = [pt(f"p_o{b}", [128, 512], F32, st4p) for b in range(2)]

    x4_free = [("v", 0)] * 2
    o4_free = [("doa", 0), ("dob", 0)]
    po_free = {0: ("v", v_moe_done), 1: ("v", v_moe_done)}
    dx4 = []
    X4SEMS = ("dxa", "dxb")
    for i in range(2):
        sem = X4SEMS[i % 2]
        emit("sync", lambda sy, i=i: sy.dma_start(x4[i][:, :], x_h[i * 128:(i + 1) * 128, :]),
             waits=[(sem, cnt[sem])], incs=[(sem, 16)])
        dx4.append((sem, cnt[sem]))
    for i in range(NT):
        b = i % 2
        ob = i % 2
        for c8 in range(NC8):
            slot = (i * NC8 + c8) % 2
            emit("tensor", lambda te, c8=c8, i=i, slot=slot: te.transpose(
                p_o[slot][:, 0:128], acc[:, c8 * T + i * 128: c8 * T + i * 128 + 128],
                ident_f[:]),
                waits=[("v", add_done[(i // 4, c8)]), po_free[slot]], incs=[("t", 1)])
            t_tr = cnt["t"]
            emit("vector", lambda v, c8=c8, b=b, ob=ob, slot=slot: v.tensor_tensor(
                o4[ob][:, c8 * 128:(c8 + 1) * 128], x4[b][:, c8 * 128:(c8 + 1) * 128],
                p_o[slot][:, 0:128], ALU.add),
                waits=[("t", t_tr), dx4[i], o4_free[ob]], incs=[("v", 1)])
            po_free[slot] = ("v", cnt["v"])
        v_out = cnt["v"]
        x4_free[b] = ("v", v_out)
        if i + 2 < NT:
            sem = X4SEMS[(i + 2) % 2]
            emit("sync", lambda sy, i=i, b=b: sy.dma_start(
                x4[b][:, :], x_h[(i + 2) * 128:(i + 3) * 128, :]),
                waits=[x4_free[b], (sem, cnt[sem])], incs=[(sem, 16)])
            dx4.append((sem, cnt[sem]))
        dosem = "doa" if ob == 0 else "dob"
        emit("sync", lambda sy, i=i, ob=ob: sy.dma_start(
            out_h[i * 128:(i + 1) * 128, :], o4[ob][:, :]),
            waits=[("v", v_out)], incs=[(dosem, 16)])
        o4_free[ob] = (dosem, cnt[dosem])
    doa_total, dob_total = cnt["doa"], cnt["dob"]

    with ExitStack() as st_sem:
        sems = {nm: st_sem.enter_context(nc.semaphore(nm)) for nm in cnt}
        with nc.Block() as block:
            def runner(engname):
                ss = {"vector": "v", "scalar": "s"}.get(engname)
                def run(e):
                    for k, (fn, waits, incs) in enumerate(ops[engname]):
                        if ss is not None and k > 0:
                            e.wait_ge(sems[ss], k)
                        for sname, val in waits:
                            e.wait_ge(sems[sname], val)
                        inst = fn(e)
                        for sname, amt in incs:
                            inst.then_inc(sems[sname], amt)
                    if engname == "sync":
                        e.wait_ge(sems["doa"], doa_total)
                        e.wait_ge(sems["dob"], dob_total)
                return run
            block.tensor(runner("tensor"))
            block.vector(runner("vector"))
            block.scalar(runner("scalar"))
            block.gpsimd(runner("gpsimd"))
            block.sync(runner("sync"))

    st4p.close()
    st4.close()
    st_acc.close()
    so.close()
    return nc


_NC_CACHE = None


def kernel(**inputs):
    global _NC_CACHE
    if _NC_CACHE is None:
        _NC_CACHE = build_nc()
    nc = _NC_CACHE
    x = np.ascontiguousarray(np.asarray(inputs["L0_attn_x"], dtype=np.float32))
    common = {
        "gate_w": np.ascontiguousarray(inputs["L0_ffn_gate_w"], dtype=np.float32),
        "w_gate": np.ascontiguousarray(inputs["L0_ffn_w_gate"], dtype=np.float32),
        "w_up": np.ascontiguousarray(inputs["L0_ffn_w_up"], dtype=np.float32),
        "w_down": np.ascontiguousarray(inputs["L0_ffn_w_down"], dtype=np.float32),
        "sw_gate": np.ascontiguousarray(inputs["L0_ffn_sw_gate"], dtype=np.float32),
        "sw_up": np.ascontiguousarray(inputs["L0_ffn_sw_up"], dtype=np.float32),
        "sw_down": np.ascontiguousarray(inputs["L0_ffn_sw_down"], dtype=np.float32),
    }
    for p in "qkvo":
        common[f"basis_{p}"] = np.ascontiguousarray(inputs[f"L0_attn_basis_{p}"], dtype=np.float32)
        common[f"phase_{p}"] = np.ascontiguousarray(inputs[f"L0_attn_phase_{p}"], dtype=np.float32)
        common[f"amp_{p}"] = np.ascontiguousarray(inputs[f"L0_attn_amp_{p}"], dtype=np.float32)
    in_maps = [dict(common, x=np.ascontiguousarray(x[i])) for i in range(8)]
    res = run_bass_kernel_spmd(nc, in_maps, core_ids=list(range(8)))
    return np.stack([r["out"] for r in res.results], axis=0)


if __name__ == "__main__":
    build_nc()
    print("built ok")
